# version 11
# baseline (speedup 1.0000x reference)
import contextlib
import math
import numpy as np
import ml_dtypes
import concourse.bass as bass
import concourse.mybir as mybir
from concourse.bass import IndirectOffsetOnAxis
from concourse.bass_utils import run_bass_kernel_spmd

F32 = mybir.dt.float32
BF16 = mybir.dt.bfloat16
I32 = mybir.dt.int32
AF = mybir.ActivationFunctionType
ALU = mybir.AluOpType
AX = mybir.AxisListType
bf16 = ml_dtypes.bfloat16

NCORES = 8
SEQ = 2048
NSEQ = 2
TOK = SEQ * NSEQ
C = 1024
CCN = 8
HD = 256
NH = 4
NE = 8
FD = 2048
CCAP = 1280
NSC = CCAP // 128
ES = CCAP + 128
NCT = TOK // 128
TWO_PI_HI = float(np.float32(2.0 * math.pi))
TWO_PI_LO = float(np.float32(2.0 * math.pi - np.float64(np.float32(2.0 * math.pi))))

ENGINES = ("tensor", "vector", "scalar", "gpsimd", "sync")


class Sem:
    __slots__ = ("h", "n")

    def __init__(self, h):
        self.h = h
        self.n = 0

    def inc(self, k=1):
        self.n += k
        return self.n


class Buf:
    __slots__ = ("w", "r", "name")

    def __init__(self, name=""):
        self.w = []
        self.r = []
        self.name = name


def _run_stream(stream):
    def run(eng):
        for it in stream:
            k = it[0]
            if k == "w":
                eng.wait_ge(it[1], it[2])
            elif k == "b":
                eng.sem_inc(it[1], 1)
                eng.wait_ge(it[1], it[2])
            else:
                _, fn, semh, delta = it
                inst = fn(eng)
                if semh is not None:
                    inst.then_inc(semh, delta)
    return run


class Kb:
    def __init__(self, nc, ctx):
        self.nc = nc
        self.ctx = ctx
        self.streams = {e: [] for e in ENGINES}
        self.seen = {e: {} for e in ENGINES}
        self.eng_sem = {e: Sem(ctx.enter_context(nc.semaphore(f"sem_{e}")))
                        for e in ("tensor", "vector", "scalar", "gpsimd")}
        self.lanes = {
            "sync": [Sem(ctx.enter_context(nc.semaphore(f"qs{i}"))) for i in range(8)],
            "gpsimd": [Sem(ctx.enter_context(nc.semaphore(f"qg{i}"))) for i in range(8)],
        }
        self.pf_lanes = {
            "gpsimd": [Sem(ctx.enter_context(nc.semaphore(f"qgp{i}"))) for i in range(4)],
        }
        self.lane_rr = {k: 0 for k in self.lanes}
        self.pf_rr = {k: 0 for k in self.pf_lanes}
        self.bar_sem = Sem(ctx.enter_context(nc.semaphore("bar")))

    def _wait(self, eng, ev):
        sem, val = ev
        key = id(sem)
        if self.seen[eng].get(key, -1) >= val:
            return
        self.seen[eng][key] = val
        self.streams[eng].append(("w", sem.h, val))

    def op(self, eng, fn, reads=(), writes=(), dma=False, prefetch=False,
           no_waw=False):
        for b in reads:
            for ev in b.w:
                self._wait(eng, ev)
        for b in writes:
            if not no_waw:
                for ev in b.w:
                    self._wait(eng, ev)
            for ev in b.r:
                self._wait(eng, ev)
        if dma:
            if prefetch:
                ls = self.pf_lanes[eng]
                s = ls[self.pf_rr[eng] % len(ls)]
                self.pf_rr[eng] += 1
            else:
                ls = self.lanes[eng]
                s = ls[self.lane_rr[eng] % len(ls)]
                self.lane_rr[eng] += 1
            if s.n:
                self._wait(eng, (s, s.n))
            delta = 16
        else:
            s = self.eng_sem[eng]
            delta = 1
        v = s.inc(delta)
        ev = (s, v)
        self.streams[eng].append(("i", fn, s.h, delta))
        for b in writes:
            if no_waw:
                b.w.append(ev)
            else:
                b.w = [ev]
                b.r = []
        for b in reads:
            b.r.append(ev)
        return ev

    def mm_group(self, psum_buf, mms, read_bufs, no_waw=False):
        eng = "tensor"
        for b in read_bufs:
            for ev in b.w:
                self._wait(eng, ev)
        if not no_waw:
            for ev in psum_buf.w:
                self._wait(eng, ev)
        for ev in psum_buf.r:
            self._wait(eng, ev)
        for fn in mms[:-1]:
            self.streams[eng].append(("i", fn, None, None))
        s = self.eng_sem[eng]
        v = s.inc(1)
        self.streams[eng].append(("i", mms[-1], s.h, 1))
        ev = (s, v)
        if no_waw:
            psum_buf.w.append(ev)
        else:
            psum_buf.w = [ev]
            psum_buf.r = []
        for b in read_bufs:
            b.r.append(ev)
        return ev

    def barrier(self, engines=("tensor", "vector", "scalar", "gpsimd", "sync"),
                lane_queues=("sync", "gpsimd")):
        full = len(engines) == 5
        target = self.bar_sem.n + len(engines)
        self.bar_sem.n = target
        for e in engines:
            if e in self.eng_sem:
                self._wait(e, (self.eng_sem[e], self.eng_sem[e].n))
            if full and e == "sync":
                for q in lane_queues:
                    for s in self.lanes[q]:
                        if s.n:
                            self._wait(e, (s, s.n))
            self.streams[e].append(("b", self.bar_sem.h, target))
            self.seen[e][id(self.bar_sem)] = target

    def finalize(self):
        for lanes in self.lanes.values():
            for s in lanes:
                if s.n:
                    self._wait("sync", (s, s.n))
        for lanes in self.pf_lanes.values():
            for s in lanes:
                if s.n:
                    self._wait("sync", (s, s.n))
        blk = self.ctx.enter_context(self.nc.Block())
        for name in ENGINES:
            getattr(blk, name)(_run_stream(self.streams[name]))



def build_program(with_ffn=True, with_attn=True, attn_level=5, qb_list=None, debug=False):
    nc = bass.Bass("TRN2", target_bir_lowering=False, debug=False)
    dt = nc.dram_tensor
    x_d = dt("x2d", [TOK, C], F32, kind="ExternalInput").ap()
    wq_d = dt("wqT", [C, C], F32, kind="ExternalInput").ap()
    wk_d = dt("wkT", [C, HD], F32, kind="ExternalInput").ap()
    wv_d = dt("wvT", [C, HD], F32, kind="ExternalInput").ap()
    wo_d = dt("woT", [C, C], F32, kind="ExternalInput").ap()
    qg_d = dt("qg", [1, HD], BF16, kind="ExternalInput").ap()
    kg_d = dt("kg", [1, HD], BF16, kind="ExternalInput").ap()
    wg_d = dt("wgT", [C, NE], F32, kind="ExternalInput").ap()
    wgu_d = dt("wgu", [NE, CCN, 128, 2 * FD], F32, kind="ExternalInput").ap()
    wd_d = dt("wd", [NE, FD // 128, 128, C], F32, kind="ExternalInput").ap()
    idf_d = dt("ident_f", [128, 128], F32, kind="ExternalInput").ap()
    idb_d = dt("ident_bf", [128, 128], BF16, kind="ExternalInput").ap()
    csum_d = dt("csum_ut", [128, 128], BF16, kind="ExternalInput").ap()
    m0_d = dt("m0", [128, 128], BF16, kind="ExternalInput").ap()
    m01_d = dt("m01", [128, 128], BF16, kind="ExternalInput").ap()
    mdiag_d = dt("mdiag", [128, 128], F32, kind="ExternalInput").ap()
    mleft_d = dt("mleft", [128, 128], F32, kind="ExternalInput").ap()
    mleft5_d = dt("mleft5", [128, 512], F32, kind="ExternalInput").ap()
    midx_d = dt("m_idx", [128, 1], F32, kind="ExternalInput").ap()
    trow_d = dt("t_row", [1, SEQ], F32, kind="ExternalInput").ap()
    onesr_d = dt("ones_r", [1, 128], F32, kind="ExternalInput").ap()
    onesc_d = dt("ones_c", [128, 1], BF16, kind="ExternalInput").ap()
    onescf_d = dt("ones_cf", [128, 1], F32, kind="ExternalInput").ap()
    tokid_d = dt("tokid", [128, NCT], I32, kind="ExternalInput").ap()
    tokidb_d = dt("tokid_b", [128, NCT], I32, kind="ExternalInput").ap()
    epsc_d = dt("epsc", [128, 2], F32, kind="ExternalInput").ap()
    out_d = dt("out", [TOK, C], F32, kind="ExternalOutput").ap()
    dbg = {}
    if debug:
        dbg["addrA"] = dt("dbg_addrA", [128, NCT], I32, kind="ExternalOutput").ap()
        dbg["addrB"] = dt("dbg_addrB", [128, NCT], I32, kind="ExternalOutput").ap()
        dbg["w1"] = dt("dbg_w1", [128, NCT], F32, kind="ExternalOutput").ap()
        dbg["w2"] = dt("dbg_w2", [128, NCT], F32, kind="ExternalOutput").ap()
        dbg["idx"] = dt("dbg_idx", [NE * ES, 1], I32, kind="ExternalOutput").ap()
        dbg["warr"] = dt("dbg_warr", [NE * ES, 1], F32, kind="ExternalOutput").ap()
        dbg["lgtm"] = dt("dbg_lgtm", [128, NCT * NE], F32, kind="ExternalOutput").ap()
        dbg["pos"] = dt("dbg_pos", [128, NCT * NE], F32, kind="ExternalOutput").ap()
        dbg["xntm"] = dt("dbg_xntm", [TOK + 128, C], BF16, kind="ExternalOutput").ap()
        dbg["dout"] = dt("dbg_dout", [NE * ES, C], BF16, kind="ExternalOutput").ap()

    xn_tm_d = dt("xn_tm", [TOK + 128, C], BF16, kind="Internal").ap()
    wo_tm_d = dt("wo_tm", [TOK, C], BF16, kind="Internal").ap()
    contrib_d = dt("contrib", [2 * (TOK + 128) + 128, C], BF16, kind="Internal").ap()
    idx_dram = dt("idx_arr", [NE * ES, 1], I32, kind="Internal").ap()
    idxo_dram = dt("idxo_arr", [NE * ES, 1], I32, kind="Internal").ap()
    w_dram = dt("w_arr", [NE * ES, 1], F32, kind="Internal").ap()

    with contextlib.ExitStack() as ctx:
        E = ctx.enter_context
        kb = Kb(nc, ctx)
        O = kb.op
        MG = kb.mm_group

        _uid = [0]

        def mk(stack, name, shape, dtype, side="left"):
            _uid[0] += 1
            t = stack.enter_context(
                nc.sbuf_tensor(f"{name}_{_uid[0]}", shape, dtype, side=side))
            return t, Buf(name)

        def mkp(stack, name, shape, dtype):
            _uid[0] += 1
            return stack.enter_context(
                nc.psum_tensor(f"{name}_{_uid[0]}", shape, dtype))

        idf, b_idf = mk(ctx, "idf", [128, 128], F32)
        idb, b_idb = mk(ctx, "idb", [128, 128], BF16)
        csum_t, b_csum = mk(ctx, "csum_t", [128, 128], BF16)
        m0_t, b_m0 = mk(ctx, "m0_t", [128, 128], BF16)
        m01_t, b_m01 = mk(ctx, "m01_t", [128, 128], BF16)
        mdiag, b_mdiag = mk(ctx, "mdiag_s", [128, 128], F32)
        mleft, b_mleft = mk(ctx, "mleft_s", [128, 128], F32)
        mleft5, b_mleft5 = mk(ctx, "mleft5_s", [128, 512], F32)
        midx, b_midx = mk(ctx, "midx", [128, 1], F32)
        onesr, b_onesr = mk(ctx, "onesr", [1, 128], F32)
        onesc, b_onesc = mk(ctx, "onesc", [128, 1], BF16)
        onescf, b_onescf = mk(ctx, "onescf", [128, 1], F32)
        tokid, b_tokid = mk(ctx, "tokid_s", [128, NCT], I32)
        tokidb, b_tokidb = mk(ctx, "tokidb_s", [128, NCT], I32)
        qg_t, b_qg = mk(ctx, "qg_t", [1, HD], BF16)
        kg_t, b_kg = mk(ctx, "kg_t", [1, HD], BF16)
        cos_t, b_cos = mk(ctx, "cos_t", [128, SEQ], BF16)
        sin_t, b_sin = mk(ctx, "sin_t", [128, SEQ], BF16)
        w1_t, b_w1 = mk(ctx, "w1_t", [128, NCT], F32)
        w2_t, b_w2 = mk(ctx, "w2_t", [128, NCT], F32)
        addrA, b_addrA = mk(ctx, "addrA", [128, NCT], I32)
        addrB, b_addrB = mk(ctx, "addrB", [128, NCT], I32)
        epsc, b_epsc = mk(ctx, "epsc_s", [128, 2], F32)
        zrow, b_zrow = mk(ctx, "zrow", [128, C], BF16)

        b_xntm = Buf("xn_tm")
        b_wotm = Buf("wo_tm")
        b_contrib = Buf("contrib")
        b_contribz = Buf("contrib_zero")
        b_idxdram = Buf("idx_dram")
        b_idxodram = Buf("idxo_dram")
        b_wdram = Buf("w_dram")
        b_wgu_bf = Buf("wgu_bf_dram")
        b_wd_bf = Buf("wd_bf_dram")

        aw_stack = ctx.enter_context(contextlib.ExitStack())
        xnfm, b_xnfm = mk(aw_stack, "xnfm", [128, CCN * TOK], BF16, side="right")
        wq_bf, b_wqbf = mk(aw_stack, "wq_bf", [128, CCN * C], BF16, side="right")
        wk_bf, b_wkbf = mk(aw_stack, "wk_bf", [128, CCN * HD], BF16, side="right")
        wv_bf, b_wvbf = mk(aw_stack, "wv_bf", [128, CCN * HD], BF16, side="right")
        wo_bfw, b_wobf = mk(aw_stack, "wo_bfw", [128, CCN * C], BF16, side="right")
        wg_bf, b_wgbf = mk(aw_stack, "wg_bf", [128, CCN * NE], BF16, side="right")

        for dst, src, b in ((idf, idf_d, b_idf), (idb, idb_d, b_idb),
                            (csum_t, csum_d, b_csum), (m0_t, m0_d, b_m0),
                            (m01_t, m01_d, b_m01), (mdiag, mdiag_d, b_mdiag),
                            (mleft, mleft_d, b_mleft),
                            (mleft5, mleft5_d, b_mleft5), (midx, midx_d, b_midx),
                            (onesr, onesr_d, b_onesr),
                            (onesc, onesc_d, b_onesc), (onescf, onescf_d, b_onescf),
                            (tokid, tokid_d, b_tokid),
                            (tokidb, tokidb_d, b_tokidb),
                            (epsc, epsc_d, b_epsc),
                            (qg_t, qg_d, b_qg),
                            (kg_t, kg_d, b_kg)):
            O("sync", lambda e, a=dst[:], s=src: e.dma_start(out=a, in_=s),
              writes=[b], dma=True)

        for cc in range(CCN):
            for wt, wsrc, b, wd_ in ((wq_bf, wq_d, b_wqbf, C), (wk_bf, wk_d, b_wkbf, HD),
                                     (wv_bf, wv_d, b_wvbf, HD), (wo_bfw, wo_d, b_wobf, C),
                                     (wg_bf, wg_d, b_wgbf, NE)):
                O("gpsimd", lambda e, a=wt[:, cc * wd_:(cc + 1) * wd_],
                  s=wsrc[cc * 128:(cc + 1) * 128, :]: e.dma_start(out=a, in_=s),
                  writes=[b], dma=True, no_waw=True)

        O("vector", lambda e, a=zrow[:]: e.memset(a, 0.0), writes=[b_zrow])

        with contextlib.ExitStack() as p0:
            trow, b_trow = mk(p0, "trow", [1, SEQ], F32, "right")
            O("sync", lambda e, a=trow[:], s=trow_d: e.dma_start(out=a, in_=s),
              writes=[b_trow], dma=True)
            base_t, b_base = mk(p0, "base_t", [128, 1], F32, "right")
            lnb_t, b_lnb = mk(p0, "lnb_t", [128, 1], F32, "right")
            expo_t, b_expo = mk(p0, "expo_t", [128, 1], F32, "right")
            invf_t, b_invf = mk(p0, "invf_t", [128, 1], F32, "right")
            invfr, b_invfr = mk(p0, "invfr", [1, 128], F32, "right")
            th_t, b_th = mk(p0, "th_t", [128, 512], F32, "right")
            z_t, b_z = mk(p0, "z_t", [128, 512], F32, "right")
            ki_t, b_ki = mk(p0, "ki_t", [128, 512], I32, "right")
            kf_t, b_kf = mk(p0, "kf_t", [128, 512], F32, "right")
            ph_t, b_ph = mk(p0, "ph_t", [128, 512], F32, "right")
            gm_t, b_gm = mk(p0, "gm_t", [128, 512], F32, "right")
            sh_t, b_sh = mk(p0, "sh_t", [128, 512], F32, "right")
            sq_t, b_sq = mk(p0, "sq_t", [128, 512], F32, "right")
            inv_t, b_inv = mk(p0, "inv_t", [128, NE * ES // 128], I32, "right")
            inv2_t, b_inv2 = mk(p0, "inv2_t", [128, NE * ES // 128], I32, "right")
            wz_t, b_wz = mk(p0, "wz_t", [128, NE * ES // 128], F32, "right")
            ps_f = mkp(p0, "ps_f", [128, 512], F32)
            b_psf = Buf("ps_f")
            ps_tr = mkp(p0, "ps_tr", [1, 128], F32)
            b_pstr = Buf("ps_tr")

            if with_ffn:
                O("vector", lambda e, a=inv_t[:]: e.memset(a, TOK), writes=[b_inv])
                O("vector", lambda e, a=wz_t[:]: e.memset(a, 0.0), writes=[b_wz])
                O("gpsimd", lambda e, a=idx_dram.rearrange("(b a) c -> b (a c)", b=128),
                  s=inv_t[:]: e.dma_start(out=a, in_=s),
                  reads=[b_inv], writes=[b_idxdram], dma=True)
                O("vector", lambda e, a=inv2_t[:]: e.memset(a, 2 * (TOK + 128)),
                  writes=[b_inv2])
                O("gpsimd", lambda e, a=idxo_dram.rearrange("(b a) c -> b (a c)", b=128),
                  s=inv2_t[:]: e.dma_start(out=a, in_=s),
                  reads=[b_inv2], writes=[b_idxodram], dma=True)
                O("gpsimd", lambda e, a=w_dram.rearrange("(b a) c -> b (a c)", b=128),
                  s=wz_t[:]: e.dma_start(out=a, in_=s),
                  reads=[b_wz], writes=[b_wdram], dma=True)
                for r0 in range(0, 2 * (TOK + 128) + 128, 128):
                    O("sync", lambda e,
                      a=contrib_d[r0:r0 + 128, :], s=zrow[:]:
                      e.dma_start(out=a, in_=s),
                      reads=[b_zrow], writes=[b_contribz], dma=True, no_waw=True)
                O("gpsimd", lambda e, a=xn_tm_d[TOK:TOK + 128, :], s=zrow[:]:
                  e.dma_start(out=a, in_=s),
                  reads=[b_zrow], writes=[b_xntm], dma=True, no_waw=True)

            O("vector", lambda e, a=base_t[:]: e.memset(a, 1.0e6), writes=[b_base])
            O("scalar", lambda e, o=lnb_t[:], i=base_t[:]:
              e.activation(out=o, in_=i, func=AF.Ln), reads=[b_base], writes=[b_lnb])
            O("vector", lambda e, o=expo_t[:], i=midx[:], s1=lnb_t[:]:
              e.tensor_scalar(out=o, in0=i, scalar1=s1, scalar2=-1.0 / 128.0,
                              op0=ALU.mult, op1=ALU.mult),
              reads=[b_midx, b_lnb], writes=[b_expo])
            O("scalar", lambda e, o=invf_t[:], i=expo_t[:]:
              e.activation(out=o, in_=i, func=AF.Exp), reads=[b_expo], writes=[b_invf])
            O("tensor", lambda e, o=ps_tr[:], i=invf_t[:], ii=idf[:]:
              e.transpose(o, i, ii), reads=[b_invf, b_idf], writes=[b_pstr])
            O("scalar", lambda e, o=invfr[:], i=ps_tr[:]: e.copy(out=o, in_=i),
              reads=[b_pstr], writes=[b_invfr])
            for j4 in range(SEQ // 512):
                sl = slice(j4 * 512, (j4 + 1) * 512)
                MG(b_psf, [lambda e, o=ps_f[:], l_=invfr[:], r_=trow[:, sl]:
                           e.matmul(o, l_, r_, start=True, stop=True)],
                   [b_invfr, b_trow])
                O("scalar", lambda e, o=th_t[:], i=ps_f[:]: e.copy(out=o, in_=i),
                  reads=[b_psf], writes=[b_th])
                O("vector", lambda e, o=z_t[:], i=th_t[:]:
                  e.tensor_single_scalar(out=o, in_=i, scalar=1.0 / TWO_PI_HI,
                                         op=ALU.mult), reads=[b_th], writes=[b_z])
                O("vector", lambda e, o=ki_t[:], i=z_t[:]: e.tensor_copy(out=o, in_=i),
                  reads=[b_z], writes=[b_ki])
                O("vector", lambda e, o=kf_t[:], i=ki_t[:]: e.tensor_copy(out=o, in_=i),
                  reads=[b_ki], writes=[b_kf])
                for sc_, b_in in ((-TWO_PI_HI, b_th), (-TWO_PI_LO, None)):
                    O("vector", lambda e, o=ph_t[:], i=kf_t[:], s=sc_,
                      i1=(th_t[:] if sc_ == -TWO_PI_HI else ph_t[:]):
                      e.scalar_tensor_tensor(out=o, in0=i, scalar=s, in1=i1,
                                             op0=ALU.mult, op1=ALU.add),
                      reads=[b_kf, b_th, b_ph], writes=[b_ph])
                for sgn in (1.0, -1.0):
                    O("vector", lambda e, o=gm_t[:], i=ph_t[:], s=sgn:
                      e.tensor_single_scalar(out=o, in_=i, scalar=s * math.pi,
                                             op=(ALU.is_gt if s > 0 else ALU.is_lt)),
                      reads=[b_ph], writes=[b_gm])
                    for cpart in (TWO_PI_HI, TWO_PI_LO):
                        O("vector", lambda e, o=ph_t[:], i=gm_t[:],
                          s=(-cpart if sgn > 0 else cpart), i1=ph_t[:]:
                          e.scalar_tensor_tensor(out=o, in0=i, scalar=s, in1=i1,
                                                 op0=ALU.mult, op1=ALU.add),
                          reads=[b_gm, b_ph], writes=[b_ph])
                O("scalar", lambda e, o=sin_t[:, sl], i=ph_t[:]:
                  e.activation(out=o, in_=i, func=AF.Sin),
                  reads=[b_ph], writes=[b_sin], no_waw=True)
                O("scalar", lambda e, o=sh_t[:], i=ph_t[:]:
                  e.activation(out=o, in_=i, func=AF.Sin, scale=0.5),
                  reads=[b_ph], writes=[b_sh])
                O("vector", lambda e, o=sq_t[:], i=sh_t[:]:
                  e.tensor_mul(out=o, in0=i, in1=i), reads=[b_sh], writes=[b_sq])
                O("vector", lambda e, o=cos_t[:, sl], i=sq_t[:]:
                  e.tensor_scalar(out=o, in0=i, scalar1=-2.0, scalar2=1.0,
                                  op0=ALU.mult, op1=ALU.add),
                  reads=[b_sq], writes=[b_cos], no_waw=True)
            kb.barrier()

        with contextlib.ExitStack() as p1:
            xts = [mk(p1, f"xt{i}", [128, C], F32, "right") for i in range(2)]
            sqs = [mk(p1, f"sqs{i}", [128, C], F32, "right") for i in range(2)]
            xnb = [mk(p1, f"xnb{i}", [128, C], BF16, "right") for i in range(2)]
            sts = [mk(p1, f"lnst{i}", [128, 8], F32, "right") for i in range(2)]
            ptr = [(mkp(p1, f"ptr{i}", [128, 128], BF16),
                    Buf(f"ptr{i}")) for i in range(2)]

            for ct in range(NCT):
                par = ct % 2
                xt, b_xt = xts[par]
                sqt, b_sqt = sqs[par]
                xnt, b_xnt = xnb[par]
                stt, b_st = sts[par]
                c_sm, c_s2, c_m, c_m2, c_var, c_rs, c_nm = (
                    stt[:, i:i + 1] for i in range(7))
                O("sync", lambda e, a=xt[:], s=x_d[ct * 128:(ct + 1) * 128, :]:
                  e.dma_start(out=a, in_=s), writes=[b_xt], dma=True)
                O("vector", lambda e, o=c_sm, i=xt[:]:
                  e.reduce_sum(out=o, in_=i, axis=AX.X), reads=[b_xt], writes=[b_st])
                O("scalar", lambda e, o=sqt[:], i=xt[:], a=c_s2:
                  e.activation(out=o, in_=i, func=AF.Square, accum_out=a),
                  reads=[b_xt], writes=[b_sqt, b_st], no_waw=True)
                O("vector", lambda e, o=c_m, i=c_sm:
                  e.tensor_single_scalar(out=o, in_=i, scalar=1.0 / C, op=ALU.mult),
                  reads=[b_st], writes=[b_st])
                O("vector", lambda e, o=c_m2, i=c_m:
                  e.tensor_mul(out=o, in0=i, in1=i), reads=[b_st], writes=[b_st])
                O("vector", lambda e, o=c_var, i=c_s2:
                  e.tensor_single_scalar(out=o, in_=i, scalar=1.0 / C, op=ALU.mult),
                  reads=[b_st], writes=[b_st])
                O("vector", lambda e, o=c_var, i=c_m2:
                  e.tensor_sub(out=o, in0=o, in1=i), reads=[b_st], writes=[b_st])
                O("scalar", lambda e, o=c_rs, i=c_var, b_=epsc[:, 0:1]:
                  e.activation(out=o, in_=i, func=AF.Ln, bias=b_),
                  reads=[b_st, b_epsc], writes=[b_st])
                O("scalar", lambda e, o=c_rs:
                  e.activation(out=o, in_=o, func=AF.Exp, scale=-0.5),
                  reads=[b_st], writes=[b_st])
                O("vector", lambda e, o=c_nm, i=c_m, s1=c_rs:
                  e.tensor_scalar(out=o, in0=i, scalar1=s1, scalar2=-1.0,
                                  op0=ALU.mult, op1=ALU.mult),
                  reads=[b_st], writes=[b_st])
                O("vector", lambda e, o=xnt[:], i=xt[:], s1=c_rs, s2=c_nm:
                  e.tensor_scalar(out=o, in0=i, scalar1=s1, scalar2=s2,
                                  op0=ALU.mult, op1=ALU.add),
                  reads=[b_xt, b_st], writes=[b_xnt])
                if with_ffn:
                    O("sync", lambda e, a=xn_tm_d[ct * 128:(ct + 1) * 128, :],
                      s=xnt[:]: e.dma_start(out=a, in_=s),
                      reads=[b_xnt], writes=[b_xntm], dma=True, no_waw=True)
                for cc in range(CCN):
                    pt, b_pt = ptr[cc % 2]
                    O("tensor", lambda e, o=pt[:], i=xnt[:, cc * 128:(cc + 1) * 128],
                      ii=idb[:]: e.transpose(o, i, ii),
                      reads=[b_xnt, b_idb], writes=[b_pt])
                    dst = xnfm[:, cc * TOK + ct * 128:cc * TOK + (ct + 1) * 128]
                    if cc % 2 == 0:
                        O("scalar", lambda e, o=dst, i=pt[:]: e.copy(out=o, in_=i),
                          reads=[b_pt], writes=[b_xnfm], no_waw=True)
                    else:
                        O("vector", lambda e, o=dst, i=pt[:]: e.tensor_copy(out=o, in_=i),
                          reads=[b_pt], writes=[b_xnfm], no_waw=True)
            kb.barrier()

        if with_ffn:
            with contextlib.ExitStack() as p2:
                lgT, b_lgT = mk(p2, "lgT", [NE, TOK], F32, "right")
                lg_tm, b_lgtm = mk(p2, "lg_tm", [128, NCT * NE], F32, "right")
                eq1, b_eq1 = mk(p2, "eq1", [128, NCT * NE], F32, "right")
                eq2, b_eq2 = mk(p2, "eq2", [128, NCT * NE], F32, "right")
                l2t, b_l2 = mk(p2, "l2t", [128, NCT * NE], F32, "right")
                me_bf, b_mebf = mk(p2, "me_bf", [128, NCT * NE], BF16, "right")
                ic_sb, b_ic = mk(p2, "ic_sb", [128, NCT * NE], F32, "right")
                pos_t, b_pos = mk(p2, "pos_t", [128, NCT * NE], F32, "right")
                m1t, b_m1 = mk(p2, "m1t", [128, NCT], F32, "right")
                m2t, b_m2 = mk(p2, "m2t", [128, NCT], F32, "right")
                dtt, b_dtt = mk(p2, "dtt", [128, NCT], F32, "right")
                edt, b_ed = mk(p2, "edt", [128, NCT], F32, "right")
                aA, b_aA = mk(p2, "aA", [128, NCT], F32, "right")
                aB, b_aB = mk(p2, "aB", [128, NCT], F32, "right")
                tmp1, b_t1 = mk(p2, "tmp1", [128, NCT], F32, "right")
                tot_sb, b_tot = mk(p2, "tot_sb", [1, NCT * NE], F32, "right")
                totcb, b_totcb = mk(p2, "totcb", [128, 2], BF16, "right")
                off_sb, b_off = mk(p2, "off_sb", [128, 2], F32, "right")
                offr, b_offr = mk(p2, "offr", [1, NCT * NE], F32, "right")
                offm, b_offm = mk(p2, "offm", [128, NCT * NE], F32, "right")
                ps_lg = mkp(p2, "ps_lg", [NE, 512], F32)
                b_pslg = Buf("pslg")
                ps_ic = mkp(p2, "ps_ic", [128, NCT * NE], F32)
                b_psic = Buf("psic")
                ps_sm = mkp(p2, "ps_sm", [128, 256], F32)
                b_pssm = Buf("pssm")
                ps_ofb = mkp(p2, "ps_ofb", [128, NCT * NE], F32)
                b_psofb = Buf("psofb")

                for tt in range(TOK // 512):
                    mms = [lambda e, o=ps_lg[:], l_=wg_bf[:, cc * NE:(cc + 1) * NE],
                           r_=xnfm[:, cc * TOK + tt * 512:cc * TOK + (tt + 1) * 512],
                           f=(cc == 0), la=(cc == CCN - 1):
                           e.matmul(o, l_, r_, start=f, stop=la)
                           for cc in range(CCN)]
                    MG(b_pslg, mms, [b_wgbf, b_xnfm])
                    O("scalar", lambda e, o=lgT[:, tt * 512:(tt + 1) * 512], i=ps_lg[:]:
                      e.copy(out=o, in_=i), reads=[b_pslg], writes=[b_lgT], no_waw=True)
                for ct in range(NCT):
                    O("tensor", lambda e, o=ps_sm[0:128, 0:NE],
                      i=lgT[:, ct * 128:(ct + 1) * 128], ii=idf[0:NE, 0:NE]:
                      e.transpose(o, i, ii), reads=[b_lgT, b_idf], writes=[b_pssm])
                    O("vector", lambda e, o=lg_tm[:, ct * NE:(ct + 1) * NE],
                      i=ps_sm[0:128, 0:NE]: e.tensor_copy(out=o, in_=i),
                      reads=[b_pssm], writes=[b_lgtm], no_waw=True)

                def esl(t, e_):
                    return t[:, e_::NE]

                O("vector", lambda e, o=m1t[:], i=esl(lg_tm, 0):
                  e.tensor_copy(out=o, in_=i), reads=[b_lgtm], writes=[b_m1])
                for e_ in range(1, NE):
                    O("vector", lambda e, o=m1t[:], i=esl(lg_tm, e_):
                      e.tensor_max(out=o, in0=o, in1=i),
                      reads=[b_lgtm, b_m1], writes=[b_m1])
                for e_ in range(NE):
                    O("vector", lambda e, o=esl(eq1, e_), i=esl(lg_tm, e_), m=m1t[:]:
                      e.tensor_tensor(out=o, in0=i, in1=m, op=ALU.is_ge),
                      reads=[b_lgtm, b_m1], writes=[b_eq1], no_waw=True)
                for e_ in range(NE):
                    O("vector", lambda e, o=esl(l2t, e_), i=esl(eq1, e_),
                      i1=esl(lg_tm, e_):
                      e.scalar_tensor_tensor(out=o, in0=i, scalar=-1.0e30, in1=i1,
                                             op0=ALU.mult, op1=ALU.add),
                      reads=[b_eq1, b_lgtm], writes=[b_l2], no_waw=True)
                O("vector", lambda e, o=m2t[:], i=esl(l2t, 0):
                  e.tensor_copy(out=o, in_=i), reads=[b_l2], writes=[b_m2])
                for e_ in range(1, NE):
                    O("vector", lambda e, o=m2t[:], i=esl(l2t, e_):
                      e.tensor_max(out=o, in0=o, in1=i),
                      reads=[b_l2, b_m2], writes=[b_m2])
                for e_ in range(NE):
                    O("vector", lambda e, o=esl(eq2, e_), i=esl(l2t, e_), m=m2t[:]:
                      e.tensor_tensor(out=o, in0=i, in1=m, op=ALU.is_ge),
                      reads=[b_l2, b_m2], writes=[b_eq2], no_waw=True)
                O("vector", lambda e, o=dtt[:], i=m2t[:], i1=m1t[:]:
                  e.tensor_sub(out=o, in0=i, in1=i1),
                  reads=[b_m1, b_m2], writes=[b_dtt])
                O("scalar", lambda e, o=edt[:], i=dtt[:]:
                  e.activation(out=o, in_=i, func=AF.Exp),
                  reads=[b_dtt], writes=[b_ed])
                O("vector", lambda e, o=edt[:]:
                  e.tensor_single_scalar(out=o, in_=o, scalar=1.0, op=ALU.add),
                  reads=[b_ed], writes=[b_ed])
                O("vector", lambda e, o=w1_t[:], i=edt[:]: e.reciprocal(out=o, in_=i),
                  reads=[b_ed], writes=[b_w1])
                O("vector", lambda e, o=w2_t[:], i=w1_t[:]:
                  e.tensor_scalar(out=o, in0=i, scalar1=-1.0, scalar2=1.0,
                                  op0=ALU.mult, op1=ALU.add),
                  reads=[b_w1], writes=[b_w2])
                O("vector", lambda e, o=me_bf[:], i=eq1[:], i1=eq2[:]:
                  e.tensor_add(out=o, in0=i, in1=i1),
                  reads=[b_eq1, b_eq2], writes=[b_mebf])
                MG(b_psic, [lambda e, o=ps_ic[:], l_=csum_t[:], r_=me_bf[:]:
                            e.matmul(o, l_, r_, start=True, stop=True)],
                   [b_csum, b_mebf])
                O("scalar", lambda e, o=ic_sb[:], i=ps_ic[:]: e.copy(out=o, in_=i),
                  reads=[b_psic], writes=[b_ic])
                MG(b_pssm, [lambda e, o=ps_sm[0:1, 0:NCT * NE], l_=onesc[:], r_=me_bf[:]:
                            e.matmul(o, l_, r_, start=True, stop=True)],
                   [b_onesc, b_mebf])
                O("scalar", lambda e, o=tot_sb[:], i=ps_sm[0:1, 0:NCT * NE]:
                  e.copy(out=o, in_=i), reads=[b_pssm], writes=[b_tot])
                for j in range(2):
                    O("tensor", lambda e, o=ps_sm[0:128, 0:1],
                      i=tot_sb[:, j * 128:(j + 1) * 128], ii=idf[0:1, 0:1]:
                      e.transpose(o, i, ii), reads=[b_tot, b_idf], writes=[b_pssm])
                    O("vector", lambda e, o=totcb[:, j:j + 1], i=ps_sm[0:128, 0:1]:
                      e.tensor_copy(out=o, in_=i),
                      reads=[b_pssm], writes=[b_totcb], no_waw=True)
                MG(b_pssm, [lambda e, o=ps_sm[0:128, 1:2], l_=m0_t[:], r_=totcb[:, 0:1]:
                            e.matmul(o, l_, r_, start=True, stop=True)],
                   [b_m0, b_totcb])
                O("vector", lambda e, o=off_sb[:, 0:1], i=ps_sm[0:128, 1:2]:
                  e.tensor_copy(out=o, in_=i), reads=[b_pssm], writes=[b_off],
                  no_waw=True)
                MG(b_pssm, [
                    lambda e, o=ps_sm[0:128, 2:3], l_=m01_t[:], r_=totcb[:, 0:1]:
                    e.matmul(o, l_, r_, start=True, stop=False),
                    lambda e, o=ps_sm[0:128, 2:3], l_=m0_t[:], r_=totcb[:, 1:2]:
                    e.matmul(o, l_, r_, start=False, stop=True)],
                   [b_m0, b_m01, b_totcb])
                O("vector", lambda e, o=off_sb[:, 1:2], i=ps_sm[0:128, 2:3]:
                  e.tensor_copy(out=o, in_=i), reads=[b_pssm], writes=[b_off],
                  no_waw=True)
                for j in range(2):
                    O("tensor", lambda e, o=ps_sm[0:1, 0:128], i=off_sb[:, j:j + 1],
                      ii=idf[:]: e.transpose(o, i, ii),
                      reads=[b_off, b_idf], writes=[b_pssm])
                    O("vector", lambda e, o=offr[:, j * 128:(j + 1) * 128],
                      i=ps_sm[0:1, 0:128]: e.tensor_copy(out=o, in_=i),
                      reads=[b_pssm], writes=[b_offr], no_waw=True)
                MG(b_psofb, [lambda e, o=ps_ofb[:], l_=onesr[:], r_=offr[:]:
                             e.matmul(o, l_, r_, start=True, stop=True)],
                   [b_onesr, b_offr])
                O("vector", lambda e, o=offm[:], i=ps_ofb[:]:
                  e.tensor_single_scalar(out=o, in_=i, scalar=-1.0, op=ALU.add),
                  reads=[b_psofb], writes=[b_offm])
                O("vector", lambda e, o=pos_t[:], i=ic_sb[:], i1=offm[:]:
                  e.tensor_add(out=o, in0=i, in1=i1),
                  reads=[b_ic, b_offm], writes=[b_pos])
                O("vector", lambda e, o=pos_t[:]:
                  e.tensor_single_scalar(out=o, in_=o, scalar=float(CCAP),
                                         op=ALU.min),
                  reads=[b_pos], writes=[b_pos])
                for dst_t, b_dst, eqt, b_eqt, adst, b_adst in (
                        (aA, b_aA, eq1, b_eq1, addrA, b_addrA),
                        (aB, b_aB, eq2, b_eq2, addrB, b_addrB)):
                    O("vector", lambda e, a=dst_t[:]: e.memset(a, 0.0), writes=[b_dst])
                    for e_ in range(NE):
                        O("vector", lambda e, o=tmp1[:], i=esl(pos_t, e_),
                          s=float(e_ * ES):
                          e.tensor_single_scalar(out=o, in_=i, scalar=s, op=ALU.add),
                          reads=[b_pos], writes=[b_t1])
                        O("vector", lambda e, o=tmp1[:], i=esl(eqt, e_):
                          e.tensor_mul(out=o, in0=o, in1=i),
                          reads=[b_t1, b_eqt], writes=[b_t1])
                        O("vector", lambda e, o=dst_t[:], i=tmp1[:]:
                          e.tensor_add(out=o, in0=o, in1=i),
                          reads=[b_t1, b_dst], writes=[b_dst])
                    O("vector", lambda e, o=dst_t[:]:
                      e.tensor_single_scalar(out=o, in_=o,
                                             scalar=float(NE * ES - 1), op=ALU.min),
                      reads=[b_dst], writes=[b_dst])
                    O("vector", lambda e, o=adst[:], i=dst_t[:]:
                      e.tensor_copy(out=o, in_=i), reads=[b_dst], writes=[b_adst])
                if debug:
                    O("sync", lambda e, a=dbg["lgtm"], s=lg_tm[:]: e.dma_start(out=a, in_=s),
                      reads=[b_lgtm], dma=True)
                    O("sync", lambda e, a=dbg["pos"], s=pos_t[:]: e.dma_start(out=a, in_=s),
                      reads=[b_pos], dma=True)
                for ct in range(NCT):
                    for a_t, b_a, val_t, b_val, dram, b_dr in (
                            (addrA, b_addrA, tokid, b_tokid, idx_dram, b_idxdram),
                            (addrB, b_addrB, tokid, b_tokid, idx_dram, b_idxdram),
                            (addrA, b_addrA, tokid, b_tokid, idxo_dram, b_idxodram),
                            (addrB, b_addrB, tokidb, b_tokidb, idxo_dram, b_idxodram),
                            (addrA, b_addrA, w1_t, b_w1, w_dram, b_wdram),
                            (addrB, b_addrB, w2_t, b_w2, w_dram, b_wdram)):
                        O("gpsimd", lambda e, od=dram, ofs=a_t[:, ct:ct + 1],
                          src=val_t[:, ct:ct + 1]:
                          e.indirect_dma_start(
                              out=od, out_offset=IndirectOffsetOnAxis(ofs, 0),
                              in_=src, in_offset=None),
                          reads=[b_a, b_val, b_dr], writes=[b_dr],
                          dma=True, no_waw=True)
            kb.barrier(engines=("tensor", "vector", "scalar", "sync"),
                       lane_queues=("sync",))

        if with_attn:
            with contextlib.ExitStack() as p3:
                q_fm, b_qfm = mk(p3, "q_fm", [128, 2 * NH * SEQ], BF16, "right")
                k_fm, b_kfm = mk(p3, "k_fm", [128, 2 * SEQ], BF16, "right")
                v_tm, b_vtm = mk(p3, "v_tm", [128, (SEQ // 128) * HD], BF16, "right")
                sqe, b_sqe = mk(p3, "sqe", [128, 512], BF16, "right")
                sqo, b_sqo = mk(p3, "sqo", [128, 512], BF16, "right")
                rstd_p, b_rstdp = mk(p3, "rstd_p", [1, 512], F32, "right")
                rstd_b, b_rstdb = mk(p3, "rstd_b", [1, 512], BF16, "right")
                sce, b_sce = mk(p3, "sce", [128, 512], F32, "right")
                sco, b_sco = mk(p3, "sco", [128, 512], F32, "right")
                ne_b, b_neb = mk(p3, "ne_b", [128, 512], BF16, "right")
                no_b, b_nob = mk(p3, "no_b", [128, 512], BF16, "right")
                r1, b_r1 = mk(p3, "r1", [128, 512], BF16, "right")
                r2, b_r2 = mk(p3, "r2", [128, 512], BF16, "right")
                s_sb = [mk(p3, f"s_sb{i}", [128, 640], F32, "right") for i in range(2)]
                pn_b = [mk(p3, f"pn_b{i}", [128, 640], BF16, "right") for i in range(2)]
                mx = [mk(p3, f"mx_t{i}", [128, 4], F32, "right") for i in range(2)]
                pT_sb, b_pT = mk(p3, "pT_sb", [128, 640], BF16, "right")
                afm = [mk(p3, f"afm{i}", [128, CCN * 128], BF16, "right")
                       for i in range(2)]
                wo_sb = [mk(p3, f"wo_sb{i}", [128, C], BF16, "right") for i in range(2)]

                for s_i in range(NSEQ):
                    if attn_level < 1:
                        break
                    with contextlib.ExitStack() as pp:
                        ps_a = mkp(pp, "ps_a", [128, 512], F32)
                        b_psa = Buf("ps_a")
                        ps_b = mkp(pp, "ps_b", [128, 512], F32)
                        b_psb = Buf("ps_b")
                        ps_ms = mkp(pp, "ps_ms", [1, 512], F32)
                        b_psms = Buf("ps_ms")
                        ps_sc = mkp(pp, "ps_sc", [128, 512], F32)
                        b_pssc = Buf("ps_sc")

                        def proj_pair(dst_t, b_dst, dst_base, w_t, b_w, wchunk,
                                      wcol0, gam_t, b_gam, tt):
                            t0 = s_i * SEQ + tt * 512
                            mmsE = [lambda e, o=ps_a[:],
                                    l_=w_t[:, cc * wchunk + wcol0:cc * wchunk + wcol0 + 128],
                                    r_=xnfm[:, cc * TOK + t0:cc * TOK + t0 + 512],
                                    f=(cc == 0), la=(cc == CCN - 1):
                                    e.matmul(o, l_, r_, start=f, stop=la)
                                    for cc in range(CCN)]
                            MG(b_psa, mmsE, [b_w, b_xnfm])
                            mmsO = [lambda e, o=ps_b[:],
                                    l_=w_t[:, cc * wchunk + wcol0 + 128:
                                           cc * wchunk + wcol0 + 256],
                                    r_=xnfm[:, cc * TOK + t0:cc * TOK + t0 + 512],
                                    f=(cc == 0), la=(cc == CCN - 1):
                                    e.matmul(o, l_, r_, start=f, stop=la)
                                    for cc in range(CCN)]
                            MG(b_psb, mmsO, [b_w, b_xnfm])
                            O("scalar", lambda e, o=sqe[:], i=ps_a[:]:
                              e.activation(out=o, in_=i, func=AF.Square),
                              reads=[b_psa], writes=[b_sqe])
                            O("scalar", lambda e, o=sqo[:], i=ps_b[:]:
                              e.activation(out=o, in_=i, func=AF.Square),
                              reads=[b_psb], writes=[b_sqo])
                            MG(b_psms, [
                                lambda e, o=ps_ms[:], l_=onesc[:], r_=sqe[:]:
                                e.matmul(o, l_, r_, start=True, stop=False),
                                lambda e, o=ps_ms[:], l_=onesc[:], r_=sqo[:]:
                                e.matmul(o, l_, r_, start=False, stop=True)],
                               [b_onesc, b_sqe, b_sqo])
                            O("scalar", lambda e, o=rstd_p[:], i=ps_ms[:],
                              b_=epsc[0:1, 1:2]:
                              e.activation(out=o, in_=i, func=AF.Ln,
                                           scale=1.0 / HD, bias=b_),
                              reads=[b_psms, b_epsc], writes=[b_rstdp])
                            O("scalar", lambda e, o=rstd_b[:], i=rstd_p[:]:
                              e.activation(out=o, in_=i, func=AF.Exp, scale=-0.5),
                              reads=[b_rstdp], writes=[b_rstdb])
                            MG(b_pssc, [lambda e, o=ps_sc[:], l_=gam_t[:, 0:128],
                                        r_=rstd_b[:]:
                                        e.matmul(o, l_, r_, start=True, stop=True)],
                               [b_gam, b_rstdb])
                            O("vector", lambda e, o=sce[:], i=ps_sc[:]:
                              e.tensor_copy(out=o, in_=i),
                              reads=[b_pssc], writes=[b_sce])
                            MG(b_pssc, [lambda e, o=ps_sc[:], l_=gam_t[:, 128:256],
                                        r_=rstd_b[:]:
                                        e.matmul(o, l_, r_, start=True, stop=True)],
                               [b_gam, b_rstdb])
                            O("vector", lambda e, o=sco[:], i=ps_sc[:]:
                              e.tensor_copy(out=o, in_=i),
                              reads=[b_pssc], writes=[b_sco])
                            O("vector", lambda e, o=ne_b[:], i=ps_a[:], i1=sce[:]:
                              e.tensor_mul(out=o, in0=i, in1=i1),
                              reads=[b_psa, b_sce, b_sqe], writes=[b_neb])
                            O("vector", lambda e, o=no_b[:], i=ps_b[:], i1=sco[:]:
                              e.tensor_mul(out=o, in0=i, in1=i1),
                              reads=[b_psb, b_sco, b_sqo], writes=[b_nob])
                            csl = slice(tt * 512, (tt + 1) * 512)
                            O("vector", lambda e, o=r1[:], i=ne_b[:], i1=cos_t[:, csl]:
                              e.tensor_mul(out=o, in0=i, in1=i1),
                              reads=[b_neb, b_cos], writes=[b_r1])
                            O("vector", lambda e, o=r2[:], i=no_b[:], i1=sin_t[:, csl]:
                              e.tensor_mul(out=o, in0=i, in1=i1),
                              reads=[b_nob, b_sin], writes=[b_r2])
                            O("vector", lambda e,
                              o=dst_t[:, dst_base + tt * 512:dst_base + (tt + 1) * 512],
                              i=r1[:], i1=r2[:]: e.tensor_sub(out=o, in0=i, in1=i1),
                              reads=[b_r1, b_r2], writes=[b_dst], no_waw=True)
                            O("vector", lambda e, o=r1[:], i=ne_b[:], i1=sin_t[:, csl]:
                              e.tensor_mul(out=o, in0=i, in1=i1),
                              reads=[b_neb, b_sin], writes=[b_r1])
                            O("vector", lambda e, o=r2[:], i=no_b[:], i1=cos_t[:, csl]:
                              e.tensor_mul(out=o, in0=i, in1=i1),
                              reads=[b_nob, b_cos], writes=[b_r2])
                            O("vector", lambda e,
                              o=dst_t[:, dst_base + SEQ + tt * 512:
                                      dst_base + SEQ + (tt + 1) * 512],
                              i=r1[:], i1=r2[:]: e.tensor_add(out=o, in0=i, in1=i1),
                              reads=[b_r1, b_r2], writes=[b_dst], no_waw=True)

                        for tt in range(SEQ // 512):
                            proj_pair(k_fm, b_kfm, 0, wk_bf, b_wkbf, HD, 0,
                                      kg_t, b_kg, tt)
                        for h in range(NH):
                            for tt in range(SEQ // 512):
                                proj_pair(q_fm, b_qfm, (2 * h) * SEQ, wq_bf, b_wqbf,
                                          C, h * HD, qg_t, b_qg, tt)
                        for tc in range(SEQ // 128):
                            t0 = s_i * SEQ + tc * 128
                            mms = [lambda e, o=ps_a[:, 0:HD],
                                   l_=xnfm[:, cc * TOK + t0:cc * TOK + t0 + 128],
                                   r_=wv_bf[:, cc * HD:(cc + 1) * HD],
                                   f=(cc == 0), la=(cc == CCN - 1):
                                   e.matmul(o, l_, r_, start=f, stop=la)
                                   for cc in range(CCN)]
                            MG(b_psa, mms, [b_xnfm, b_wvbf])
                            O("scalar", lambda e, o=v_tm[:, tc * HD:(tc + 1) * HD],
                              i=ps_a[:, 0:HD]: e.copy(out=o, in_=i),
                              reads=[b_psa], writes=[b_vtm], no_waw=True)
                        kb.barrier(("tensor", "vector", "scalar"))

                    if attn_level < 2:
                        continue
                    with contextlib.ExitStack() as pa:
                        psA = [(mkp(pa, f"psA{i}", [128, 512], F32),
                                Buf(f"psA{i}")) for i in range(2)]
                        psDp = [(mkp(pa, f"psD{i}", [128, 128], F32),
                                 Buf(f"psD{i}")) for i in range(2)]
                        psav = [(mkp(pa, f"psav{i}", [128, 256], F32),
                                 Buf(f"psav{i}")) for i in range(1)]
                        pspt = [(mkp(pa, f"pspt{i}", [128, 128], BF16),
                                 Buf(f"pspt{i}")) for i in range(2)]
                        pswo = mkp(pa, "pswo", [128, 512], F32)
                        b_pswo = Buf("pswo")

                        iters = [(qb, h) for qb in (qb_list if qb_list is not None
                                                    else range(SEQ // 128))
                                 for h in range(NH)]

                        def emit_qk(i):
                            qb, h = iters[i]
                            hp = i % 2
                            sA, b_sA = psA[hp]
                            nA = min(qb, 4)
                            wA = nA * 128
                            j0 = qb * 128 - wA
                            qsl_e = q_fm[:, 2 * h * SEQ + qb * 128:
                                         2 * h * SEQ + (qb + 1) * 128]
                            qsl_o = q_fm[:, (2 * h + 1) * SEQ + qb * 128:
                                         (2 * h + 1) * SEQ + (qb + 1) * 128]
                            if nA:
                                MG(b_sA, [
                                    lambda e, o=sA[:, 0:wA], l_=qsl_e,
                                    r_=k_fm[:, j0:j0 + wA]:
                                    e.matmul(o, l_, r_, start=True, stop=False),
                                    lambda e, o=sA[:, 0:wA], l_=qsl_o,
                                    r_=k_fm[:, SEQ + j0:SEQ + j0 + wA]:
                                    e.matmul(o, l_, r_, start=False, stop=True)],
                                   [b_qfm, b_kfm])
                            pD, b_pD = psDp[hp]
                            MG(b_pD, [
                                lambda e, o=pD[:], l_=qsl_e,
                                r_=k_fm[:, qb * 128:(qb + 1) * 128]:
                                e.matmul(o, l_, r_, start=True, stop=False),
                                lambda e, o=pD[:], l_=qsl_o,
                                r_=k_fm[:, SEQ + qb * 128:SEQ + (qb + 1) * 128]:
                                e.matmul(o, l_, r_, start=False, stop=True)],
                               [b_qfm, b_kfm])

                        def emit_rest(i):
                            qb, h = iters[i]
                            hp = i % 2
                            par = qb % 2
                            afm_t, b_afm = afm[par]
                            sA, b_sA = psA[hp]
                            pD, b_pD = psDp[hp]
                            ssb, b_ssb = s_sb[hp]
                            pnb, b_pnb = pn_b[hp]
                            mx_t, b_mx = mx[hp]
                            nA = min(qb, 4)
                            wA = nA * 128
                            j0 = qb * 128 - wA
                            if nA:
                                if qb >= 4:
                                    O("vector", lambda e, o=ssb[:, 0:512],
                                      i=sA[:, 0:512], i1=mleft5[:]:
                                      e.tensor_tensor(out=o, in0=i, in1=i1,
                                                      op=ALU.add),
                                      reads=[b_sA, b_mleft5], writes=[b_ssb],
                                      no_waw=True)
                                else:
                                    O("scalar", lambda e, o=ssb[:, 0:wA],
                                      i=sA[:, 0:wA]: e.copy(out=o, in_=i),
                                      reads=[b_sA], writes=[b_ssb], no_waw=True)
                            O("vector", lambda e, o=ssb[:, wA:wA + 128], i=pD[:],
                              i1=mdiag[:]:
                              e.tensor_tensor(out=o, in0=i, in1=i1, op=ALU.add),
                              reads=[b_pD, b_mdiag], writes=[b_ssb], no_waw=True)
                            wtot = wA + 128
                            O("vector", lambda e, o=mx_t[:, 0:1], i=ssb[:, 0:wtot]:
                              e.reduce_max(out=o, in_=i, axis=AX.X, negate=True),
                              reads=[b_ssb], writes=[b_mx])
                            O("scalar", lambda e, o=ssb[:, 0:wtot],
                              i=ssb[:, 0:wtot], b_=mx_t[:, 0:1], a=mx_t[:, 2:3]:
                              e.activation(out=o, in_=i, func=AF.Exp,
                                           bias=b_, accum_out=a),
                              reads=[b_ssb, b_mx], writes=[b_ssb, b_mx],
                              no_waw=True)
                            O("vector", lambda e, o=mx_t[:, 3:4], i=mx_t[:, 2:3]:
                              e.reciprocal(out=o, in_=i),
                              reads=[b_mx], writes=[b_mx])
                            O("vector", lambda e, o=pnb[:, 0:wtot],
                              i=ssb[:, 0:wtot], s1=mx_t[:, 3:4]:
                              e.tensor_scalar(out=o, in0=i, scalar1=s1,
                                              scalar2=None, op0=ALU.mult),
                              reads=[b_ssb, b_mx], writes=[b_pnb])
                            for jc in range(nA + 1):
                                pt, b_pt = pspt[jc % 2]
                                O("tensor", lambda e, o=pt[:],
                                  i=pnb[:, jc * 128:(jc + 1) * 128], ii=idb[:]:
                                  e.transpose(o, i, ii),
                                  reads=[b_pnb, b_idb], writes=[b_pt])
                                dst = pT_sb[:, jc * 128:(jc + 1) * 128]
                                if jc % 2 == 0:
                                    O("scalar", lambda e, o=dst, i=pt[:]:
                                      e.copy(out=o, in_=i),
                                      reads=[b_pt], writes=[b_pT], no_waw=True)
                                else:
                                    O("vector", lambda e, o=dst, i=pt[:]:
                                      e.tensor_copy(out=o, in_=i),
                                      reads=[b_pt], writes=[b_pT], no_waw=True)
                            av, b_av = psav[0]
                            for dc in range(2):
                                mms = []
                                for jc in range(nA + 1):
                                    jcg = j0 // 128 + jc
                                    mms.append(
                                        lambda e, o=av[:, dc * 128:(dc + 1) * 128],
                                        l_=v_tm[:, jcg * HD + dc * 128:
                                                jcg * HD + (dc + 1) * 128],
                                        r_=pT_sb[:, jc * 128:(jc + 1) * 128],
                                        f=(jc == 0), la=(jc == nA):
                                        e.matmul(o, l_, r_, start=f, stop=la))
                                MG(b_av, mms, [b_vtm, b_pT], no_waw=(dc == 1))
                            for dc in range(2):
                                O("scalar", lambda e,
                                  o=afm_t[:, (2 * h + dc) * 128:(2 * h + dc + 1) * 128],
                                  i=av[:, dc * 128:(dc + 1) * 128]:
                                  e.copy(out=o, in_=i),
                                  reads=[b_av], writes=[b_afm], no_waw=True)
                            if h == NH - 1:
                                for ot in range(2):
                                    mms = [lambda e, o=pswo[:],
                                           l_=afm_t[:, dc_ * 128:(dc_ + 1) * 128],
                                           r_=wo_bfw[:, dc_ * C + ot * 512:
                                                     dc_ * C + (ot + 1) * 512],
                                           f=(dc_ == 0), la=(dc_ == CCN - 1):
                                           e.matmul(o, l_, r_, start=f, stop=la)
                                           for dc_ in range(CCN)]
                                    MG(b_pswo, mms, [b_afm, b_wobf])
                                    O("scalar", lambda e,
                                      o=wo_sb[par][0][:, ot * 512:(ot + 1) * 512],
                                      i=pswo[:]: e.copy(out=o, in_=i),
                                      reads=[b_pswo], writes=[wo_sb[par][1]],
                                      no_waw=(ot == 1))
                                O("sync", lambda e,
                                  a=wo_tm_d[s_i * SEQ + qb * 128:
                                            s_i * SEQ + (qb + 1) * 128, :],
                                  s=wo_sb[par][0][:]: e.dma_start(out=a, in_=s),
                                  reads=[wo_sb[par][1]], writes=[b_wotm],
                                  dma=True, no_waw=True)

                        for ii_ in range(len(iters)):
                            emit_qk(ii_)
                            if ii_ > 0:
                                emit_rest(ii_ - 1)
                        if iters:
                            emit_rest(len(iters) - 1)
                        kb.barrier(("tensor", "vector", "scalar"))
        if (not with_attn) or attn_level < 5:
            for ct in range(NCT):
                O("sync", lambda e, a=wo_tm_d[ct * 128:(ct + 1) * 128, :], s=zrow[:]:
                  e.dma_start(out=a, in_=s),
                  reads=[b_zrow], writes=[b_wotm], dma=True, no_waw=True)
        kb.barrier()
        aw_stack.close()

        if with_ffn:
            with contextlib.ExitStack() as p4:
                xg2 = [mk(p4, f"xg_fm{i}", [128, CCN * CCAP], BF16, "right")
                       for i in range(2)]
                h_fm, b_hfm = mk(p4, "h_fm", [128, (FD // 128) * CCAP], BF16, "right")
                wgu_sb = [mk(p4, f"wgu_sb{i}", [128, CCN * 1024], BF16, "right")
                          for i in range(2)]
                wd_sb, b_wdsb = mk(p4, "wd_sb", [128, (FD // 128) * C], BF16, "right")
                idx2 = [mk(p4, f"idx_sb{i}", [128, NSC], I32, "right")
                        for i in range(2)]
                idxo2 = [mk(p4, f"idxo_sb{i}", [128, NSC], I32, "right")
                         for i in range(2)]
                wsl2 = [mk(p4, f"wsl_sb{i}", [128, NSC], F32, "right")
                        for i in range(2)]
                xg_tm = [mk(p4, f"xg_tm{i}", [128, C], BF16, "right") for i in range(2)]
                gg_sb = [mk(p4, f"gg_sb{i}", [128, 512], BF16, "right") for i in range(2)]
                dout_sb = [mk(p4, f"dout_sb{i}", [128, C], BF16, "right")
                           for i in range(2)]
                ptm = [(mkp(p4, f"ptm{i}", [128, 128], BF16),
                        Buf(f"ptm{i}")) for i in range(2)]
                psg = [(mkp(p4, f"psg{i}", [128, 512], F32),
                        Buf(f"psg{i}")) for i in range(2)]
                psu = [(mkp(p4, f"psu{i}", [128, 512], F32),
                        Buf(f"psu{i}")) for i in range(2)]
                psd = [(mkp(p4, f"psd{i}", [128, 512], F32),
                        Buf(f"psd{i}")) for i in range(2)]

                def fetch_expert(e_i):
                    ep = e_i % 2
                    idx_sb, b_idxsb = idx2[ep]
                    idxo_sb, b_idxosb = idxo2[ep]
                    wsl_sb, b_wsl = wsl2[ep]
                    xg_fm, b_xgfm = xg2[ep]
                    for sc in range(NSC):
                        r0 = e_i * ES + sc * 128
                        O("sync", lambda e, a=idx_sb[:, sc:sc + 1],
                          s=idx_dram[r0:r0 + 128, :]: e.dma_start(out=a, in_=s),
                          reads=[b_idxdram], writes=[b_idxsb], dma=True,
                          no_waw=(sc > 0))
                        O("sync", lambda e, a=idxo_sb[:, sc:sc + 1],
                          s=idxo_dram[r0:r0 + 128, :]: e.dma_start(out=a, in_=s),
                          reads=[b_idxodram], writes=[b_idxosb], dma=True,
                          no_waw=(sc > 0))
                        O("sync", lambda e, a=wsl_sb[:, sc:sc + 1],
                          s=w_dram[r0:r0 + 128, :]: e.dma_start(out=a, in_=s),
                          reads=[b_wdram], writes=[b_wsl], dma=True,
                          no_waw=(sc > 0))
                    for sc in range(NSC):
                        gt, b_gt = xg_tm[sc % 2]
                        O("gpsimd", lambda e, a=gt[:], src=xn_tm_d,
                          ofs=idx_sb[:, sc:sc + 1]:
                          e.indirect_dma_start(
                              out=a, out_offset=None, in_=src,
                              in_offset=IndirectOffsetOnAxis(ofs, 0)),
                          reads=[b_idxsb, b_xntm], writes=[b_gt], dma=True)
                        for cc in range(CCN):
                            pt, b_pt = ptm[cc % 2]
                            O("tensor", lambda e, o=pt[:],
                              i=gt[:, cc * 128:(cc + 1) * 128], ii=idb[:]:
                              e.transpose(o, i, ii),
                              reads=[b_gt, b_idb], writes=[b_pt])
                            dst = xg_fm[:, cc * CCAP + sc * 128:cc * CCAP + (sc + 1) * 128]
                            if cc % 2 == 0:
                                O("scalar", lambda e, o=dst, i=pt[:]:
                                  e.copy(out=o, in_=i),
                                  reads=[b_pt], writes=[b_xgfm], no_waw=True)
                            else:
                                O("vector", lambda e, o=dst, i=pt[:]:
                                  e.tensor_copy(out=o, in_=i),
                                  reads=[b_pt], writes=[b_xgfm], no_waw=True)

                fetch_expert(0)
                for e_i in range(NE):
                    idx_sb, b_idxsb = idx2[e_i % 2]
                    idxo_sb, b_idxosb = idxo2[e_i % 2]
                    wsl_sb, b_wsl = wsl2[e_i % 2]
                    xg_fm, b_xgfm = xg2[e_i % 2]
                    for q4 in range(4):
                        wbuf, b_wbuf = wgu_sb[q4 % 2]
                        for cc in range(CCN):
                            O("gpsimd", lambda e, a=wbuf[:, cc * 1024:cc * 1024 + 512],
                              s=wgu_d[e_i, cc, :, q4 * 512:(q4 + 1) * 512]:
                              e.dma_start(out=a, in_=s),
                              reads=[b_wgu_bf], writes=[b_wbuf], dma=True,
                              no_waw=(cc > 0))
                            O("gpsimd", lambda e, a=wbuf[:, cc * 1024 + 512:(cc + 1) * 1024],
                              s=wgu_d[e_i, cc, :, FD + q4 * 512:FD + (q4 + 1) * 512]:
                              e.dma_start(out=a, in_=s),
                              reads=[b_wgu_bf], writes=[b_wbuf], dma=True, no_waw=True)
                        NTILES = [(i * 512, min(512, CCAP - i * 512))
                                  for i in range((CCAP + 511) // 512)]
                        for fi in range(4):
                            fc = q4 * 4 + fi
                            for nt, (n0, nw) in enumerate(NTILES):
                                pg, b_pg = psg[nt % 2]
                                pu, b_pu = psu[nt % 2]
                                mms = [lambda e, o=pg[:, 0:nw],
                                       l_=wbuf[:, cc * 1024 + fi * 128:
                                               cc * 1024 + (fi + 1) * 128],
                                       r_=xg_fm[:, cc * CCAP + n0:
                                                cc * CCAP + n0 + nw],
                                       f=(cc == 0), la=(cc == CCN - 1):
                                       e.matmul(o, l_, r_, start=f, stop=la)
                                       for cc in range(CCN)]
                                MG(b_pg, mms, [b_wbuf, b_xgfm])
                                mms = [lambda e, o=pu[:, 0:nw],
                                       l_=wbuf[:, cc * 1024 + 512 + fi * 128:
                                               cc * 1024 + 512 + (fi + 1) * 128],
                                       r_=xg_fm[:, cc * CCAP + n0:
                                                cc * CCAP + n0 + nw],
                                       f=(cc == 0), la=(cc == CCN - 1):
                                       e.matmul(o, l_, r_, start=f, stop=la)
                                       for cc in range(CCN)]
                                MG(b_pu, mms, [b_wbuf, b_xgfm])
                                gg, b_gg = gg_sb[nt % 2]
                                O("scalar", lambda e, o=gg[:, 0:nw], i=pg[:, 0:nw]:
                                  e.activation(out=o, in_=i, func=AF.Gelu),
                                  reads=[b_pg], writes=[b_gg])
                                O("vector", lambda e,
                                  o=h_fm[:, fc * CCAP + n0:fc * CCAP + n0 + nw],
                                  i=pu[:, 0:nw], i1=gg[:, 0:nw]:
                                  e.tensor_mul(out=o, in0=i, in1=i1),
                                  reads=[b_pu, b_gg], writes=[b_hfm], no_waw=True)
                    if e_i + 1 < NE:
                        fetch_expert(e_i + 1)
                    for fc in range(FD // 128):
                        O("gpsimd", lambda e, a=wd_sb[:, fc * C:(fc + 1) * C],
                          s=wd_d[e_i, fc]: e.dma_start(out=a, in_=s),
                          reads=[b_wd_bf], writes=[b_wdsb], dma=True,
                          no_waw=(fc > 0))
                    for sc in range(NSC):
                        do_t, b_do = dout_sb[sc % 2]
                        for ctile in range(2):
                            pd, b_pd = psd[ctile % 2]
                            mms = [lambda e, o=pd[:],
                                   l_=h_fm[:, fc_ * CCAP + sc * 128:
                                           fc_ * CCAP + (sc + 1) * 128],
                                   r_=wd_sb[:, fc_ * C + ctile * 512:
                                            fc_ * C + (ctile + 1) * 512],
                                   f=(fc_ == 0), la=(fc_ == FD // 128 - 1):
                                   e.matmul(o, l_, r_, start=f, stop=la)
                                   for fc_ in range(FD // 128)]
                            MG(b_pd, mms, [b_hfm, b_wdsb])
                            O("scalar", lambda e,
                              o=do_t[:, ctile * 512:(ctile + 1) * 512], i=pd[:],
                              s1=wsl_sb[:, sc:sc + 1]:
                              e.activation(out=o, in_=i, func=AF.Copy, scale=s1),
                              reads=[b_pd, b_wsl], writes=[b_do],
                              no_waw=(ctile > 0))
                        O("gpsimd", lambda e, od=contrib_d,
                          ofs=idxo_sb[:, sc:sc + 1], s=do_t[:]:
                          e.indirect_dma_start(
                              out=od, out_offset=IndirectOffsetOnAxis(ofs, 0),
                              in_=s, in_offset=None),
                          reads=[b_do, b_idxosb, b_contribz], writes=[b_contrib],
                          dma=True, no_waw=True)
            kb.barrier()

        if with_ffn and debug:
            O("sync", lambda e, a=dbg["addrA"], s=addrA[:]: e.dma_start(out=a, in_=s),
              reads=[b_addrA], dma=True)
            O("sync", lambda e, a=dbg["addrB"], s=addrB[:]: e.dma_start(out=a, in_=s),
              reads=[b_addrB], dma=True)
            O("sync", lambda e, a=dbg["w1"], s=w1_t[:]: e.dma_start(out=a, in_=s),
              reads=[b_w1], dma=True)
            O("sync", lambda e, a=dbg["w2"], s=w2_t[:]: e.dma_start(out=a, in_=s),
              reads=[b_w2], dma=True)
            O("gpsimd", lambda e, a=dbg["idx"], s=idx_dram: e.dma_start(out=a, in_=s),
              reads=[b_idxdram], dma=True)
            O("gpsimd", lambda e, a=dbg["warr"], s=w_dram: e.dma_start(out=a, in_=s),
              reads=[b_wdram], dma=True)
            O("gpsimd", lambda e, a=dbg["xntm"], s=xn_tm_d: e.dma_start(out=a, in_=s),
              reads=[b_xntm], dma=True)
            O("gpsimd", lambda e, a=dbg["dout"], s=dout_d: e.dma_start(out=a, in_=s),
              reads=[b_dout, b_doutz], dma=True)

        with contextlib.ExitStack() as p5:
            NPAR5 = 3
            xt5 = [mk(p5, f"xt5{i}", [128, C], F32, "right") for i in range(NPAR5)]
            wt5 = [mk(p5, f"wt5{i}", [128, C], BF16, "right") for i in range(NPAR5)]
            ga5 = [mk(p5, f"ga5{i}", [128, C], BF16, "right") for i in range(NPAR5)]
            gb5 = [mk(p5, f"gb5{i}", [128, C], BF16, "right") for i in range(NPAR5)]
            o5 = [mk(p5, f"o5{i}", [128, C], F32, "right") for i in range(NPAR5)]
            t5 = [mk(p5, f"t5{i}", [128, C], F32, "right") for i in range(NPAR5)]

            for ct in range(NCT):
                par = ct % NPAR5
                xt, b_xt = xt5[par]
                wt, b_wt = wt5[par]
                ga, b_ga = ga5[par]
                gb, b_gb = gb5[par]
                ot, b_ot = o5[par]
                tt_, b_tt = t5[par]
                rows = slice(ct * 128, (ct + 1) * 128)
                O("sync", lambda e, a=xt[:], s=x_d[rows, :]: e.dma_start(out=a, in_=s),
                  writes=[b_xt], dma=True)
                O("sync", lambda e, a=wt[:], s=wo_tm_d[rows, :]:
                  e.dma_start(out=a, in_=s),
                  reads=[b_wotm], writes=[b_wt], dma=True)
                if with_ffn:
                    O("sync", lambda e, a=ga[:],
                      s=contrib_d[ct * 128:(ct + 1) * 128, :]:
                      e.dma_start(out=a, in_=s),
                      reads=[b_contrib, b_contribz], writes=[b_ga], dma=True)
                    O("gpsimd", lambda e, a=gb[:],
                      s=contrib_d[TOK + 128 + ct * 128:TOK + 128 + (ct + 1) * 128, :]:
                      e.dma_start(out=a, in_=s),
                      reads=[b_contrib, b_contribz], writes=[b_gb], dma=True)
                    eng5 = "gpsimd" if ct % 3 == 2 else "vector"
                    O(eng5, lambda e, o=tt_[:], i=ga[:], i1=gb[:]:
                      e.tensor_add(out=o, in0=i, in1=i1),
                      reads=[b_ga, b_gb], writes=[b_tt])
                    O(eng5, lambda e, o=tt_[:], i=wt[:]:
                      e.tensor_add(out=o, in0=o, in1=i),
                      reads=[b_tt, b_wt], writes=[b_tt])
                else:
                    eng5 = "vector"
                    O("vector", lambda e, o=tt_[:], i=wt[:]: e.tensor_copy(out=o, in_=i),
                      reads=[b_wt], writes=[b_tt])
                O(eng5, lambda e, o=ot[:], i=tt_[:], i1=xt[:]:
                  e.tensor_add(out=o, in0=i, in1=i1),
                  reads=[b_tt, b_xt], writes=[b_ot])
                O("sync", lambda e, a=out_d[rows, :], s=ot[:]: e.dma_start(out=a, in_=s),
                  reads=[b_ot], dma=True)

        kb.finalize()
    return nc



def _perm_cols(n_out):
    p = np.empty(n_out, dtype=np.int64)
    for n in range(n_out):
        h, r = divmod(n, 256)
        half, m = divmod(r, 128)
        p[n] = 256 * h + 2 * m + half
    return p


def _host_consts():
    c = {}
    i = np.arange(128)
    c["ident_f"] = np.eye(128, dtype=np.float32)
    c["ident_bf"] = np.eye(128, dtype=bf16)
    c["csum_ut"] = (i[:, None] <= i[None, :]).astype(bf16)
    e_of = i % NE
    ct_of = i // NE
    c["m0"] = ((e_of[:, None] == e_of[None, :]) &
               (ct_of[:, None] < ct_of[None, :])).astype(bf16)
    c["m01"] = (e_of[:, None] == e_of[None, :]).astype(bf16)
    c["mdiag"] = np.where(i[None, :] <= i[:, None], 0.0, -1.0e30).astype(np.float32)
    c["mleft"] = np.where(i[None, :] >= i[:, None], 0.0, -1.0e30).astype(np.float32)
    c["mleft5"] = np.concatenate([c["mleft"], np.zeros((128, 384), np.float32)],
                                 axis=1)
    c["m_idx"] = np.arange(128, dtype=np.float32).reshape(128, 1)
    c["t_row"] = np.arange(SEQ, dtype=np.float32).reshape(1, SEQ)
    c["ones_r"] = np.ones((1, 128), dtype=np.float32)
    c["ones_c"] = np.ones((128, 1), dtype=bf16)
    c["ones_cf"] = np.ones((128, 1), dtype=np.float32)
    c["tokid"] = (np.arange(NCT)[None, :] * 128 +
                  np.arange(128)[:, None]).astype(np.int32)
    c["tokid_b"] = (c["tokid"] + (TOK + 128)).astype(np.int32)
    c["epsc"] = np.tile(np.array([[1e-5, 1e-6]], dtype=np.float32), (128, 1))
    return c


def prep_inputs(inputs):
    x = np.asarray(inputs["L0_attn_x"])
    wq = np.asarray(inputs["L0_attn_Wq"])
    wk = np.asarray(inputs["L0_attn_Wk"])
    wv = np.asarray(inputs["L0_attn_Wv"])
    wo = np.asarray(inputs["L0_attn_Wo"])
    qg = np.asarray(inputs["L0_attn_q_gamma"])
    kg = np.asarray(inputs["L0_attn_k_gamma"])
    wg = np.asarray(inputs["L0_ffn_W_gate"])
    wgu = np.asarray(inputs["L0_ffn_W_gate_up"])
    wd = np.asarray(inputs["L0_ffn_W_down"])

    pq = _perm_cols(C)
    pk = _perm_cols(HD)
    shared = {
        "wqT": np.ascontiguousarray(wq.T[:, pq], dtype=np.float32),
        "wkT": np.ascontiguousarray(wk.T[:, pk], dtype=np.float32),
        "wvT": np.ascontiguousarray(wv.T, dtype=np.float32),
        "woT": np.ascontiguousarray(wo.T, dtype=np.float32),
        "qg": np.ascontiguousarray((qg[pk] / 16.0).reshape(1, HD)).astype(bf16),
        "kg": np.ascontiguousarray(kg[pk].reshape(1, HD)).astype(bf16),
        "wgT": np.ascontiguousarray(wg.T, dtype=np.float32),
        "wgu": np.ascontiguousarray(
            wgu.transpose(0, 2, 1).reshape(NE, CCN, 128, 2 * FD), dtype=np.float32),
        "wd": np.ascontiguousarray(
            wd.transpose(0, 2, 1).reshape(NE, FD // 128, 128, C), dtype=np.float32),
    }
    shared.update(_host_consts())
    in_maps = []
    for core in range(NCORES):
        m = dict(shared)
        m["x2d"] = np.ascontiguousarray(
            x[core * NSEQ:(core + 1) * NSEQ].reshape(TOK, C), dtype=np.float32)
        in_maps.append(m)
    return in_maps


def kernel(**inputs):
    in_maps = prep_inputs(inputs)
    nc = build_program()
    res = run_bass_kernel_spmd(nc, in_maps, core_ids=list(range(NCORES)))
    outs = [np.asarray(r["out"], dtype=np.float32) for r in res.results]
    return np.concatenate(outs, axis=0).reshape(16, SEQ, C)


# version 12
# speedup vs baseline: 1.0749x; 1.0749x over previous
import contextlib
import math
import numpy as np
import ml_dtypes
import concourse.bass as bass
import concourse.mybir as mybir
from concourse.bass import IndirectOffsetOnAxis
from concourse.bass_utils import run_bass_kernel_spmd

F32 = mybir.dt.float32
BF16 = mybir.dt.bfloat16
I32 = mybir.dt.int32
AF = mybir.ActivationFunctionType
ALU = mybir.AluOpType
AX = mybir.AxisListType
bf16 = ml_dtypes.bfloat16

NCORES = 8
SEQ = 2048
NSEQ = 2
TOK = SEQ * NSEQ
C = 1024
CCN = 8
HD = 256
NH = 4
NE = 8
FD = 2048
CCAP = 1280
NSC = CCAP // 128
ES = CCAP + 128
NCT = TOK // 128
TWO_PI_HI = float(np.float32(2.0 * math.pi))
TWO_PI_LO = float(np.float32(2.0 * math.pi - np.float64(np.float32(2.0 * math.pi))))

ENGINES = ("tensor", "vector", "scalar", "gpsimd", "sync")


class Sem:
    __slots__ = ("h", "n")

    def __init__(self, h):
        self.h = h
        self.n = 0

    def inc(self, k=1):
        self.n += k
        return self.n


class Buf:
    __slots__ = ("w", "r", "name")

    def __init__(self, name=""):
        self.w = []
        self.r = []
        self.name = name


def _run_stream(stream):
    def run(eng):
        for it in stream:
            k = it[0]
            if k == "w":
                eng.wait_ge(it[1], it[2])
            elif k == "b":
                eng.sem_inc(it[1], 1)
                eng.wait_ge(it[1], it[2])
            else:
                _, fn, semh, delta = it
                inst = fn(eng)
                if semh is not None:
                    inst.then_inc(semh, delta)
    return run


class Kb:
    def __init__(self, nc, ctx):
        self.nc = nc
        self.ctx = ctx
        self.streams = {e: [] for e in ENGINES}
        self.seen = {e: {} for e in ENGINES}
        self.eng_sem = {e: Sem(ctx.enter_context(nc.semaphore(f"sem_{e}")))
                        for e in ("tensor", "vector", "scalar", "gpsimd")}
        self.lanes = {
            "sync": [Sem(ctx.enter_context(nc.semaphore(f"qs{i}"))) for i in range(8)],
            "gpsimd": [Sem(ctx.enter_context(nc.semaphore(f"qg{i}"))) for i in range(8)],
        }
        self.pf_lanes = {
            "gpsimd": [Sem(ctx.enter_context(nc.semaphore(f"qgp{i}"))) for i in range(4)],
        }
        self.lane_rr = {k: 0 for k in self.lanes}
        self.pf_rr = {k: 0 for k in self.pf_lanes}
        self.bar_sem = Sem(ctx.enter_context(nc.semaphore("bar")))

    def _wait(self, eng, ev):
        sem, val = ev
        key = id(sem)
        if self.seen[eng].get(key, -1) >= val:
            return
        self.seen[eng][key] = val
        self.streams[eng].append(("w", sem.h, val))

    def op(self, eng, fn, reads=(), writes=(), dma=False, prefetch=False,
           no_waw=False):
        for b in reads:
            for ev in b.w:
                self._wait(eng, ev)
        for b in writes:
            if not no_waw:
                for ev in b.w:
                    self._wait(eng, ev)
            for ev in b.r:
                self._wait(eng, ev)
        if dma:
            if prefetch:
                ls = self.pf_lanes[eng]
                s = ls[self.pf_rr[eng] % len(ls)]
                self.pf_rr[eng] += 1
            else:
                ls = self.lanes[eng]
                s = ls[self.lane_rr[eng] % len(ls)]
                self.lane_rr[eng] += 1
            if s.n:
                self._wait(eng, (s, s.n))
            delta = 16
        else:
            s = self.eng_sem[eng]
            delta = 1
        v = s.inc(delta)
        ev = (s, v)
        self.streams[eng].append(("i", fn, s.h, delta))
        for b in writes:
            if no_waw:
                b.w.append(ev)
            else:
                b.w = [ev]
                b.r = []
        for b in reads:
            b.r.append(ev)
        return ev

    def mm_group(self, psum_buf, mms, read_bufs, no_waw=False):
        eng = "tensor"
        for b in read_bufs:
            for ev in b.w:
                self._wait(eng, ev)
        if not no_waw:
            for ev in psum_buf.w:
                self._wait(eng, ev)
        for ev in psum_buf.r:
            self._wait(eng, ev)
        for fn in mms[:-1]:
            self.streams[eng].append(("i", fn, None, None))
        s = self.eng_sem[eng]
        v = s.inc(1)
        self.streams[eng].append(("i", mms[-1], s.h, 1))
        ev = (s, v)
        if no_waw:
            psum_buf.w.append(ev)
        else:
            psum_buf.w = [ev]
            psum_buf.r = []
        for b in read_bufs:
            b.r.append(ev)
        return ev

    def barrier(self, engines=("tensor", "vector", "scalar", "gpsimd", "sync"),
                lane_queues=("sync", "gpsimd")):
        full = len(engines) == 5
        target = self.bar_sem.n + len(engines)
        self.bar_sem.n = target
        for e in engines:
            if e in self.eng_sem:
                self._wait(e, (self.eng_sem[e], self.eng_sem[e].n))
            if full and e == "sync":
                for q in lane_queues:
                    for s in self.lanes[q]:
                        if s.n:
                            self._wait(e, (s, s.n))
            self.streams[e].append(("b", self.bar_sem.h, target))
            self.seen[e][id(self.bar_sem)] = target

    def finalize(self):
        for lanes in self.lanes.values():
            for s in lanes:
                if s.n:
                    self._wait("sync", (s, s.n))
        for lanes in self.pf_lanes.values():
            for s in lanes:
                if s.n:
                    self._wait("sync", (s, s.n))
        blk = self.ctx.enter_context(self.nc.Block())
        for name in ENGINES:
            getattr(blk, name)(_run_stream(self.streams[name]))



def build_program(with_ffn=True, with_attn=True, attn_level=5, qb_list=None, debug=False):
    nc = bass.Bass("TRN2", target_bir_lowering=False, debug=False)
    dt = nc.dram_tensor
    x_d = dt("x2d", [TOK, C], F32, kind="ExternalInput").ap()
    wq_d = dt("wqT", [C, C], F32, kind="ExternalInput").ap()
    wk_d = dt("wkT", [C, HD], F32, kind="ExternalInput").ap()
    wv_d = dt("wvT", [C, HD], F32, kind="ExternalInput").ap()
    wo_d = dt("woT", [C, C], F32, kind="ExternalInput").ap()
    qg_d = dt("qg", [1, HD], BF16, kind="ExternalInput").ap()
    kg_d = dt("kg", [1, HD], BF16, kind="ExternalInput").ap()
    wg_d = dt("wgT", [C, NE], F32, kind="ExternalInput").ap()
    wgu_d = dt("wgu", [NE, CCN, 128, 2 * FD], F32, kind="ExternalInput").ap()
    wd_d = dt("wd", [NE, FD // 128, 128, C], F32, kind="ExternalInput").ap()
    idf_d = dt("ident_f", [128, 128], F32, kind="ExternalInput").ap()
    idb_d = dt("ident_bf", [128, 128], BF16, kind="ExternalInput").ap()
    csum_d = dt("csum_ut", [128, 128], BF16, kind="ExternalInput").ap()
    m0_d = dt("m0", [128, 128], BF16, kind="ExternalInput").ap()
    m01_d = dt("m01", [128, 128], BF16, kind="ExternalInput").ap()
    mdiag_d = dt("mdiag", [128, 128], F32, kind="ExternalInput").ap()
    mleft_d = dt("mleft", [128, 128], F32, kind="ExternalInput").ap()
    mleft5_d = dt("mleft5", [128, 512], F32, kind="ExternalInput").ap()
    midx_d = dt("m_idx", [128, 1], F32, kind="ExternalInput").ap()
    trow_d = dt("t_row", [1, SEQ], F32, kind="ExternalInput").ap()
    onesr_d = dt("ones_r", [1, 128], F32, kind="ExternalInput").ap()
    onesc_d = dt("ones_c", [128, 1], BF16, kind="ExternalInput").ap()
    onescf_d = dt("ones_cf", [128, 1], F32, kind="ExternalInput").ap()
    tokid_d = dt("tokid", [128, NCT], I32, kind="ExternalInput").ap()
    epsc_d = dt("epsc", [128, 2], F32, kind="ExternalInput").ap()
    out_d = dt("out", [TOK, C], F32, kind="ExternalOutput").ap()
    dbg = {}
    if debug:
        dbg["addrA"] = dt("dbg_addrA", [128, NCT], I32, kind="ExternalOutput").ap()
        dbg["addrB"] = dt("dbg_addrB", [128, NCT], I32, kind="ExternalOutput").ap()
        dbg["w1"] = dt("dbg_w1", [128, NCT], F32, kind="ExternalOutput").ap()
        dbg["w2"] = dt("dbg_w2", [128, NCT], F32, kind="ExternalOutput").ap()
        dbg["idx"] = dt("dbg_idx", [NE * ES, 1], I32, kind="ExternalOutput").ap()
        dbg["warr"] = dt("dbg_warr", [NE * ES, 1], F32, kind="ExternalOutput").ap()
        dbg["lgtm"] = dt("dbg_lgtm", [128, NCT * NE], F32, kind="ExternalOutput").ap()
        dbg["pos"] = dt("dbg_pos", [128, NCT * NE], F32, kind="ExternalOutput").ap()
        dbg["xntm"] = dt("dbg_xntm", [TOK + 128, C], BF16, kind="ExternalOutput").ap()
        dbg["dout"] = dt("dbg_dout", [NE * ES, C], BF16, kind="ExternalOutput").ap()

    xn_tm_d = dt("xn_tm", [TOK + 128, C], BF16, kind="Internal").ap()
    wo_tm_d = dt("wo_tm", [TOK, C], BF16, kind="Internal").ap()
    dout_d = dt("dout", [NE * ES, C], BF16, kind="Internal").ap()
    idx_dram = dt("idx_arr", [NE * ES, 1], I32, kind="Internal").ap()
    w_dram = dt("w_arr", [NE * ES, 1], F32, kind="Internal").ap()

    with contextlib.ExitStack() as ctx:
        E = ctx.enter_context
        kb = Kb(nc, ctx)
        O = kb.op
        MG = kb.mm_group

        _uid = [0]

        def mk(stack, name, shape, dtype, side="left"):
            _uid[0] += 1
            t = stack.enter_context(
                nc.sbuf_tensor(f"{name}_{_uid[0]}", shape, dtype, side=side))
            return t, Buf(name)

        def mkp(stack, name, shape, dtype):
            _uid[0] += 1
            return stack.enter_context(
                nc.psum_tensor(f"{name}_{_uid[0]}", shape, dtype))

        idf, b_idf = mk(ctx, "idf", [128, 128], F32)
        idb, b_idb = mk(ctx, "idb", [128, 128], BF16)
        csum_t, b_csum = mk(ctx, "csum_t", [128, 128], BF16)
        m0_t, b_m0 = mk(ctx, "m0_t", [128, 128], BF16)
        m01_t, b_m01 = mk(ctx, "m01_t", [128, 128], BF16)
        mdiag, b_mdiag = mk(ctx, "mdiag_s", [128, 128], F32)
        mleft, b_mleft = mk(ctx, "mleft_s", [128, 128], F32)
        mleft5, b_mleft5 = mk(ctx, "mleft5_s", [128, 512], F32)
        midx, b_midx = mk(ctx, "midx", [128, 1], F32)
        onesr, b_onesr = mk(ctx, "onesr", [1, 128], F32)
        onesc, b_onesc = mk(ctx, "onesc", [128, 1], BF16)
        onescf, b_onescf = mk(ctx, "onescf", [128, 1], F32)
        tokid, b_tokid = mk(ctx, "tokid_s", [128, NCT], I32)
        qg_t, b_qg = mk(ctx, "qg_t", [1, HD], BF16)
        kg_t, b_kg = mk(ctx, "kg_t", [1, HD], BF16)
        cos_t, b_cos = mk(ctx, "cos_t", [128, SEQ], BF16)
        sin_t, b_sin = mk(ctx, "sin_t", [128, SEQ], BF16)
        w1_t, b_w1 = mk(ctx, "w1_t", [128, NCT], F32)
        w2_t, b_w2 = mk(ctx, "w2_t", [128, NCT], F32)
        addrA, b_addrA = mk(ctx, "addrA", [128, NCT], I32)
        addrB, b_addrB = mk(ctx, "addrB", [128, NCT], I32)
        epsc, b_epsc = mk(ctx, "epsc_s", [128, 2], F32)
        zrow, b_zrow = mk(ctx, "zrow", [128, C], BF16)

        b_xntm = Buf("xn_tm")
        b_wotm = Buf("wo_tm")
        b_dout = Buf("dout")
        b_doutz = Buf("dout_zero")
        b_idxdram = Buf("idx_dram")
        b_wdram = Buf("w_dram")
        b_wgu_bf = Buf("wgu_bf_dram")
        b_wd_bf = Buf("wd_bf_dram")

        aw_stack = ctx.enter_context(contextlib.ExitStack())
        xnfm, b_xnfm = mk(aw_stack, "xnfm", [128, CCN * TOK], BF16, side="right")
        wq_bf, b_wqbf = mk(aw_stack, "wq_bf", [128, CCN * C], BF16, side="right")
        wk_bf, b_wkbf = mk(aw_stack, "wk_bf", [128, CCN * HD], BF16, side="right")
        wv_bf, b_wvbf = mk(aw_stack, "wv_bf", [128, CCN * HD], BF16, side="right")
        wo_bfw, b_wobf = mk(aw_stack, "wo_bfw", [128, CCN * C], BF16, side="right")
        wg_bf, b_wgbf = mk(aw_stack, "wg_bf", [128, CCN * NE], BF16, side="right")

        for dst, src, b in ((idf, idf_d, b_idf), (idb, idb_d, b_idb),
                            (csum_t, csum_d, b_csum), (m0_t, m0_d, b_m0),
                            (m01_t, m01_d, b_m01), (mdiag, mdiag_d, b_mdiag),
                            (mleft, mleft_d, b_mleft),
                            (mleft5, mleft5_d, b_mleft5), (midx, midx_d, b_midx),
                            (onesr, onesr_d, b_onesr),
                            (onesc, onesc_d, b_onesc), (onescf, onescf_d, b_onescf),
                            (tokid, tokid_d, b_tokid), (epsc, epsc_d, b_epsc),
                            (qg_t, qg_d, b_qg),
                            (kg_t, kg_d, b_kg)):
            O("sync", lambda e, a=dst[:], s=src: e.dma_start(out=a, in_=s),
              writes=[b], dma=True)

        for cc in range(CCN):
            for wt, wsrc, b, wd_ in ((wq_bf, wq_d, b_wqbf, C), (wk_bf, wk_d, b_wkbf, HD),
                                     (wv_bf, wv_d, b_wvbf, HD), (wo_bfw, wo_d, b_wobf, C),
                                     (wg_bf, wg_d, b_wgbf, NE)):
                O("gpsimd", lambda e, a=wt[:, cc * wd_:(cc + 1) * wd_],
                  s=wsrc[cc * 128:(cc + 1) * 128, :]: e.dma_start(out=a, in_=s),
                  writes=[b], dma=True, no_waw=True)

        O("vector", lambda e, a=zrow[:]: e.memset(a, 0.0), writes=[b_zrow])

        with contextlib.ExitStack() as p0:
            trow, b_trow = mk(p0, "trow", [1, SEQ], F32, "right")
            O("sync", lambda e, a=trow[:], s=trow_d: e.dma_start(out=a, in_=s),
              writes=[b_trow], dma=True)
            base_t, b_base = mk(p0, "base_t", [128, 1], F32, "right")
            lnb_t, b_lnb = mk(p0, "lnb_t", [128, 1], F32, "right")
            expo_t, b_expo = mk(p0, "expo_t", [128, 1], F32, "right")
            invf_t, b_invf = mk(p0, "invf_t", [128, 1], F32, "right")
            invfr, b_invfr = mk(p0, "invfr", [1, 128], F32, "right")
            th_t, b_th = mk(p0, "th_t", [128, 512], F32, "right")
            z_t, b_z = mk(p0, "z_t", [128, 512], F32, "right")
            ki_t, b_ki = mk(p0, "ki_t", [128, 512], I32, "right")
            kf_t, b_kf = mk(p0, "kf_t", [128, 512], F32, "right")
            ph_t, b_ph = mk(p0, "ph_t", [128, 512], F32, "right")
            gm_t, b_gm = mk(p0, "gm_t", [128, 512], F32, "right")
            sh_t, b_sh = mk(p0, "sh_t", [128, 512], F32, "right")
            sq_t, b_sq = mk(p0, "sq_t", [128, 512], F32, "right")
            inv_t, b_inv = mk(p0, "inv_t", [128, NE * ES // 128], I32, "right")
            wz_t, b_wz = mk(p0, "wz_t", [128, NE * ES // 128], F32, "right")
            ps_f = mkp(p0, "ps_f", [128, 512], F32)
            b_psf = Buf("ps_f")
            ps_tr = mkp(p0, "ps_tr", [1, 128], F32)
            b_pstr = Buf("ps_tr")

            if with_ffn:
                O("vector", lambda e, a=inv_t[:]: e.memset(a, TOK), writes=[b_inv])
                O("vector", lambda e, a=wz_t[:]: e.memset(a, 0.0), writes=[b_wz])
                O("gpsimd", lambda e, a=idx_dram.rearrange("(b a) c -> b (a c)", b=128),
                  s=inv_t[:]: e.dma_start(out=a, in_=s),
                  reads=[b_inv], writes=[b_idxdram], dma=True)
                O("gpsimd", lambda e, a=w_dram.rearrange("(b a) c -> b (a c)", b=128),
                  s=wz_t[:]: e.dma_start(out=a, in_=s),
                  reads=[b_wz], writes=[b_wdram], dma=True)
                for e_i in range(NE):
                    O("gpsimd", lambda e,
                      a=dout_d[e_i * ES + CCAP:(e_i + 1) * ES, :], s=zrow[:]:
                      e.dma_start(out=a, in_=s),
                      reads=[b_zrow], writes=[b_doutz], dma=True, no_waw=True)
                O("gpsimd", lambda e, a=xn_tm_d[TOK:TOK + 128, :], s=zrow[:]:
                  e.dma_start(out=a, in_=s),
                  reads=[b_zrow], writes=[b_xntm], dma=True, no_waw=True)

            O("vector", lambda e, a=base_t[:]: e.memset(a, 1.0e6), writes=[b_base])
            O("scalar", lambda e, o=lnb_t[:], i=base_t[:]:
              e.activation(out=o, in_=i, func=AF.Ln), reads=[b_base], writes=[b_lnb])
            O("vector", lambda e, o=expo_t[:], i=midx[:], s1=lnb_t[:]:
              e.tensor_scalar(out=o, in0=i, scalar1=s1, scalar2=-1.0 / 128.0,
                              op0=ALU.mult, op1=ALU.mult),
              reads=[b_midx, b_lnb], writes=[b_expo])
            O("scalar", lambda e, o=invf_t[:], i=expo_t[:]:
              e.activation(out=o, in_=i, func=AF.Exp), reads=[b_expo], writes=[b_invf])
            O("tensor", lambda e, o=ps_tr[:], i=invf_t[:], ii=idf[:]:
              e.transpose(o, i, ii), reads=[b_invf, b_idf], writes=[b_pstr])
            O("scalar", lambda e, o=invfr[:], i=ps_tr[:]: e.copy(out=o, in_=i),
              reads=[b_pstr], writes=[b_invfr])
            for j4 in range(SEQ // 512):
                sl = slice(j4 * 512, (j4 + 1) * 512)
                MG(b_psf, [lambda e, o=ps_f[:], l_=invfr[:], r_=trow[:, sl]:
                           e.matmul(o, l_, r_, start=True, stop=True)],
                   [b_invfr, b_trow])
                O("scalar", lambda e, o=th_t[:], i=ps_f[:]: e.copy(out=o, in_=i),
                  reads=[b_psf], writes=[b_th])
                O("vector", lambda e, o=z_t[:], i=th_t[:]:
                  e.tensor_single_scalar(out=o, in_=i, scalar=1.0 / TWO_PI_HI,
                                         op=ALU.mult), reads=[b_th], writes=[b_z])
                O("vector", lambda e, o=ki_t[:], i=z_t[:]: e.tensor_copy(out=o, in_=i),
                  reads=[b_z], writes=[b_ki])
                O("vector", lambda e, o=kf_t[:], i=ki_t[:]: e.tensor_copy(out=o, in_=i),
                  reads=[b_ki], writes=[b_kf])
                for sc_, b_in in ((-TWO_PI_HI, b_th), (-TWO_PI_LO, None)):
                    O("vector", lambda e, o=ph_t[:], i=kf_t[:], s=sc_,
                      i1=(th_t[:] if sc_ == -TWO_PI_HI else ph_t[:]):
                      e.scalar_tensor_tensor(out=o, in0=i, scalar=s, in1=i1,
                                             op0=ALU.mult, op1=ALU.add),
                      reads=[b_kf, b_th, b_ph], writes=[b_ph])
                for sgn in (1.0, -1.0):
                    O("vector", lambda e, o=gm_t[:], i=ph_t[:], s=sgn:
                      e.tensor_single_scalar(out=o, in_=i, scalar=s * math.pi,
                                             op=(ALU.is_gt if s > 0 else ALU.is_lt)),
                      reads=[b_ph], writes=[b_gm])
                    for cpart in (TWO_PI_HI, TWO_PI_LO):
                        O("vector", lambda e, o=ph_t[:], i=gm_t[:],
                          s=(-cpart if sgn > 0 else cpart), i1=ph_t[:]:
                          e.scalar_tensor_tensor(out=o, in0=i, scalar=s, in1=i1,
                                                 op0=ALU.mult, op1=ALU.add),
                          reads=[b_gm, b_ph], writes=[b_ph])
                O("scalar", lambda e, o=sin_t[:, sl], i=ph_t[:]:
                  e.activation(out=o, in_=i, func=AF.Sin),
                  reads=[b_ph], writes=[b_sin], no_waw=True)
                O("scalar", lambda e, o=sh_t[:], i=ph_t[:]:
                  e.activation(out=o, in_=i, func=AF.Sin, scale=0.5),
                  reads=[b_ph], writes=[b_sh])
                O("vector", lambda e, o=sq_t[:], i=sh_t[:]:
                  e.tensor_mul(out=o, in0=i, in1=i), reads=[b_sh], writes=[b_sq])
                O("vector", lambda e, o=cos_t[:, sl], i=sq_t[:]:
                  e.tensor_scalar(out=o, in0=i, scalar1=-2.0, scalar2=1.0,
                                  op0=ALU.mult, op1=ALU.add),
                  reads=[b_sq], writes=[b_cos], no_waw=True)
            kb.barrier()

        with contextlib.ExitStack() as p1:
            LNP = 4
            xts = [mk(p1, f"xt{i}", [128, C], F32, "right") for i in range(LNP)]
            sqs = [mk(p1, f"sqs{i}", [128, C], F32, "right") for i in range(LNP)]
            xnb = [mk(p1, f"xnb{i}", [128, C], BF16, "right") for i in range(LNP)]
            sts = [mk(p1, f"lnst{i}", [128, 8], F32, "right") for i in range(LNP)]
            ptr = [(mkp(p1, f"ptr{i}", [128, 128], BF16),
                    Buf(f"ptr{i}")) for i in range(2)]

            for ct in range(NCT):
                par = ct % LNP
                xt, b_xt = xts[par]
                sqt, b_sqt = sqs[par]
                xnt, b_xnt = xnb[par]
                stt, b_st = sts[par]
                c_sm, c_s2, c_m, c_m2, c_var, c_rs, c_nm = (
                    stt[:, i:i + 1] for i in range(7))
                O("sync", lambda e, a=xt[:], s=x_d[ct * 128:(ct + 1) * 128, :]:
                  e.dma_start(out=a, in_=s), writes=[b_xt], dma=True)
                O("vector", lambda e, o=c_sm, i=xt[:]:
                  e.reduce_sum(out=o, in_=i, axis=AX.X), reads=[b_xt], writes=[b_st])
                O("scalar", lambda e, o=sqt[:], i=xt[:], a=c_s2:
                  e.activation(out=o, in_=i, func=AF.Square, accum_out=a),
                  reads=[b_xt], writes=[b_sqt, b_st], no_waw=True)
                O("vector", lambda e, o=c_m, i=c_sm:
                  e.tensor_single_scalar(out=o, in_=i, scalar=1.0 / C, op=ALU.mult),
                  reads=[b_st], writes=[b_st])
                O("vector", lambda e, o=c_m2, i=c_m:
                  e.tensor_mul(out=o, in0=i, in1=i), reads=[b_st], writes=[b_st])
                O("vector", lambda e, o=c_var, i=c_s2:
                  e.tensor_single_scalar(out=o, in_=i, scalar=1.0 / C, op=ALU.mult),
                  reads=[b_st], writes=[b_st])
                O("vector", lambda e, o=c_var, i=c_m2:
                  e.tensor_sub(out=o, in0=o, in1=i), reads=[b_st], writes=[b_st])
                O("scalar", lambda e, o=c_rs, i=c_var, b_=epsc[:, 0:1]:
                  e.activation(out=o, in_=i, func=AF.Ln, bias=b_),
                  reads=[b_st, b_epsc], writes=[b_st])
                O("scalar", lambda e, o=c_rs:
                  e.activation(out=o, in_=o, func=AF.Exp, scale=-0.5),
                  reads=[b_st], writes=[b_st])
                O("vector", lambda e, o=c_nm, i=c_m, s1=c_rs:
                  e.tensor_scalar(out=o, in0=i, scalar1=s1, scalar2=-1.0,
                                  op0=ALU.mult, op1=ALU.mult),
                  reads=[b_st], writes=[b_st])
                O("vector", lambda e, o=xnt[:], i=xt[:], s1=c_rs, s2=c_nm:
                  e.tensor_scalar(out=o, in0=i, scalar1=s1, scalar2=s2,
                                  op0=ALU.mult, op1=ALU.add),
                  reads=[b_xt, b_st], writes=[b_xnt])
                if with_ffn:
                    O("sync", lambda e, a=xn_tm_d[ct * 128:(ct + 1) * 128, :],
                      s=xnt[:]: e.dma_start(out=a, in_=s),
                      reads=[b_xnt], writes=[b_xntm], dma=True, no_waw=True)
                for cc in range(CCN):
                    pt, b_pt = ptr[cc % 2]
                    O("tensor", lambda e, o=pt[:], i=xnt[:, cc * 128:(cc + 1) * 128],
                      ii=idb[:]: e.transpose(o, i, ii),
                      reads=[b_xnt, b_idb], writes=[b_pt])
                    dst = xnfm[:, cc * TOK + ct * 128:cc * TOK + (ct + 1) * 128]
                    if cc % 2 == 0:
                        O("scalar", lambda e, o=dst, i=pt[:]: e.copy(out=o, in_=i),
                          reads=[b_pt], writes=[b_xnfm], no_waw=True)
                    else:
                        O("vector", lambda e, o=dst, i=pt[:]: e.tensor_copy(out=o, in_=i),
                          reads=[b_pt], writes=[b_xnfm], no_waw=True)
            kb.barrier()

        if with_ffn:
            with contextlib.ExitStack() as p2:
                lgT, b_lgT = mk(p2, "lgT", [NE, TOK], F32, "right")
                lg_tm, b_lgtm = mk(p2, "lg_tm", [128, NCT * NE], F32, "right")
                eq1, b_eq1 = mk(p2, "eq1", [128, NCT * NE], F32, "right")
                eq2, b_eq2 = mk(p2, "eq2", [128, NCT * NE], F32, "right")
                l2t, b_l2 = mk(p2, "l2t", [128, NCT * NE], F32, "right")
                me_bf, b_mebf = mk(p2, "me_bf", [128, NCT * NE], BF16, "right")
                ic_sb, b_ic = mk(p2, "ic_sb", [128, NCT * NE], F32, "right")
                pos_t, b_pos = mk(p2, "pos_t", [128, NCT * NE], F32, "right")
                m1t, b_m1 = mk(p2, "m1t", [128, NCT], F32, "right")
                m2t, b_m2 = mk(p2, "m2t", [128, NCT], F32, "right")
                dtt, b_dtt = mk(p2, "dtt", [128, NCT], F32, "right")
                edt, b_ed = mk(p2, "edt", [128, NCT], F32, "right")
                aA, b_aA = mk(p2, "aA", [128, NCT], F32, "right")
                aB, b_aB = mk(p2, "aB", [128, NCT], F32, "right")
                tmp1, b_t1 = mk(p2, "tmp1", [128, NCT], F32, "right")
                tot_sb, b_tot = mk(p2, "tot_sb", [1, NCT * NE], F32, "right")
                totcb, b_totcb = mk(p2, "totcb", [128, 2], BF16, "right")
                off_sb, b_off = mk(p2, "off_sb", [128, 2], F32, "right")
                offr, b_offr = mk(p2, "offr", [1, NCT * NE], F32, "right")
                offm, b_offm = mk(p2, "offm", [128, NCT * NE], F32, "right")
                ps_lg = mkp(p2, "ps_lg", [NE, 512], F32)
                b_pslg = Buf("pslg")
                ps_ic = mkp(p2, "ps_ic", [128, NCT * NE], F32)
                b_psic = Buf("psic")
                ps_sm = mkp(p2, "ps_sm", [128, 256], F32)
                b_pssm = Buf("pssm")
                ps_ofb = mkp(p2, "ps_ofb", [128, NCT * NE], F32)
                b_psofb = Buf("psofb")

                for tt in range(TOK // 512):
                    mms = [lambda e, o=ps_lg[:], l_=wg_bf[:, cc * NE:(cc + 1) * NE],
                           r_=xnfm[:, cc * TOK + tt * 512:cc * TOK + (tt + 1) * 512],
                           f=(cc == 0), la=(cc == CCN - 1):
                           e.matmul(o, l_, r_, start=f, stop=la)
                           for cc in range(CCN)]
                    MG(b_pslg, mms, [b_wgbf, b_xnfm])
                    O("scalar", lambda e, o=lgT[:, tt * 512:(tt + 1) * 512], i=ps_lg[:]:
                      e.copy(out=o, in_=i), reads=[b_pslg], writes=[b_lgT], no_waw=True)
                for ct in range(NCT):
                    O("tensor", lambda e, o=ps_sm[0:128, 0:NE],
                      i=lgT[:, ct * 128:(ct + 1) * 128], ii=idf[0:NE, 0:NE]:
                      e.transpose(o, i, ii), reads=[b_lgT, b_idf], writes=[b_pssm])
                    O("vector", lambda e, o=lg_tm[:, ct * NE:(ct + 1) * NE],
                      i=ps_sm[0:128, 0:NE]: e.tensor_copy(out=o, in_=i),
                      reads=[b_pssm], writes=[b_lgtm], no_waw=True)

                def esl(t, e_):
                    return t[:, e_::NE]

                O("vector", lambda e, o=m1t[:], i=esl(lg_tm, 0):
                  e.tensor_copy(out=o, in_=i), reads=[b_lgtm], writes=[b_m1])
                for e_ in range(1, NE):
                    O("vector", lambda e, o=m1t[:], i=esl(lg_tm, e_):
                      e.tensor_max(out=o, in0=o, in1=i),
                      reads=[b_lgtm, b_m1], writes=[b_m1])
                for e_ in range(NE):
                    O("vector", lambda e, o=esl(eq1, e_), i=esl(lg_tm, e_), m=m1t[:]:
                      e.tensor_tensor(out=o, in0=i, in1=m, op=ALU.is_ge),
                      reads=[b_lgtm, b_m1], writes=[b_eq1], no_waw=True)
                for e_ in range(NE):
                    O("vector", lambda e, o=esl(l2t, e_), i=esl(eq1, e_),
                      i1=esl(lg_tm, e_):
                      e.scalar_tensor_tensor(out=o, in0=i, scalar=-1.0e30, in1=i1,
                                             op0=ALU.mult, op1=ALU.add),
                      reads=[b_eq1, b_lgtm], writes=[b_l2], no_waw=True)
                O("vector", lambda e, o=m2t[:], i=esl(l2t, 0):
                  e.tensor_copy(out=o, in_=i), reads=[b_l2], writes=[b_m2])
                for e_ in range(1, NE):
                    O("vector", lambda e, o=m2t[:], i=esl(l2t, e_):
                      e.tensor_max(out=o, in0=o, in1=i),
                      reads=[b_l2, b_m2], writes=[b_m2])
                for e_ in range(NE):
                    O("vector", lambda e, o=esl(eq2, e_), i=esl(l2t, e_), m=m2t[:]:
                      e.tensor_tensor(out=o, in0=i, in1=m, op=ALU.is_ge),
                      reads=[b_l2, b_m2], writes=[b_eq2], no_waw=True)
                O("vector", lambda e, o=dtt[:], i=m2t[:], i1=m1t[:]:
                  e.tensor_sub(out=o, in0=i, in1=i1),
                  reads=[b_m1, b_m2], writes=[b_dtt])
                O("scalar", lambda e, o=edt[:], i=dtt[:]:
                  e.activation(out=o, in_=i, func=AF.Exp),
                  reads=[b_dtt], writes=[b_ed])
                O("vector", lambda e, o=edt[:]:
                  e.tensor_single_scalar(out=o, in_=o, scalar=1.0, op=ALU.add),
                  reads=[b_ed], writes=[b_ed])
                O("vector", lambda e, o=w1_t[:], i=edt[:]: e.reciprocal(out=o, in_=i),
                  reads=[b_ed], writes=[b_w1])
                O("vector", lambda e, o=w2_t[:], i=w1_t[:]:
                  e.tensor_scalar(out=o, in0=i, scalar1=-1.0, scalar2=1.0,
                                  op0=ALU.mult, op1=ALU.add),
                  reads=[b_w1], writes=[b_w2])
                O("vector", lambda e, o=me_bf[:], i=eq1[:], i1=eq2[:]:
                  e.tensor_add(out=o, in0=i, in1=i1),
                  reads=[b_eq1, b_eq2], writes=[b_mebf])
                MG(b_psic, [lambda e, o=ps_ic[:], l_=csum_t[:], r_=me_bf[:]:
                            e.matmul(o, l_, r_, start=True, stop=True)],
                   [b_csum, b_mebf])
                O("scalar", lambda e, o=ic_sb[:], i=ps_ic[:]: e.copy(out=o, in_=i),
                  reads=[b_psic], writes=[b_ic])
                MG(b_pssm, [lambda e, o=ps_sm[0:1, 0:NCT * NE], l_=onesc[:], r_=me_bf[:]:
                            e.matmul(o, l_, r_, start=True, stop=True)],
                   [b_onesc, b_mebf])
                O("scalar", lambda e, o=tot_sb[:], i=ps_sm[0:1, 0:NCT * NE]:
                  e.copy(out=o, in_=i), reads=[b_pssm], writes=[b_tot])
                for j in range(2):
                    O("tensor", lambda e, o=ps_sm[0:128, 0:1],
                      i=tot_sb[:, j * 128:(j + 1) * 128], ii=idf[0:1, 0:1]:
                      e.transpose(o, i, ii), reads=[b_tot, b_idf], writes=[b_pssm])
                    O("vector", lambda e, o=totcb[:, j:j + 1], i=ps_sm[0:128, 0:1]:
                      e.tensor_copy(out=o, in_=i),
                      reads=[b_pssm], writes=[b_totcb], no_waw=True)
                MG(b_pssm, [lambda e, o=ps_sm[0:128, 1:2], l_=m0_t[:], r_=totcb[:, 0:1]:
                            e.matmul(o, l_, r_, start=True, stop=True)],
                   [b_m0, b_totcb])
                O("vector", lambda e, o=off_sb[:, 0:1], i=ps_sm[0:128, 1:2]:
                  e.tensor_copy(out=o, in_=i), reads=[b_pssm], writes=[b_off],
                  no_waw=True)
                MG(b_pssm, [
                    lambda e, o=ps_sm[0:128, 2:3], l_=m01_t[:], r_=totcb[:, 0:1]:
                    e.matmul(o, l_, r_, start=True, stop=False),
                    lambda e, o=ps_sm[0:128, 2:3], l_=m0_t[:], r_=totcb[:, 1:2]:
                    e.matmul(o, l_, r_, start=False, stop=True)],
                   [b_m0, b_m01, b_totcb])
                O("vector", lambda e, o=off_sb[:, 1:2], i=ps_sm[0:128, 2:3]:
                  e.tensor_copy(out=o, in_=i), reads=[b_pssm], writes=[b_off],
                  no_waw=True)
                for j in range(2):
                    O("tensor", lambda e, o=ps_sm[0:1, 0:128], i=off_sb[:, j:j + 1],
                      ii=idf[:]: e.transpose(o, i, ii),
                      reads=[b_off, b_idf], writes=[b_pssm])
                    O("vector", lambda e, o=offr[:, j * 128:(j + 1) * 128],
                      i=ps_sm[0:1, 0:128]: e.tensor_copy(out=o, in_=i),
                      reads=[b_pssm], writes=[b_offr], no_waw=True)
                MG(b_psofb, [lambda e, o=ps_ofb[:], l_=onesr[:], r_=offr[:]:
                             e.matmul(o, l_, r_, start=True, stop=True)],
                   [b_onesr, b_offr])
                O("vector", lambda e, o=offm[:], i=ps_ofb[:]:
                  e.tensor_single_scalar(out=o, in_=i, scalar=-1.0, op=ALU.add),
                  reads=[b_psofb], writes=[b_offm])
                O("vector", lambda e, o=pos_t[:], i=ic_sb[:], i1=offm[:]:
                  e.tensor_add(out=o, in0=i, in1=i1),
                  reads=[b_ic, b_offm], writes=[b_pos])
                O("vector", lambda e, o=pos_t[:]:
                  e.tensor_single_scalar(out=o, in_=o, scalar=float(CCAP),
                                         op=ALU.min),
                  reads=[b_pos], writes=[b_pos])
                for dst_t, b_dst, eqt, b_eqt, adst, b_adst in (
                        (aA, b_aA, eq1, b_eq1, addrA, b_addrA),
                        (aB, b_aB, eq2, b_eq2, addrB, b_addrB)):
                    O("vector", lambda e, a=dst_t[:]: e.memset(a, 0.0), writes=[b_dst])
                    for e_ in range(NE):
                        O("vector", lambda e, o=tmp1[:], i=esl(pos_t, e_),
                          s=float(e_ * ES):
                          e.tensor_single_scalar(out=o, in_=i, scalar=s, op=ALU.add),
                          reads=[b_pos], writes=[b_t1])
                        O("vector", lambda e, o=tmp1[:], i=esl(eqt, e_):
                          e.tensor_mul(out=o, in0=o, in1=i),
                          reads=[b_t1, b_eqt], writes=[b_t1])
                        O("vector", lambda e, o=dst_t[:], i=tmp1[:]:
                          e.tensor_add(out=o, in0=o, in1=i),
                          reads=[b_t1, b_dst], writes=[b_dst])
                    O("vector", lambda e, o=dst_t[:]:
                      e.tensor_single_scalar(out=o, in_=o,
                                             scalar=float(NE * ES - 1), op=ALU.min),
                      reads=[b_dst], writes=[b_dst])
                    O("vector", lambda e, o=adst[:], i=dst_t[:]:
                      e.tensor_copy(out=o, in_=i), reads=[b_dst], writes=[b_adst])
                if debug:
                    O("sync", lambda e, a=dbg["lgtm"], s=lg_tm[:]: e.dma_start(out=a, in_=s),
                      reads=[b_lgtm], dma=True)
                    O("sync", lambda e, a=dbg["pos"], s=pos_t[:]: e.dma_start(out=a, in_=s),
                      reads=[b_pos], dma=True)
                for ct in range(NCT):
                    for a_t, b_a, val_t, b_val, dram, b_dr in (
                            (addrA, b_addrA, tokid, b_tokid, idx_dram, b_idxdram),
                            (addrB, b_addrB, tokid, b_tokid, idx_dram, b_idxdram),
                            (addrA, b_addrA, w1_t, b_w1, w_dram, b_wdram),
                            (addrB, b_addrB, w2_t, b_w2, w_dram, b_wdram)):
                        O("gpsimd", lambda e, od=dram, ofs=a_t[:, ct:ct + 1],
                          src=val_t[:, ct:ct + 1]:
                          e.indirect_dma_start(
                              out=od, out_offset=IndirectOffsetOnAxis(ofs, 0),
                              in_=src, in_offset=None),
                          reads=[b_a, b_val, b_dr], writes=[b_dr],
                          dma=True, no_waw=True)
            kb.barrier(engines=("tensor", "vector", "scalar", "sync"),
                       lane_queues=("sync",))

        if with_attn:
            with contextlib.ExitStack() as p3:
                q_fm, b_qfm = mk(p3, "q_fm", [128, 2 * NH * SEQ], BF16, "right")
                k_fm, b_kfm = mk(p3, "k_fm", [128, 2 * SEQ], BF16, "right")
                v_tm, b_vtm = mk(p3, "v_tm", [128, (SEQ // 128) * HD], BF16, "right")
                sqe, b_sqe = mk(p3, "sqe", [128, 512], BF16, "right")
                sqo, b_sqo = mk(p3, "sqo", [128, 512], BF16, "right")
                rstd_p, b_rstdp = mk(p3, "rstd_p", [1, 512], F32, "right")
                rstd_b, b_rstdb = mk(p3, "rstd_b", [1, 512], BF16, "right")
                sce, b_sce = mk(p3, "sce", [128, 512], F32, "right")
                sco, b_sco = mk(p3, "sco", [128, 512], F32, "right")
                ne_b, b_neb = mk(p3, "ne_b", [128, 512], BF16, "right")
                no_b, b_nob = mk(p3, "no_b", [128, 512], BF16, "right")
                r1, b_r1 = mk(p3, "r1", [128, 512], BF16, "right")
                r2, b_r2 = mk(p3, "r2", [128, 512], BF16, "right")
                s_sb = [mk(p3, f"s_sb{i}", [128, 640], F32, "right") for i in range(2)]
                pn_b = [mk(p3, f"pn_b{i}", [128, 640], BF16, "right") for i in range(2)]
                mx = [mk(p3, f"mx_t{i}", [128, 4], F32, "right") for i in range(2)]
                pT_sb, b_pT = mk(p3, "pT_sb", [128, 640], BF16, "right")
                afm = [mk(p3, f"afm{i}", [128, CCN * 128], BF16, "right")
                       for i in range(2)]
                wo_sb = [mk(p3, f"wo_sb{i}", [128, C], BF16, "right") for i in range(2)]

                for s_i in range(NSEQ):
                    if attn_level < 1:
                        break
                    with contextlib.ExitStack() as pp:
                        ps_a = mkp(pp, "ps_a", [128, 512], F32)
                        b_psa = Buf("ps_a")
                        ps_b = mkp(pp, "ps_b", [128, 512], F32)
                        b_psb = Buf("ps_b")
                        ps_ms = mkp(pp, "ps_ms", [1, 512], F32)
                        b_psms = Buf("ps_ms")
                        ps_sc = mkp(pp, "ps_sc", [128, 512], F32)
                        b_pssc = Buf("ps_sc")

                        def proj_pair(dst_t, b_dst, dst_base, w_t, b_w, wchunk,
                                      wcol0, gam_t, b_gam, tt):
                            t0 = s_i * SEQ + tt * 512
                            mmsE = [lambda e, o=ps_a[:],
                                    l_=w_t[:, cc * wchunk + wcol0:cc * wchunk + wcol0 + 128],
                                    r_=xnfm[:, cc * TOK + t0:cc * TOK + t0 + 512],
                                    f=(cc == 0), la=(cc == CCN - 1):
                                    e.matmul(o, l_, r_, start=f, stop=la)
                                    for cc in range(CCN)]
                            MG(b_psa, mmsE, [b_w, b_xnfm])
                            mmsO = [lambda e, o=ps_b[:],
                                    l_=w_t[:, cc * wchunk + wcol0 + 128:
                                           cc * wchunk + wcol0 + 256],
                                    r_=xnfm[:, cc * TOK + t0:cc * TOK + t0 + 512],
                                    f=(cc == 0), la=(cc == CCN - 1):
                                    e.matmul(o, l_, r_, start=f, stop=la)
                                    for cc in range(CCN)]
                            MG(b_psb, mmsO, [b_w, b_xnfm])
                            O("scalar", lambda e, o=sqe[:], i=ps_a[:]:
                              e.activation(out=o, in_=i, func=AF.Square),
                              reads=[b_psa], writes=[b_sqe])
                            O("scalar", lambda e, o=sqo[:], i=ps_b[:]:
                              e.activation(out=o, in_=i, func=AF.Square),
                              reads=[b_psb], writes=[b_sqo])
                            MG(b_psms, [
                                lambda e, o=ps_ms[:], l_=onesc[:], r_=sqe[:]:
                                e.matmul(o, l_, r_, start=True, stop=False),
                                lambda e, o=ps_ms[:], l_=onesc[:], r_=sqo[:]:
                                e.matmul(o, l_, r_, start=False, stop=True)],
                               [b_onesc, b_sqe, b_sqo])
                            O("scalar", lambda e, o=rstd_p[:], i=ps_ms[:],
                              b_=epsc[0:1, 1:2]:
                              e.activation(out=o, in_=i, func=AF.Ln,
                                           scale=1.0 / HD, bias=b_),
                              reads=[b_psms, b_epsc], writes=[b_rstdp])
                            O("scalar", lambda e, o=rstd_b[:], i=rstd_p[:]:
                              e.activation(out=o, in_=i, func=AF.Exp, scale=-0.5),
                              reads=[b_rstdp], writes=[b_rstdb])
                            MG(b_pssc, [lambda e, o=ps_sc[:], l_=gam_t[:, 0:128],
                                        r_=rstd_b[:]:
                                        e.matmul(o, l_, r_, start=True, stop=True)],
                               [b_gam, b_rstdb])
                            O("vector", lambda e, o=sce[:], i=ps_sc[:]:
                              e.tensor_copy(out=o, in_=i),
                              reads=[b_pssc], writes=[b_sce])
                            MG(b_pssc, [lambda e, o=ps_sc[:], l_=gam_t[:, 128:256],
                                        r_=rstd_b[:]:
                                        e.matmul(o, l_, r_, start=True, stop=True)],
                               [b_gam, b_rstdb])
                            O("vector", lambda e, o=sco[:], i=ps_sc[:]:
                              e.tensor_copy(out=o, in_=i),
                              reads=[b_pssc], writes=[b_sco])
                            O("vector", lambda e, o=ne_b[:], i=ps_a[:], i1=sce[:]:
                              e.tensor_mul(out=o, in0=i, in1=i1),
                              reads=[b_psa, b_sce, b_sqe], writes=[b_neb])
                            O("vector", lambda e, o=no_b[:], i=ps_b[:], i1=sco[:]:
                              e.tensor_mul(out=o, in0=i, in1=i1),
                              reads=[b_psb, b_sco, b_sqo], writes=[b_nob])
                            csl = slice(tt * 512, (tt + 1) * 512)
                            O("vector", lambda e, o=r1[:], i=ne_b[:], i1=cos_t[:, csl]:
                              e.tensor_mul(out=o, in0=i, in1=i1),
                              reads=[b_neb, b_cos], writes=[b_r1])
                            O("vector", lambda e, o=r2[:], i=no_b[:], i1=sin_t[:, csl]:
                              e.tensor_mul(out=o, in0=i, in1=i1),
                              reads=[b_nob, b_sin], writes=[b_r2])
                            O("vector", lambda e,
                              o=dst_t[:, dst_base + tt * 512:dst_base + (tt + 1) * 512],
                              i=r1[:], i1=r2[:]: e.tensor_sub(out=o, in0=i, in1=i1),
                              reads=[b_r1, b_r2], writes=[b_dst], no_waw=True)
                            O("vector", lambda e, o=r1[:], i=ne_b[:], i1=sin_t[:, csl]:
                              e.tensor_mul(out=o, in0=i, in1=i1),
                              reads=[b_neb, b_sin], writes=[b_r1])
                            O("vector", lambda e, o=r2[:], i=no_b[:], i1=cos_t[:, csl]:
                              e.tensor_mul(out=o, in0=i, in1=i1),
                              reads=[b_nob, b_cos], writes=[b_r2])
                            O("vector", lambda e,
                              o=dst_t[:, dst_base + SEQ + tt * 512:
                                      dst_base + SEQ + (tt + 1) * 512],
                              i=r1[:], i1=r2[:]: e.tensor_add(out=o, in0=i, in1=i1),
                              reads=[b_r1, b_r2], writes=[b_dst], no_waw=True)

                        for tt in range(SEQ // 512):
                            proj_pair(k_fm, b_kfm, 0, wk_bf, b_wkbf, HD, 0,
                                      kg_t, b_kg, tt)
                        for h in range(NH):
                            for tt in range(SEQ // 512):
                                proj_pair(q_fm, b_qfm, (2 * h) * SEQ, wq_bf, b_wqbf,
                                          C, h * HD, qg_t, b_qg, tt)
                        for tc in range(SEQ // 128):
                            t0 = s_i * SEQ + tc * 128
                            mms = [lambda e, o=ps_a[:, 0:HD],
                                   l_=xnfm[:, cc * TOK + t0:cc * TOK + t0 + 128],
                                   r_=wv_bf[:, cc * HD:(cc + 1) * HD],
                                   f=(cc == 0), la=(cc == CCN - 1):
                                   e.matmul(o, l_, r_, start=f, stop=la)
                                   for cc in range(CCN)]
                            MG(b_psa, mms, [b_xnfm, b_wvbf])
                            O("scalar", lambda e, o=v_tm[:, tc * HD:(tc + 1) * HD],
                              i=ps_a[:, 0:HD]: e.copy(out=o, in_=i),
                              reads=[b_psa], writes=[b_vtm], no_waw=True)
                        kb.barrier(("tensor", "vector", "scalar"))

                    if attn_level < 2:
                        continue
                    with contextlib.ExitStack() as pa:
                        psA = [(mkp(pa, f"psA{i}", [128, 512], F32),
                                Buf(f"psA{i}")) for i in range(2)]
                        psDp = [(mkp(pa, f"psD{i}", [128, 128], F32),
                                 Buf(f"psD{i}")) for i in range(2)]
                        psav = [(mkp(pa, f"psav{i}", [128, 256], F32),
                                 Buf(f"psav{i}")) for i in range(1)]
                        pspt = [(mkp(pa, f"pspt{i}", [128, 128], BF16),
                                 Buf(f"pspt{i}")) for i in range(2)]
                        pswo = mkp(pa, "pswo", [128, 512], F32)
                        b_pswo = Buf("pswo")

                        iters = [(qb, h) for qb in (qb_list if qb_list is not None
                                                    else range(SEQ // 128))
                                 for h in range(NH)]

                        def emit_qk(i):
                            qb, h = iters[i]
                            hp = i % 2
                            sA, b_sA = psA[hp]
                            nA = min(qb, 4)
                            wA = nA * 128
                            j0 = qb * 128 - wA
                            qsl_e = q_fm[:, 2 * h * SEQ + qb * 128:
                                         2 * h * SEQ + (qb + 1) * 128]
                            qsl_o = q_fm[:, (2 * h + 1) * SEQ + qb * 128:
                                         (2 * h + 1) * SEQ + (qb + 1) * 128]
                            if nA:
                                MG(b_sA, [
                                    lambda e, o=sA[:, 0:wA], l_=qsl_e,
                                    r_=k_fm[:, j0:j0 + wA]:
                                    e.matmul(o, l_, r_, start=True, stop=False),
                                    lambda e, o=sA[:, 0:wA], l_=qsl_o,
                                    r_=k_fm[:, SEQ + j0:SEQ + j0 + wA]:
                                    e.matmul(o, l_, r_, start=False, stop=True)],
                                   [b_qfm, b_kfm])
                            pD, b_pD = psDp[hp]
                            MG(b_pD, [
                                lambda e, o=pD[:], l_=qsl_e,
                                r_=k_fm[:, qb * 128:(qb + 1) * 128]:
                                e.matmul(o, l_, r_, start=True, stop=False),
                                lambda e, o=pD[:], l_=qsl_o,
                                r_=k_fm[:, SEQ + qb * 128:SEQ + (qb + 1) * 128]:
                                e.matmul(o, l_, r_, start=False, stop=True)],
                               [b_qfm, b_kfm])

                        def emit_rest(i):
                            qb, h = iters[i]
                            hp = i % 2
                            par = qb % 2
                            afm_t, b_afm = afm[par]
                            sA, b_sA = psA[hp]
                            pD, b_pD = psDp[hp]
                            ssb, b_ssb = s_sb[hp]
                            pnb, b_pnb = pn_b[hp]
                            mx_t, b_mx = mx[hp]
                            nA = min(qb, 4)
                            wA = nA * 128
                            j0 = qb * 128 - wA
                            if nA:
                                if qb >= 4:
                                    O("vector", lambda e, o=ssb[:, 0:512],
                                      i=sA[:, 0:512], i1=mleft5[:]:
                                      e.tensor_tensor(out=o, in0=i, in1=i1,
                                                      op=ALU.add),
                                      reads=[b_sA, b_mleft5], writes=[b_ssb],
                                      no_waw=True)
                                else:
                                    O("scalar", lambda e, o=ssb[:, 0:wA],
                                      i=sA[:, 0:wA]: e.copy(out=o, in_=i),
                                      reads=[b_sA], writes=[b_ssb], no_waw=True)
                            O("vector", lambda e, o=ssb[:, wA:wA + 128], i=pD[:],
                              i1=mdiag[:]:
                              e.tensor_tensor(out=o, in0=i, in1=i1, op=ALU.add),
                              reads=[b_pD, b_mdiag], writes=[b_ssb], no_waw=True)
                            wtot = wA + 128
                            O("vector", lambda e, o=mx_t[:, 0:1], i=ssb[:, 0:wtot]:
                              e.reduce_max(out=o, in_=i, axis=AX.X, negate=True),
                              reads=[b_ssb], writes=[b_mx])
                            O("scalar", lambda e, o=ssb[:, 0:wtot],
                              i=ssb[:, 0:wtot], b_=mx_t[:, 0:1], a=mx_t[:, 2:3]:
                              e.activation(out=o, in_=i, func=AF.Exp,
                                           bias=b_, accum_out=a),
                              reads=[b_ssb, b_mx], writes=[b_ssb, b_mx],
                              no_waw=True)
                            O("vector", lambda e, o=mx_t[:, 3:4], i=mx_t[:, 2:3]:
                              e.reciprocal(out=o, in_=i),
                              reads=[b_mx], writes=[b_mx])
                            O("vector", lambda e, o=pnb[:, 0:wtot],
                              i=ssb[:, 0:wtot], s1=mx_t[:, 3:4]:
                              e.tensor_scalar(out=o, in0=i, scalar1=s1,
                                              scalar2=None, op0=ALU.mult),
                              reads=[b_ssb, b_mx], writes=[b_pnb])
                            for jc in range(nA + 1):
                                pt, b_pt = pspt[jc % 2]
                                O("tensor", lambda e, o=pt[:],
                                  i=pnb[:, jc * 128:(jc + 1) * 128], ii=idb[:]:
                                  e.transpose(o, i, ii),
                                  reads=[b_pnb, b_idb], writes=[b_pt])
                                dst = pT_sb[:, jc * 128:(jc + 1) * 128]
                                if jc % 2 == 0:
                                    O("scalar", lambda e, o=dst, i=pt[:]:
                                      e.copy(out=o, in_=i),
                                      reads=[b_pt], writes=[b_pT], no_waw=True)
                                else:
                                    O("vector", lambda e, o=dst, i=pt[:]:
                                      e.tensor_copy(out=o, in_=i),
                                      reads=[b_pt], writes=[b_pT], no_waw=True)
                            av, b_av = psav[0]
                            for dc in range(2):
                                mms = []
                                for jc in range(nA + 1):
                                    jcg = j0 // 128 + jc
                                    mms.append(
                                        lambda e, o=av[:, dc * 128:(dc + 1) * 128],
                                        l_=v_tm[:, jcg * HD + dc * 128:
                                                jcg * HD + (dc + 1) * 128],
                                        r_=pT_sb[:, jc * 128:(jc + 1) * 128],
                                        f=(jc == 0), la=(jc == nA):
                                        e.matmul(o, l_, r_, start=f, stop=la))
                                MG(b_av, mms, [b_vtm, b_pT], no_waw=(dc == 1))
                            for dc in range(2):
                                O("scalar", lambda e,
                                  o=afm_t[:, (2 * h + dc) * 128:(2 * h + dc + 1) * 128],
                                  i=av[:, dc * 128:(dc + 1) * 128]:
                                  e.copy(out=o, in_=i),
                                  reads=[b_av], writes=[b_afm], no_waw=True)
                            if h == NH - 1:
                                for ot in range(2):
                                    mms = [lambda e, o=pswo[:],
                                           l_=afm_t[:, dc_ * 128:(dc_ + 1) * 128],
                                           r_=wo_bfw[:, dc_ * C + ot * 512:
                                                     dc_ * C + (ot + 1) * 512],
                                           f=(dc_ == 0), la=(dc_ == CCN - 1):
                                           e.matmul(o, l_, r_, start=f, stop=la)
                                           for dc_ in range(CCN)]
                                    MG(b_pswo, mms, [b_afm, b_wobf])
                                    O("scalar", lambda e,
                                      o=wo_sb[par][0][:, ot * 512:(ot + 1) * 512],
                                      i=pswo[:]: e.copy(out=o, in_=i),
                                      reads=[b_pswo], writes=[wo_sb[par][1]],
                                      no_waw=(ot == 1))
                                O("sync", lambda e,
                                  a=wo_tm_d[s_i * SEQ + qb * 128:
                                            s_i * SEQ + (qb + 1) * 128, :],
                                  s=wo_sb[par][0][:]: e.dma_start(out=a, in_=s),
                                  reads=[wo_sb[par][1]], writes=[b_wotm],
                                  dma=True, no_waw=True)

                        for ii_ in range(len(iters)):
                            emit_qk(ii_)
                            if ii_ > 0:
                                emit_rest(ii_ - 1)
                        if iters:
                            emit_rest(len(iters) - 1)
                        kb.barrier(("tensor", "vector", "scalar"))
        if (not with_attn) or attn_level < 5:
            for ct in range(NCT):
                O("sync", lambda e, a=wo_tm_d[ct * 128:(ct + 1) * 128, :], s=zrow[:]:
                  e.dma_start(out=a, in_=s),
                  reads=[b_zrow], writes=[b_wotm], dma=True, no_waw=True)
        kb.barrier()
        aw_stack.close()

        if with_ffn:
            with contextlib.ExitStack() as p4:
                xg2 = [mk(p4, f"xg_fm{i}", [128, CCN * CCAP], BF16, "right")
                       for i in range(2)]
                h_fm, b_hfm = mk(p4, "h_fm", [128, (FD // 128) * CCAP], BF16, "right")
                wgu_sb = [mk(p4, f"wgu_sb{i}", [128, CCN * 1024], BF16, "right")
                          for i in range(2)]
                wd_sb, b_wdsb = mk(p4, "wd_sb", [128, (FD // 128) * C], BF16, "right")
                idx2 = [mk(p4, f"idx_sb{i}", [128, NSC], I32, "right")
                        for i in range(2)]
                wsl2 = [mk(p4, f"wsl_sb{i}", [128, NSC], F32, "right")
                        for i in range(2)]
                xg_tm = [mk(p4, f"xg_tm{i}", [128, C], BF16, "right") for i in range(2)]
                gg_sb = [mk(p4, f"gg_sb{i}", [128, 512], BF16, "right") for i in range(2)]
                dout_sb = [mk(p4, f"dout_sb{i}", [128, C], BF16, "right")
                           for i in range(2)]
                ptm = [(mkp(p4, f"ptm{i}", [128, 128], BF16),
                        Buf(f"ptm{i}")) for i in range(2)]
                psg = [(mkp(p4, f"psg{i}", [128, 512], F32),
                        Buf(f"psg{i}")) for i in range(2)]
                psu = [(mkp(p4, f"psu{i}", [128, 512], F32),
                        Buf(f"psu{i}")) for i in range(2)]
                psd = [(mkp(p4, f"psd{i}", [128, 512], F32),
                        Buf(f"psd{i}")) for i in range(2)]

                def fetch_expert(e_i):
                    ep = e_i % 2
                    idx_sb, b_idxsb = idx2[ep]
                    wsl_sb, b_wsl = wsl2[ep]
                    xg_fm, b_xgfm = xg2[ep]
                    for sc in range(NSC):
                        r0 = e_i * ES + sc * 128
                        O("sync", lambda e, a=idx_sb[:, sc:sc + 1],
                          s=idx_dram[r0:r0 + 128, :]: e.dma_start(out=a, in_=s),
                          reads=[b_idxdram], writes=[b_idxsb], dma=True,
                          no_waw=(sc > 0))
                        O("sync", lambda e, a=wsl_sb[:, sc:sc + 1],
                          s=w_dram[r0:r0 + 128, :]: e.dma_start(out=a, in_=s),
                          reads=[b_wdram], writes=[b_wsl], dma=True,
                          no_waw=(sc > 0))
                    for sc in range(NSC):
                        gt, b_gt = xg_tm[sc % 2]
                        O("gpsimd", lambda e, a=gt[:], src=xn_tm_d,
                          ofs=idx_sb[:, sc:sc + 1]:
                          e.indirect_dma_start(
                              out=a, out_offset=None, in_=src,
                              in_offset=IndirectOffsetOnAxis(ofs, 0)),
                          reads=[b_idxsb, b_xntm], writes=[b_gt], dma=True)
                        for cc in range(CCN):
                            pt, b_pt = ptm[cc % 2]
                            O("tensor", lambda e, o=pt[:],
                              i=gt[:, cc * 128:(cc + 1) * 128], ii=idb[:]:
                              e.transpose(o, i, ii),
                              reads=[b_gt, b_idb], writes=[b_pt])
                            dst = xg_fm[:, cc * CCAP + sc * 128:cc * CCAP + (sc + 1) * 128]
                            if cc % 2 == 0:
                                O("scalar", lambda e, o=dst, i=pt[:]:
                                  e.copy(out=o, in_=i),
                                  reads=[b_pt], writes=[b_xgfm], no_waw=True)
                            else:
                                O("vector", lambda e, o=dst, i=pt[:]:
                                  e.tensor_copy(out=o, in_=i),
                                  reads=[b_pt], writes=[b_xgfm], no_waw=True)

                fetch_expert(0)
                for e_i in range(NE):
                    idx_sb, b_idxsb = idx2[e_i % 2]
                    wsl_sb, b_wsl = wsl2[e_i % 2]
                    xg_fm, b_xgfm = xg2[e_i % 2]
                    for q4 in range(4):
                        wbuf, b_wbuf = wgu_sb[q4 % 2]
                        for cc in range(CCN):
                            O("gpsimd", lambda e, a=wbuf[:, cc * 1024:cc * 1024 + 512],
                              s=wgu_d[e_i, cc, :, q4 * 512:(q4 + 1) * 512]:
                              e.dma_start(out=a, in_=s),
                              reads=[b_wgu_bf], writes=[b_wbuf], dma=True,
                              no_waw=(cc > 0))
                            O("gpsimd", lambda e, a=wbuf[:, cc * 1024 + 512:(cc + 1) * 1024],
                              s=wgu_d[e_i, cc, :, FD + q4 * 512:FD + (q4 + 1) * 512]:
                              e.dma_start(out=a, in_=s),
                              reads=[b_wgu_bf], writes=[b_wbuf], dma=True, no_waw=True)
                        NTILES = [(i * 512, min(512, CCAP - i * 512))
                                  for i in range((CCAP + 511) // 512)]
                        for fi in range(4):
                            fc = q4 * 4 + fi
                            for nt, (n0, nw) in enumerate(NTILES):
                                pg, b_pg = psg[nt % 2]
                                pu, b_pu = psu[nt % 2]
                                mms = [lambda e, o=pg[:, 0:nw],
                                       l_=wbuf[:, cc * 1024 + fi * 128:
                                               cc * 1024 + (fi + 1) * 128],
                                       r_=xg_fm[:, cc * CCAP + n0:
                                                cc * CCAP + n0 + nw],
                                       f=(cc == 0), la=(cc == CCN - 1):
                                       e.matmul(o, l_, r_, start=f, stop=la)
                                       for cc in range(CCN)]
                                MG(b_pg, mms, [b_wbuf, b_xgfm])
                                mms = [lambda e, o=pu[:, 0:nw],
                                       l_=wbuf[:, cc * 1024 + 512 + fi * 128:
                                               cc * 1024 + 512 + (fi + 1) * 128],
                                       r_=xg_fm[:, cc * CCAP + n0:
                                                cc * CCAP + n0 + nw],
                                       f=(cc == 0), la=(cc == CCN - 1):
                                       e.matmul(o, l_, r_, start=f, stop=la)
                                       for cc in range(CCN)]
                                MG(b_pu, mms, [b_wbuf, b_xgfm])
                                gg, b_gg = gg_sb[nt % 2]
                                O("scalar", lambda e, o=gg[:, 0:nw], i=pg[:, 0:nw]:
                                  e.activation(out=o, in_=i, func=AF.Gelu),
                                  reads=[b_pg], writes=[b_gg])
                                O("vector", lambda e,
                                  o=h_fm[:, fc * CCAP + n0:fc * CCAP + n0 + nw],
                                  i=pu[:, 0:nw], i1=gg[:, 0:nw]:
                                  e.tensor_mul(out=o, in0=i, in1=i1),
                                  reads=[b_pu, b_gg], writes=[b_hfm], no_waw=True)
                    if e_i + 1 < NE:
                        fetch_expert(e_i + 1)
                    for fc in range(FD // 128):
                        O("gpsimd", lambda e, a=wd_sb[:, fc * C:(fc + 1) * C],
                          s=wd_d[e_i, fc]: e.dma_start(out=a, in_=s),
                          reads=[b_wd_bf], writes=[b_wdsb], dma=True,
                          no_waw=(fc > 0))
                    for sc in range(NSC):
                        do_t, b_do = dout_sb[sc % 2]
                        for ctile in range(2):
                            pd, b_pd = psd[ctile % 2]
                            mms = [lambda e, o=pd[:],
                                   l_=h_fm[:, fc_ * CCAP + sc * 128:
                                           fc_ * CCAP + (sc + 1) * 128],
                                   r_=wd_sb[:, fc_ * C + ctile * 512:
                                            fc_ * C + (ctile + 1) * 512],
                                   f=(fc_ == 0), la=(fc_ == FD // 128 - 1):
                                   e.matmul(o, l_, r_, start=f, stop=la)
                                   for fc_ in range(FD // 128)]
                            MG(b_pd, mms, [b_hfm, b_wdsb])
                            O("scalar", lambda e,
                              o=do_t[:, ctile * 512:(ctile + 1) * 512], i=pd[:],
                              s1=wsl_sb[:, sc:sc + 1]:
                              e.activation(out=o, in_=i, func=AF.Copy, scale=s1),
                              reads=[b_pd, b_wsl], writes=[b_do],
                              no_waw=(ctile > 0))
                        O("sync", lambda e,
                          a=dout_d[e_i * ES + sc * 128:e_i * ES + (sc + 1) * 128, :],
                          s=do_t[:]: e.dma_start(out=a, in_=s),
                          reads=[b_do], writes=[b_dout], dma=True, no_waw=True)
            kb.barrier()

        if with_ffn and debug:
            O("sync", lambda e, a=dbg["addrA"], s=addrA[:]: e.dma_start(out=a, in_=s),
              reads=[b_addrA], dma=True)
            O("sync", lambda e, a=dbg["addrB"], s=addrB[:]: e.dma_start(out=a, in_=s),
              reads=[b_addrB], dma=True)
            O("sync", lambda e, a=dbg["w1"], s=w1_t[:]: e.dma_start(out=a, in_=s),
              reads=[b_w1], dma=True)
            O("sync", lambda e, a=dbg["w2"], s=w2_t[:]: e.dma_start(out=a, in_=s),
              reads=[b_w2], dma=True)
            O("gpsimd", lambda e, a=dbg["idx"], s=idx_dram: e.dma_start(out=a, in_=s),
              reads=[b_idxdram], dma=True)
            O("gpsimd", lambda e, a=dbg["warr"], s=w_dram: e.dma_start(out=a, in_=s),
              reads=[b_wdram], dma=True)
            O("gpsimd", lambda e, a=dbg["xntm"], s=xn_tm_d: e.dma_start(out=a, in_=s),
              reads=[b_xntm], dma=True)
            O("gpsimd", lambda e, a=dbg["dout"], s=dout_d: e.dma_start(out=a, in_=s),
              reads=[b_dout, b_doutz], dma=True)

        with contextlib.ExitStack() as p5:
            NPAR5 = 6
            xt5 = [mk(p5, f"xt5{i}", [128, C], F32, "right") for i in range(NPAR5)]
            wt5 = [mk(p5, f"wt5{i}", [128, C], BF16, "right") for i in range(NPAR5)]
            ga5 = [mk(p5, f"ga5{i}", [128, C], BF16, "right") for i in range(NPAR5)]
            gb5 = [mk(p5, f"gb5{i}", [128, C], BF16, "right") for i in range(NPAR5)]
            o5 = [mk(p5, f"o5{i}", [128, C], F32, "right") for i in range(NPAR5)]
            t5 = [mk(p5, f"t5{i}", [128, C], F32, "right") for i in range(NPAR5)]

            for ct in range(NCT):
                par = ct % NPAR5
                xt, b_xt = xt5[par]
                wt, b_wt = wt5[par]
                ga, b_ga = ga5[par]
                gb, b_gb = gb5[par]
                ot, b_ot = o5[par]
                tt_, b_tt = t5[par]
                rows = slice(ct * 128, (ct + 1) * 128)
                O("sync", lambda e, a=xt[:], s=x_d[rows, :]: e.dma_start(out=a, in_=s),
                  writes=[b_xt], dma=True)
                O("sync", lambda e, a=wt[:], s=wo_tm_d[rows, :]:
                  e.dma_start(out=a, in_=s),
                  reads=[b_wotm], writes=[b_wt], dma=True)
                if with_ffn:
                    O("gpsimd", lambda e, a=ga[:], src=dout_d, ofs=addrA[:, ct:ct + 1]:
                      e.indirect_dma_start(out=a, out_offset=None, in_=src,
                                           in_offset=IndirectOffsetOnAxis(ofs, 0)),
                      reads=[b_addrA, b_dout, b_doutz], writes=[b_ga], dma=True)
                    O("gpsimd", lambda e, a=gb[:], src=dout_d, ofs=addrB[:, ct:ct + 1]:
                      e.indirect_dma_start(out=a, out_offset=None, in_=src,
                                           in_offset=IndirectOffsetOnAxis(ofs, 0)),
                      reads=[b_addrB, b_dout, b_doutz], writes=[b_gb], dma=True)
                    eng5 = "gpsimd" if ct % 3 == 2 else "vector"
                    O(eng5, lambda e, o=tt_[:], i=ga[:], i1=gb[:]:
                      e.tensor_add(out=o, in0=i, in1=i1),
                      reads=[b_ga, b_gb], writes=[b_tt])
                    O(eng5, lambda e, o=tt_[:], i=wt[:]:
                      e.tensor_add(out=o, in0=o, in1=i),
                      reads=[b_tt, b_wt], writes=[b_tt])
                else:
                    eng5 = "vector"
                    O("vector", lambda e, o=tt_[:], i=wt[:]: e.tensor_copy(out=o, in_=i),
                      reads=[b_wt], writes=[b_tt])
                O(eng5, lambda e, o=ot[:], i=tt_[:], i1=xt[:]:
                  e.tensor_add(out=o, in0=i, in1=i1),
                  reads=[b_tt, b_xt], writes=[b_ot])
                O("sync", lambda e, a=out_d[rows, :], s=ot[:]: e.dma_start(out=a, in_=s),
                  reads=[b_ot], dma=True)

        kb.finalize()
    return nc



def _perm_cols(n_out):
    p = np.empty(n_out, dtype=np.int64)
    for n in range(n_out):
        h, r = divmod(n, 256)
        half, m = divmod(r, 128)
        p[n] = 256 * h + 2 * m + half
    return p


def _host_consts():
    c = {}
    i = np.arange(128)
    c["ident_f"] = np.eye(128, dtype=np.float32)
    c["ident_bf"] = np.eye(128, dtype=bf16)
    c["csum_ut"] = (i[:, None] <= i[None, :]).astype(bf16)
    e_of = i % NE
    ct_of = i // NE
    c["m0"] = ((e_of[:, None] == e_of[None, :]) &
               (ct_of[:, None] < ct_of[None, :])).astype(bf16)
    c["m01"] = (e_of[:, None] == e_of[None, :]).astype(bf16)
    c["mdiag"] = np.where(i[None, :] <= i[:, None], 0.0, -1.0e30).astype(np.float32)
    c["mleft"] = np.where(i[None, :] >= i[:, None], 0.0, -1.0e30).astype(np.float32)
    c["mleft5"] = np.concatenate([c["mleft"], np.zeros((128, 384), np.float32)],
                                 axis=1)
    c["m_idx"] = np.arange(128, dtype=np.float32).reshape(128, 1)
    c["t_row"] = np.arange(SEQ, dtype=np.float32).reshape(1, SEQ)
    c["ones_r"] = np.ones((1, 128), dtype=np.float32)
    c["ones_c"] = np.ones((128, 1), dtype=bf16)
    c["ones_cf"] = np.ones((128, 1), dtype=np.float32)
    c["tokid"] = (np.arange(NCT)[None, :] * 128 +
                  np.arange(128)[:, None]).astype(np.int32)
    c["epsc"] = np.tile(np.array([[1e-5, 1e-6]], dtype=np.float32), (128, 1))
    return c


def prep_inputs(inputs):
    x = np.asarray(inputs["L0_attn_x"])
    wq = np.asarray(inputs["L0_attn_Wq"])
    wk = np.asarray(inputs["L0_attn_Wk"])
    wv = np.asarray(inputs["L0_attn_Wv"])
    wo = np.asarray(inputs["L0_attn_Wo"])
    qg = np.asarray(inputs["L0_attn_q_gamma"])
    kg = np.asarray(inputs["L0_attn_k_gamma"])
    wg = np.asarray(inputs["L0_ffn_W_gate"])
    wgu = np.asarray(inputs["L0_ffn_W_gate_up"])
    wd = np.asarray(inputs["L0_ffn_W_down"])

    pq = _perm_cols(C)
    pk = _perm_cols(HD)
    shared = {
        "wqT": np.ascontiguousarray(wq.T[:, pq], dtype=np.float32),
        "wkT": np.ascontiguousarray(wk.T[:, pk], dtype=np.float32),
        "wvT": np.ascontiguousarray(wv.T, dtype=np.float32),
        "woT": np.ascontiguousarray(wo.T, dtype=np.float32),
        "qg": np.ascontiguousarray((qg[pk] / 16.0).reshape(1, HD)).astype(bf16),
        "kg": np.ascontiguousarray(kg[pk].reshape(1, HD)).astype(bf16),
        "wgT": np.ascontiguousarray(wg.T, dtype=np.float32),
        "wgu": np.ascontiguousarray(
            wgu.transpose(0, 2, 1).reshape(NE, CCN, 128, 2 * FD), dtype=np.float32),
        "wd": np.ascontiguousarray(
            wd.transpose(0, 2, 1).reshape(NE, FD // 128, 128, C), dtype=np.float32),
    }
    shared.update(_host_consts())
    in_maps = []
    for core in range(NCORES):
        m = dict(shared)
        m["x2d"] = np.ascontiguousarray(
            x[core * NSEQ:(core + 1) * NSEQ].reshape(TOK, C), dtype=np.float32)
        in_maps.append(m)
    return in_maps


def kernel(**inputs):
    in_maps = prep_inputs(inputs)
    nc = build_program()
    res = run_bass_kernel_spmd(nc, in_maps, core_ids=list(range(NCORES)))
    outs = [np.asarray(r["out"], dtype=np.float32) for r in res.results]
    return np.concatenate(outs, axis=0).reshape(16, SEQ, C)


# version 13
# speedup vs baseline: 1.0821x; 1.0068x over previous
import contextlib
import math
import numpy as np
import ml_dtypes
import concourse.bass as bass
import concourse.mybir as mybir
from concourse.bass import IndirectOffsetOnAxis
from concourse.bass_utils import run_bass_kernel_spmd

F32 = mybir.dt.float32
BF16 = mybir.dt.bfloat16
I32 = mybir.dt.int32
AF = mybir.ActivationFunctionType
ALU = mybir.AluOpType
AX = mybir.AxisListType
bf16 = ml_dtypes.bfloat16

NCORES = 8
SEQ = 2048
NSEQ = 2
TOK = SEQ * NSEQ
C = 1024
CCN = 8
HD = 256
NH = 4
NE = 8
FD = 2048
CCAP = 1280
NSC = CCAP // 128
ES = CCAP + 128
NCT = TOK // 128
TWO_PI_HI = float(np.float32(2.0 * math.pi))
TWO_PI_LO = float(np.float32(2.0 * math.pi - np.float64(np.float32(2.0 * math.pi))))

ENGINES = ("tensor", "vector", "scalar", "gpsimd", "sync")


class Sem:
    __slots__ = ("h", "n")

    def __init__(self, h):
        self.h = h
        self.n = 0

    def inc(self, k=1):
        self.n += k
        return self.n


class Buf:
    __slots__ = ("w", "r", "name")

    def __init__(self, name=""):
        self.w = []
        self.r = []
        self.name = name


def _run_stream(stream):
    def run(eng):
        for it in stream:
            k = it[0]
            if k == "w":
                eng.wait_ge(it[1], it[2])
            elif k == "b":
                eng.sem_inc(it[1], 1)
                eng.wait_ge(it[1], it[2])
            else:
                _, fn, semh, delta = it
                inst = fn(eng)
                if semh is not None:
                    inst.then_inc(semh, delta)
    return run


class Kb:
    def __init__(self, nc, ctx):
        self.nc = nc
        self.ctx = ctx
        self.streams = {e: [] for e in ENGINES}
        self.seen = {e: {} for e in ENGINES}
        self.eng_sem = {e: Sem(ctx.enter_context(nc.semaphore(f"sem_{e}")))
                        for e in ("tensor", "vector", "scalar", "gpsimd")}
        self.lanes = {
            "sync": [Sem(ctx.enter_context(nc.semaphore(f"qs{i}"))) for i in range(8)],
            "gpsimd": [Sem(ctx.enter_context(nc.semaphore(f"qg{i}"))) for i in range(8)],
        }
        self.pf_lanes = {
            "gpsimd": [Sem(ctx.enter_context(nc.semaphore(f"qgp{i}"))) for i in range(4)],
        }
        self.lane_rr = {k: 0 for k in self.lanes}
        self.pf_rr = {k: 0 for k in self.pf_lanes}
        self.bar_sem = Sem(ctx.enter_context(nc.semaphore("bar")))

    def _wait(self, eng, ev):
        sem, val = ev
        key = id(sem)
        if self.seen[eng].get(key, -1) >= val:
            return
        self.seen[eng][key] = val
        self.streams[eng].append(("w", sem.h, val))

    def op(self, eng, fn, reads=(), writes=(), dma=False, prefetch=False,
           no_waw=False):
        for b in reads:
            for ev in b.w:
                self._wait(eng, ev)
        for b in writes:
            if not no_waw:
                for ev in b.w:
                    self._wait(eng, ev)
            for ev in b.r:
                self._wait(eng, ev)
        if dma:
            if prefetch:
                ls = self.pf_lanes[eng]
                s = ls[self.pf_rr[eng] % len(ls)]
                self.pf_rr[eng] += 1
            else:
                ls = self.lanes[eng]
                s = ls[self.lane_rr[eng] % len(ls)]
                self.lane_rr[eng] += 1
            if s.n:
                self._wait(eng, (s, s.n))
            delta = 16
        else:
            s = self.eng_sem[eng]
            delta = 1
        v = s.inc(delta)
        ev = (s, v)
        self.streams[eng].append(("i", fn, s.h, delta))
        for b in writes:
            if no_waw:
                b.w.append(ev)
            else:
                b.w = [ev]
                b.r = []
        for b in reads:
            b.r.append(ev)
        return ev

    def mm_group(self, psum_buf, mms, read_bufs, no_waw=False):
        eng = "tensor"
        for b in read_bufs:
            for ev in b.w:
                self._wait(eng, ev)
        if not no_waw:
            for ev in psum_buf.w:
                self._wait(eng, ev)
        for ev in psum_buf.r:
            self._wait(eng, ev)
        for fn in mms[:-1]:
            self.streams[eng].append(("i", fn, None, None))
        s = self.eng_sem[eng]
        v = s.inc(1)
        self.streams[eng].append(("i", mms[-1], s.h, 1))
        ev = (s, v)
        if no_waw:
            psum_buf.w.append(ev)
        else:
            psum_buf.w = [ev]
            psum_buf.r = []
        for b in read_bufs:
            b.r.append(ev)
        return ev

    def barrier(self, engines=("tensor", "vector", "scalar", "gpsimd", "sync"),
                lane_queues=("sync", "gpsimd")):
        full = len(engines) == 5
        target = self.bar_sem.n + len(engines)
        self.bar_sem.n = target
        for e in engines:
            if e in self.eng_sem:
                self._wait(e, (self.eng_sem[e], self.eng_sem[e].n))
            if full and e == "sync":
                for q in lane_queues:
                    for s in self.lanes[q]:
                        if s.n:
                            self._wait(e, (s, s.n))
            self.streams[e].append(("b", self.bar_sem.h, target))
            self.seen[e][id(self.bar_sem)] = target

    def finalize(self):
        for lanes in self.lanes.values():
            for s in lanes:
                if s.n:
                    self._wait("sync", (s, s.n))
        for lanes in self.pf_lanes.values():
            for s in lanes:
                if s.n:
                    self._wait("sync", (s, s.n))
        blk = self.ctx.enter_context(self.nc.Block())
        for name in ENGINES:
            getattr(blk, name)(_run_stream(self.streams[name]))



def build_program(with_ffn=True, with_attn=True, attn_level=5, qb_list=None, debug=False):
    nc = bass.Bass("TRN2", target_bir_lowering=False, debug=False)
    dt = nc.dram_tensor
    x_d = dt("x2d", [TOK, C], F32, kind="ExternalInput").ap()
    wq_d = dt("wqT", [C, C], F32, kind="ExternalInput").ap()
    wk_d = dt("wkT", [C, HD], F32, kind="ExternalInput").ap()
    wv_d = dt("wvT", [C, HD], F32, kind="ExternalInput").ap()
    wo_d = dt("woT", [C, C], F32, kind="ExternalInput").ap()
    qg_d = dt("qg", [1, HD], BF16, kind="ExternalInput").ap()
    kg_d = dt("kg", [1, HD], BF16, kind="ExternalInput").ap()
    wg_d = dt("wgT", [C, NE], F32, kind="ExternalInput").ap()
    wgu_d = dt("wgu", [NE, CCN, 128, 2 * FD], F32, kind="ExternalInput").ap()
    wd_d = dt("wd", [NE, FD // 128, 128, C], F32, kind="ExternalInput").ap()
    idf_d = dt("ident_f", [128, 128], F32, kind="ExternalInput").ap()
    idb_d = dt("ident_bf", [128, 128], BF16, kind="ExternalInput").ap()
    csum_d = dt("csum_ut", [128, 128], BF16, kind="ExternalInput").ap()
    m0_d = dt("m0", [128, 128], BF16, kind="ExternalInput").ap()
    m01_d = dt("m01", [128, 128], BF16, kind="ExternalInput").ap()
    mdiag_d = dt("mdiag", [128, 128], F32, kind="ExternalInput").ap()
    mleft_d = dt("mleft", [128, 128], F32, kind="ExternalInput").ap()
    mleft5_d = dt("mleft5", [128, 512], F32, kind="ExternalInput").ap()
    midx_d = dt("m_idx", [128, 1], F32, kind="ExternalInput").ap()
    trow_d = dt("t_row", [1, SEQ], F32, kind="ExternalInput").ap()
    onesr_d = dt("ones_r", [1, 128], F32, kind="ExternalInput").ap()
    onesc_d = dt("ones_c", [128, 1], BF16, kind="ExternalInput").ap()
    onescf_d = dt("ones_cf", [128, 1], F32, kind="ExternalInput").ap()
    tokid_d = dt("tokid", [128, NCT], I32, kind="ExternalInput").ap()
    epsc_d = dt("epsc", [128, 2], F32, kind="ExternalInput").ap()
    out_d = dt("out", [TOK, C], F32, kind="ExternalOutput").ap()
    dbg = {}
    if debug:
        dbg["addrA"] = dt("dbg_addrA", [128, NCT], I32, kind="ExternalOutput").ap()
        dbg["addrB"] = dt("dbg_addrB", [128, NCT], I32, kind="ExternalOutput").ap()
        dbg["w1"] = dt("dbg_w1", [128, NCT], F32, kind="ExternalOutput").ap()
        dbg["w2"] = dt("dbg_w2", [128, NCT], F32, kind="ExternalOutput").ap()
        dbg["idx"] = dt("dbg_idx", [NE * ES, 1], I32, kind="ExternalOutput").ap()
        dbg["warr"] = dt("dbg_warr", [NE * ES, 1], F32, kind="ExternalOutput").ap()
        dbg["lgtm"] = dt("dbg_lgtm", [128, NCT * NE], F32, kind="ExternalOutput").ap()
        dbg["pos"] = dt("dbg_pos", [128, NCT * NE], F32, kind="ExternalOutput").ap()
        dbg["xntm"] = dt("dbg_xntm", [TOK + 128, C], BF16, kind="ExternalOutput").ap()
        dbg["dout"] = dt("dbg_dout", [NE * ES, C], BF16, kind="ExternalOutput").ap()

    xn_tm_d = dt("xn_tm", [TOK + 128, C], BF16, kind="Internal").ap()
    wo_tm_d = dt("wo_tm", [TOK, C], BF16, kind="Internal").ap()
    dout_d = dt("dout", [NE * ES, C], BF16, kind="Internal").ap()
    idx_dram = dt("idx_arr", [NE * ES, 1], I32, kind="Internal").ap()
    w_dram = dt("w_arr", [NE * ES, 1], F32, kind="Internal").ap()

    with contextlib.ExitStack() as ctx:
        E = ctx.enter_context
        kb = Kb(nc, ctx)
        O = kb.op
        MG = kb.mm_group

        _uid = [0]

        def mk(stack, name, shape, dtype, side="left"):
            _uid[0] += 1
            t = stack.enter_context(
                nc.sbuf_tensor(f"{name}_{_uid[0]}", shape, dtype, side=side))
            return t, Buf(name)

        def mkp(stack, name, shape, dtype):
            _uid[0] += 1
            return stack.enter_context(
                nc.psum_tensor(f"{name}_{_uid[0]}", shape, dtype))

        idf, b_idf = mk(ctx, "idf", [128, 128], F32)
        idb, b_idb = mk(ctx, "idb", [128, 128], BF16)
        csum_t, b_csum = mk(ctx, "csum_t", [128, 128], BF16)
        m0_t, b_m0 = mk(ctx, "m0_t", [128, 128], BF16)
        m01_t, b_m01 = mk(ctx, "m01_t", [128, 128], BF16)
        mdiag, b_mdiag = mk(ctx, "mdiag_s", [128, 128], F32)
        mleft, b_mleft = mk(ctx, "mleft_s", [128, 128], F32)
        mleft5, b_mleft5 = mk(ctx, "mleft5_s", [128, 512], F32)
        midx, b_midx = mk(ctx, "midx", [128, 1], F32)
        onesr, b_onesr = mk(ctx, "onesr", [1, 128], F32)
        onesc, b_onesc = mk(ctx, "onesc", [128, 1], BF16)
        onescf, b_onescf = mk(ctx, "onescf", [128, 1], F32)
        tokid, b_tokid = mk(ctx, "tokid_s", [128, NCT], I32)
        qg_t, b_qg = mk(ctx, "qg_t", [1, HD], BF16)
        kg_t, b_kg = mk(ctx, "kg_t", [1, HD], BF16)
        cos_t, b_cos = mk(ctx, "cos_t", [128, SEQ], BF16)
        sin_t, b_sin = mk(ctx, "sin_t", [128, SEQ], BF16)
        w1_t, b_w1 = mk(ctx, "w1_t", [128, NCT], F32)
        w2_t, b_w2 = mk(ctx, "w2_t", [128, NCT], F32)
        addrA, b_addrA = mk(ctx, "addrA", [128, NCT], I32)
        addrB, b_addrB = mk(ctx, "addrB", [128, NCT], I32)
        epsc, b_epsc = mk(ctx, "epsc_s", [128, 2], F32)
        zrow, b_zrow = mk(ctx, "zrow", [128, C], BF16)

        b_xntm = Buf("xn_tm")
        b_wotm = Buf("wo_tm")
        b_dout = Buf("dout")
        b_doutz = Buf("dout_zero")
        b_idxdram = Buf("idx_dram")
        b_wdram = Buf("w_dram")
        b_wgu_bf = Buf("wgu_bf_dram")
        b_wd_bf = Buf("wd_bf_dram")

        aw_stack = ctx.enter_context(contextlib.ExitStack())
        xnfm, b_xnfm = mk(aw_stack, "xnfm", [128, CCN * TOK], BF16, side="right")
        wq_bf, b_wqbf = mk(aw_stack, "wq_bf", [128, CCN * C], BF16, side="right")
        wk_bf, b_wkbf = mk(aw_stack, "wk_bf", [128, CCN * HD], BF16, side="right")
        wv_bf, b_wvbf = mk(aw_stack, "wv_bf", [128, CCN * HD], BF16, side="right")
        wo_bfw, b_wobf = mk(aw_stack, "wo_bfw", [128, CCN * C], BF16, side="right")
        wg_bf, b_wgbf = mk(aw_stack, "wg_bf", [128, CCN * NE], BF16, side="right")

        for dst, src, b in ((idf, idf_d, b_idf), (idb, idb_d, b_idb),
                            (csum_t, csum_d, b_csum), (m0_t, m0_d, b_m0),
                            (m01_t, m01_d, b_m01), (mdiag, mdiag_d, b_mdiag),
                            (mleft, mleft_d, b_mleft),
                            (mleft5, mleft5_d, b_mleft5), (midx, midx_d, b_midx),
                            (onesr, onesr_d, b_onesr),
                            (onesc, onesc_d, b_onesc), (onescf, onescf_d, b_onescf),
                            (tokid, tokid_d, b_tokid), (epsc, epsc_d, b_epsc),
                            (qg_t, qg_d, b_qg),
                            (kg_t, kg_d, b_kg)):
            O("sync", lambda e, a=dst[:], s=src: e.dma_start(out=a, in_=s),
              writes=[b], dma=True)

        for cc in range(CCN):
            for wt, wsrc, b, wd_ in ((wq_bf, wq_d, b_wqbf, C), (wk_bf, wk_d, b_wkbf, HD),
                                     (wv_bf, wv_d, b_wvbf, HD), (wo_bfw, wo_d, b_wobf, C),
                                     (wg_bf, wg_d, b_wgbf, NE)):
                O("gpsimd", lambda e, a=wt[:, cc * wd_:(cc + 1) * wd_],
                  s=wsrc[cc * 128:(cc + 1) * 128, :]: e.dma_start(out=a, in_=s),
                  writes=[b], dma=True, no_waw=True)

        O("vector", lambda e, a=zrow[:]: e.memset(a, 0.0), writes=[b_zrow])

        with contextlib.ExitStack() as p0:
            trow, b_trow = mk(p0, "trow", [1, SEQ], F32, "right")
            O("sync", lambda e, a=trow[:], s=trow_d: e.dma_start(out=a, in_=s),
              writes=[b_trow], dma=True)
            base_t, b_base = mk(p0, "base_t", [128, 1], F32, "right")
            lnb_t, b_lnb = mk(p0, "lnb_t", [128, 1], F32, "right")
            expo_t, b_expo = mk(p0, "expo_t", [128, 1], F32, "right")
            invf_t, b_invf = mk(p0, "invf_t", [128, 1], F32, "right")
            invfr, b_invfr = mk(p0, "invfr", [1, 128], F32, "right")
            th_t, b_th = mk(p0, "th_t", [128, 512], F32, "right")
            z_t, b_z = mk(p0, "z_t", [128, 512], F32, "right")
            ki_t, b_ki = mk(p0, "ki_t", [128, 512], I32, "right")
            kf_t, b_kf = mk(p0, "kf_t", [128, 512], F32, "right")
            ph_t, b_ph = mk(p0, "ph_t", [128, 512], F32, "right")
            gm_t, b_gm = mk(p0, "gm_t", [128, 512], F32, "right")
            sh_t, b_sh = mk(p0, "sh_t", [128, 512], F32, "right")
            sq_t, b_sq = mk(p0, "sq_t", [128, 512], F32, "right")
            inv_t, b_inv = mk(p0, "inv_t", [128, NE * ES // 128], I32, "right")
            wz_t, b_wz = mk(p0, "wz_t", [128, NE * ES // 128], F32, "right")
            ps_f = mkp(p0, "ps_f", [128, 512], F32)
            b_psf = Buf("ps_f")
            ps_tr = mkp(p0, "ps_tr", [1, 128], F32)
            b_pstr = Buf("ps_tr")

            if with_ffn:
                O("vector", lambda e, a=inv_t[:]: e.memset(a, TOK), writes=[b_inv])
                O("vector", lambda e, a=wz_t[:]: e.memset(a, 0.0), writes=[b_wz])
                O("gpsimd", lambda e, a=idx_dram.rearrange("(b a) c -> b (a c)", b=128),
                  s=inv_t[:]: e.dma_start(out=a, in_=s),
                  reads=[b_inv], writes=[b_idxdram], dma=True)
                O("gpsimd", lambda e, a=w_dram.rearrange("(b a) c -> b (a c)", b=128),
                  s=wz_t[:]: e.dma_start(out=a, in_=s),
                  reads=[b_wz], writes=[b_wdram], dma=True)
                for e_i in range(NE):
                    O("gpsimd", lambda e,
                      a=dout_d[e_i * ES + CCAP:(e_i + 1) * ES, :], s=zrow[:]:
                      e.dma_start(out=a, in_=s),
                      reads=[b_zrow], writes=[b_doutz], dma=True, no_waw=True)
                O("gpsimd", lambda e, a=xn_tm_d[TOK:TOK + 128, :], s=zrow[:]:
                  e.dma_start(out=a, in_=s),
                  reads=[b_zrow], writes=[b_xntm], dma=True, no_waw=True)

            O("vector", lambda e, a=base_t[:]: e.memset(a, 1.0e6), writes=[b_base])
            O("scalar", lambda e, o=lnb_t[:], i=base_t[:]:
              e.activation(out=o, in_=i, func=AF.Ln), reads=[b_base], writes=[b_lnb])
            O("vector", lambda e, o=expo_t[:], i=midx[:], s1=lnb_t[:]:
              e.tensor_scalar(out=o, in0=i, scalar1=s1, scalar2=-1.0 / 128.0,
                              op0=ALU.mult, op1=ALU.mult),
              reads=[b_midx, b_lnb], writes=[b_expo])
            O("scalar", lambda e, o=invf_t[:], i=expo_t[:]:
              e.activation(out=o, in_=i, func=AF.Exp), reads=[b_expo], writes=[b_invf])
            O("tensor", lambda e, o=ps_tr[:], i=invf_t[:], ii=idf[:]:
              e.transpose(o, i, ii), reads=[b_invf, b_idf], writes=[b_pstr])
            O("scalar", lambda e, o=invfr[:], i=ps_tr[:]: e.copy(out=o, in_=i),
              reads=[b_pstr], writes=[b_invfr])
            for j4 in range(SEQ // 512):
                sl = slice(j4 * 512, (j4 + 1) * 512)
                MG(b_psf, [lambda e, o=ps_f[:], l_=invfr[:], r_=trow[:, sl]:
                           e.matmul(o, l_, r_, start=True, stop=True)],
                   [b_invfr, b_trow])
                O("scalar", lambda e, o=th_t[:], i=ps_f[:]: e.copy(out=o, in_=i),
                  reads=[b_psf], writes=[b_th])
                O("vector", lambda e, o=z_t[:], i=th_t[:]:
                  e.tensor_single_scalar(out=o, in_=i, scalar=1.0 / TWO_PI_HI,
                                         op=ALU.mult), reads=[b_th], writes=[b_z])
                O("vector", lambda e, o=ki_t[:], i=z_t[:]: e.tensor_copy(out=o, in_=i),
                  reads=[b_z], writes=[b_ki])
                O("vector", lambda e, o=kf_t[:], i=ki_t[:]: e.tensor_copy(out=o, in_=i),
                  reads=[b_ki], writes=[b_kf])
                for sc_, b_in in ((-TWO_PI_HI, b_th), (-TWO_PI_LO, None)):
                    O("vector", lambda e, o=ph_t[:], i=kf_t[:], s=sc_,
                      i1=(th_t[:] if sc_ == -TWO_PI_HI else ph_t[:]):
                      e.scalar_tensor_tensor(out=o, in0=i, scalar=s, in1=i1,
                                             op0=ALU.mult, op1=ALU.add),
                      reads=[b_kf, b_th, b_ph], writes=[b_ph])
                for sgn in (1.0, -1.0):
                    O("vector", lambda e, o=gm_t[:], i=ph_t[:], s=sgn:
                      e.tensor_single_scalar(out=o, in_=i, scalar=s * math.pi,
                                             op=(ALU.is_gt if s > 0 else ALU.is_lt)),
                      reads=[b_ph], writes=[b_gm])
                    for cpart in (TWO_PI_HI, TWO_PI_LO):
                        O("vector", lambda e, o=ph_t[:], i=gm_t[:],
                          s=(-cpart if sgn > 0 else cpart), i1=ph_t[:]:
                          e.scalar_tensor_tensor(out=o, in0=i, scalar=s, in1=i1,
                                                 op0=ALU.mult, op1=ALU.add),
                          reads=[b_gm, b_ph], writes=[b_ph])
                O("scalar", lambda e, o=sin_t[:, sl], i=ph_t[:]:
                  e.activation(out=o, in_=i, func=AF.Sin),
                  reads=[b_ph], writes=[b_sin], no_waw=True)
                O("scalar", lambda e, o=sh_t[:], i=ph_t[:]:
                  e.activation(out=o, in_=i, func=AF.Sin, scale=0.5),
                  reads=[b_ph], writes=[b_sh])
                O("vector", lambda e, o=sq_t[:], i=sh_t[:]:
                  e.tensor_mul(out=o, in0=i, in1=i), reads=[b_sh], writes=[b_sq])
                O("vector", lambda e, o=cos_t[:, sl], i=sq_t[:]:
                  e.tensor_scalar(out=o, in0=i, scalar1=-2.0, scalar2=1.0,
                                  op0=ALU.mult, op1=ALU.add),
                  reads=[b_sq], writes=[b_cos], no_waw=True)
            kb.barrier()

        with contextlib.ExitStack() as p1:
            LNP = 4
            xts = [mk(p1, f"xt{i}", [128, C], F32, "right") for i in range(LNP)]
            sqs = [mk(p1, f"sqs{i}", [128, C], F32, "right") for i in range(LNP)]
            xnb = [mk(p1, f"xnb{i}", [128, C], BF16, "right") for i in range(LNP)]
            sts = [mk(p1, f"lnst{i}", [128, 8], F32, "right") for i in range(LNP)]
            ptr = [(mkp(p1, f"ptr{i}", [128, 128], BF16),
                    Buf(f"ptr{i}")) for i in range(2)]

            for ct in range(NCT):
                par = ct % LNP
                xt, b_xt = xts[par]
                sqt, b_sqt = sqs[par]
                xnt, b_xnt = xnb[par]
                stt, b_st = sts[par]
                c_sm, c_s2, c_m, c_m2, c_var, c_rs, c_nm = (
                    stt[:, i:i + 1] for i in range(7))
                O("sync", lambda e, a=xt[:], s=x_d[ct * 128:(ct + 1) * 128, :]:
                  e.dma_start(out=a, in_=s), writes=[b_xt], dma=True)
                O("vector", lambda e, o=c_sm, i=xt[:]:
                  e.reduce_sum(out=o, in_=i, axis=AX.X), reads=[b_xt], writes=[b_st])
                O("scalar", lambda e, o=sqt[:], i=xt[:], a=c_s2:
                  e.activation(out=o, in_=i, func=AF.Square, accum_out=a),
                  reads=[b_xt], writes=[b_sqt, b_st], no_waw=True)
                O("vector", lambda e, o=c_m, i=c_sm:
                  e.tensor_single_scalar(out=o, in_=i, scalar=1.0 / C, op=ALU.mult),
                  reads=[b_st], writes=[b_st])
                O("vector", lambda e, o=c_m2, i=c_m:
                  e.tensor_mul(out=o, in0=i, in1=i), reads=[b_st], writes=[b_st])
                O("vector", lambda e, o=c_var, i=c_s2:
                  e.tensor_single_scalar(out=o, in_=i, scalar=1.0 / C, op=ALU.mult),
                  reads=[b_st], writes=[b_st])
                O("vector", lambda e, o=c_var, i=c_m2:
                  e.tensor_sub(out=o, in0=o, in1=i), reads=[b_st], writes=[b_st])
                O("scalar", lambda e, o=c_rs, i=c_var, b_=epsc[:, 0:1]:
                  e.activation(out=o, in_=i, func=AF.Ln, bias=b_),
                  reads=[b_st, b_epsc], writes=[b_st])
                O("scalar", lambda e, o=c_rs:
                  e.activation(out=o, in_=o, func=AF.Exp, scale=-0.5),
                  reads=[b_st], writes=[b_st])
                O("vector", lambda e, o=c_nm, i=c_m, s1=c_rs:
                  e.tensor_scalar(out=o, in0=i, scalar1=s1, scalar2=-1.0,
                                  op0=ALU.mult, op1=ALU.mult),
                  reads=[b_st], writes=[b_st])
                O("vector", lambda e, o=xnt[:], i=xt[:], s1=c_rs, s2=c_nm:
                  e.tensor_scalar(out=o, in0=i, scalar1=s1, scalar2=s2,
                                  op0=ALU.mult, op1=ALU.add),
                  reads=[b_xt, b_st], writes=[b_xnt])
                if with_ffn:
                    O("sync", lambda e, a=xn_tm_d[ct * 128:(ct + 1) * 128, :],
                      s=xnt[:]: e.dma_start(out=a, in_=s),
                      reads=[b_xnt], writes=[b_xntm], dma=True, no_waw=True)
                for cc in range(CCN):
                    pt, b_pt = ptr[cc % 2]
                    O("tensor", lambda e, o=pt[:], i=xnt[:, cc * 128:(cc + 1) * 128],
                      ii=idb[:]: e.transpose(o, i, ii),
                      reads=[b_xnt, b_idb], writes=[b_pt])
                    dst = xnfm[:, cc * TOK + ct * 128:cc * TOK + (ct + 1) * 128]
                    if cc % 2 == 0:
                        O("scalar", lambda e, o=dst, i=pt[:]: e.copy(out=o, in_=i),
                          reads=[b_pt], writes=[b_xnfm], no_waw=True)
                    else:
                        O("vector", lambda e, o=dst, i=pt[:]: e.tensor_copy(out=o, in_=i),
                          reads=[b_pt], writes=[b_xnfm], no_waw=True)
            kb.barrier()

        if with_ffn:
            with contextlib.ExitStack() as p2:
                lgT, b_lgT = mk(p2, "lgT", [NE, TOK], F32, "right")
                lg_tm, b_lgtm = mk(p2, "lg_tm", [128, NCT * NE], F32, "right")
                eq1, b_eq1 = mk(p2, "eq1", [128, NCT * NE], F32, "right")
                eq2, b_eq2 = mk(p2, "eq2", [128, NCT * NE], F32, "right")
                l2t, b_l2 = mk(p2, "l2t", [128, NCT * NE], F32, "right")
                me_bf, b_mebf = mk(p2, "me_bf", [128, NCT * NE], BF16, "right")
                ic_sb, b_ic = mk(p2, "ic_sb", [128, NCT * NE], F32, "right")
                pos_t, b_pos = mk(p2, "pos_t", [128, NCT * NE], F32, "right")
                m1t, b_m1 = mk(p2, "m1t", [128, NCT], F32, "right")
                m2t, b_m2 = mk(p2, "m2t", [128, NCT], F32, "right")
                dtt, b_dtt = mk(p2, "dtt", [128, NCT], F32, "right")
                edt, b_ed = mk(p2, "edt", [128, NCT], F32, "right")
                aA, b_aA = mk(p2, "aA", [128, NCT], F32, "right")
                aB, b_aB = mk(p2, "aB", [128, NCT], F32, "right")
                tmp1, b_t1 = mk(p2, "tmp1", [128, NCT], F32, "right")
                tot_sb, b_tot = mk(p2, "tot_sb", [1, NCT * NE], F32, "right")
                totcb, b_totcb = mk(p2, "totcb", [128, 2], BF16, "right")
                off_sb, b_off = mk(p2, "off_sb", [128, 2], F32, "right")
                offr, b_offr = mk(p2, "offr", [1, NCT * NE], F32, "right")
                offm, b_offm = mk(p2, "offm", [128, NCT * NE], F32, "right")
                ps_lg = mkp(p2, "ps_lg", [NE, 512], F32)
                b_pslg = Buf("pslg")
                ps_ic = mkp(p2, "ps_ic", [128, NCT * NE], F32)
                b_psic = Buf("psic")
                ps_sm = mkp(p2, "ps_sm", [128, 256], F32)
                b_pssm = Buf("pssm")
                ps_ofb = mkp(p2, "ps_ofb", [128, NCT * NE], F32)
                b_psofb = Buf("psofb")

                for tt in range(TOK // 512):
                    mms = [lambda e, o=ps_lg[:], l_=wg_bf[:, cc * NE:(cc + 1) * NE],
                           r_=xnfm[:, cc * TOK + tt * 512:cc * TOK + (tt + 1) * 512],
                           f=(cc == 0), la=(cc == CCN - 1):
                           e.matmul(o, l_, r_, start=f, stop=la)
                           for cc in range(CCN)]
                    MG(b_pslg, mms, [b_wgbf, b_xnfm])
                    O("scalar", lambda e, o=lgT[:, tt * 512:(tt + 1) * 512], i=ps_lg[:]:
                      e.copy(out=o, in_=i), reads=[b_pslg], writes=[b_lgT], no_waw=True)
                for ct in range(NCT):
                    O("tensor", lambda e, o=ps_sm[0:128, 0:NE],
                      i=lgT[:, ct * 128:(ct + 1) * 128], ii=idf[0:NE, 0:NE]:
                      e.transpose(o, i, ii), reads=[b_lgT, b_idf], writes=[b_pssm])
                    O("vector", lambda e, o=lg_tm[:, ct * NE:(ct + 1) * NE],
                      i=ps_sm[0:128, 0:NE]: e.tensor_copy(out=o, in_=i),
                      reads=[b_pssm], writes=[b_lgtm], no_waw=True)

                def esl(t, e_):
                    return t[:, e_::NE]

                O("vector", lambda e, o=m1t[:], i=esl(lg_tm, 0):
                  e.tensor_copy(out=o, in_=i), reads=[b_lgtm], writes=[b_m1])
                for e_ in range(1, NE):
                    O("vector", lambda e, o=m1t[:], i=esl(lg_tm, e_):
                      e.tensor_max(out=o, in0=o, in1=i),
                      reads=[b_lgtm, b_m1], writes=[b_m1])
                for e_ in range(NE):
                    O("vector", lambda e, o=esl(eq1, e_), i=esl(lg_tm, e_), m=m1t[:]:
                      e.tensor_tensor(out=o, in0=i, in1=m, op=ALU.is_ge),
                      reads=[b_lgtm, b_m1], writes=[b_eq1], no_waw=True)
                for e_ in range(NE):
                    O("vector", lambda e, o=esl(l2t, e_), i=esl(eq1, e_),
                      i1=esl(lg_tm, e_):
                      e.scalar_tensor_tensor(out=o, in0=i, scalar=-1.0e30, in1=i1,
                                             op0=ALU.mult, op1=ALU.add),
                      reads=[b_eq1, b_lgtm], writes=[b_l2], no_waw=True)
                O("vector", lambda e, o=m2t[:], i=esl(l2t, 0):
                  e.tensor_copy(out=o, in_=i), reads=[b_l2], writes=[b_m2])
                for e_ in range(1, NE):
                    O("vector", lambda e, o=m2t[:], i=esl(l2t, e_):
                      e.tensor_max(out=o, in0=o, in1=i),
                      reads=[b_l2, b_m2], writes=[b_m2])
                for e_ in range(NE):
                    O("vector", lambda e, o=esl(eq2, e_), i=esl(l2t, e_), m=m2t[:]:
                      e.tensor_tensor(out=o, in0=i, in1=m, op=ALU.is_ge),
                      reads=[b_l2, b_m2], writes=[b_eq2], no_waw=True)
                O("vector", lambda e, o=dtt[:], i=m2t[:], i1=m1t[:]:
                  e.tensor_sub(out=o, in0=i, in1=i1),
                  reads=[b_m1, b_m2], writes=[b_dtt])
                O("scalar", lambda e, o=edt[:], i=dtt[:]:
                  e.activation(out=o, in_=i, func=AF.Exp),
                  reads=[b_dtt], writes=[b_ed])
                O("vector", lambda e, o=edt[:]:
                  e.tensor_single_scalar(out=o, in_=o, scalar=1.0, op=ALU.add),
                  reads=[b_ed], writes=[b_ed])
                O("vector", lambda e, o=w1_t[:], i=edt[:]: e.reciprocal(out=o, in_=i),
                  reads=[b_ed], writes=[b_w1])
                O("vector", lambda e, o=w2_t[:], i=w1_t[:]:
                  e.tensor_scalar(out=o, in0=i, scalar1=-1.0, scalar2=1.0,
                                  op0=ALU.mult, op1=ALU.add),
                  reads=[b_w1], writes=[b_w2])
                O("vector", lambda e, o=me_bf[:], i=eq1[:], i1=eq2[:]:
                  e.tensor_add(out=o, in0=i, in1=i1),
                  reads=[b_eq1, b_eq2], writes=[b_mebf])
                MG(b_psic, [lambda e, o=ps_ic[:], l_=csum_t[:], r_=me_bf[:]:
                            e.matmul(o, l_, r_, start=True, stop=True)],
                   [b_csum, b_mebf])
                O("scalar", lambda e, o=ic_sb[:], i=ps_ic[:]: e.copy(out=o, in_=i),
                  reads=[b_psic], writes=[b_ic])
                MG(b_pssm, [lambda e, o=ps_sm[0:1, 0:NCT * NE], l_=onesc[:], r_=me_bf[:]:
                            e.matmul(o, l_, r_, start=True, stop=True)],
                   [b_onesc, b_mebf])
                O("scalar", lambda e, o=tot_sb[:], i=ps_sm[0:1, 0:NCT * NE]:
                  e.copy(out=o, in_=i), reads=[b_pssm], writes=[b_tot])
                for j in range(2):
                    O("tensor", lambda e, o=ps_sm[0:128, 0:1],
                      i=tot_sb[:, j * 128:(j + 1) * 128], ii=idf[0:1, 0:1]:
                      e.transpose(o, i, ii), reads=[b_tot, b_idf], writes=[b_pssm])
                    O("vector", lambda e, o=totcb[:, j:j + 1], i=ps_sm[0:128, 0:1]:
                      e.tensor_copy(out=o, in_=i),
                      reads=[b_pssm], writes=[b_totcb], no_waw=True)
                MG(b_pssm, [lambda e, o=ps_sm[0:128, 1:2], l_=m0_t[:], r_=totcb[:, 0:1]:
                            e.matmul(o, l_, r_, start=True, stop=True)],
                   [b_m0, b_totcb])
                O("vector", lambda e, o=off_sb[:, 0:1], i=ps_sm[0:128, 1:2]:
                  e.tensor_copy(out=o, in_=i), reads=[b_pssm], writes=[b_off],
                  no_waw=True)
                MG(b_pssm, [
                    lambda e, o=ps_sm[0:128, 2:3], l_=m01_t[:], r_=totcb[:, 0:1]:
                    e.matmul(o, l_, r_, start=True, stop=False),
                    lambda e, o=ps_sm[0:128, 2:3], l_=m0_t[:], r_=totcb[:, 1:2]:
                    e.matmul(o, l_, r_, start=False, stop=True)],
                   [b_m0, b_m01, b_totcb])
                O("vector", lambda e, o=off_sb[:, 1:2], i=ps_sm[0:128, 2:3]:
                  e.tensor_copy(out=o, in_=i), reads=[b_pssm], writes=[b_off],
                  no_waw=True)
                for j in range(2):
                    O("tensor", lambda e, o=ps_sm[0:1, 0:128], i=off_sb[:, j:j + 1],
                      ii=idf[:]: e.transpose(o, i, ii),
                      reads=[b_off, b_idf], writes=[b_pssm])
                    O("vector", lambda e, o=offr[:, j * 128:(j + 1) * 128],
                      i=ps_sm[0:1, 0:128]: e.tensor_copy(out=o, in_=i),
                      reads=[b_pssm], writes=[b_offr], no_waw=True)
                MG(b_psofb, [lambda e, o=ps_ofb[:], l_=onesr[:], r_=offr[:]:
                             e.matmul(o, l_, r_, start=True, stop=True)],
                   [b_onesr, b_offr])
                O("vector", lambda e, o=offm[:], i=ps_ofb[:]:
                  e.tensor_single_scalar(out=o, in_=i, scalar=-1.0, op=ALU.add),
                  reads=[b_psofb], writes=[b_offm])
                O("vector", lambda e, o=pos_t[:], i=ic_sb[:], i1=offm[:]:
                  e.tensor_add(out=o, in0=i, in1=i1),
                  reads=[b_ic, b_offm], writes=[b_pos])
                O("vector", lambda e, o=pos_t[:]:
                  e.tensor_single_scalar(out=o, in_=o, scalar=float(CCAP),
                                         op=ALU.min),
                  reads=[b_pos], writes=[b_pos])
                for dst_t, b_dst, eqt, b_eqt, adst, b_adst in (
                        (aA, b_aA, eq1, b_eq1, addrA, b_addrA),
                        (aB, b_aB, eq2, b_eq2, addrB, b_addrB)):
                    O("vector", lambda e, a=dst_t[:]: e.memset(a, 0.0), writes=[b_dst])
                    for e_ in range(NE):
                        O("vector", lambda e, o=tmp1[:], i=esl(pos_t, e_),
                          s=float(e_ * ES):
                          e.tensor_single_scalar(out=o, in_=i, scalar=s, op=ALU.add),
                          reads=[b_pos], writes=[b_t1])
                        O("vector", lambda e, o=tmp1[:], i=esl(eqt, e_):
                          e.tensor_mul(out=o, in0=o, in1=i),
                          reads=[b_t1, b_eqt], writes=[b_t1])
                        O("vector", lambda e, o=dst_t[:], i=tmp1[:]:
                          e.tensor_add(out=o, in0=o, in1=i),
                          reads=[b_t1, b_dst], writes=[b_dst])
                    O("vector", lambda e, o=dst_t[:]:
                      e.tensor_single_scalar(out=o, in_=o,
                                             scalar=float(NE * ES - 1), op=ALU.min),
                      reads=[b_dst], writes=[b_dst])
                    O("vector", lambda e, o=adst[:], i=dst_t[:]:
                      e.tensor_copy(out=o, in_=i), reads=[b_dst], writes=[b_adst])
                if debug:
                    O("sync", lambda e, a=dbg["lgtm"], s=lg_tm[:]: e.dma_start(out=a, in_=s),
                      reads=[b_lgtm], dma=True)
                    O("sync", lambda e, a=dbg["pos"], s=pos_t[:]: e.dma_start(out=a, in_=s),
                      reads=[b_pos], dma=True)
                for ct in range(NCT):
                    for a_t, b_a, val_t, b_val, dram, b_dr in (
                            (addrA, b_addrA, tokid, b_tokid, idx_dram, b_idxdram),
                            (addrB, b_addrB, tokid, b_tokid, idx_dram, b_idxdram),
                            (addrA, b_addrA, w1_t, b_w1, w_dram, b_wdram),
                            (addrB, b_addrB, w2_t, b_w2, w_dram, b_wdram)):
                        O("gpsimd", lambda e, od=dram, ofs=a_t[:, ct:ct + 1],
                          src=val_t[:, ct:ct + 1]:
                          e.indirect_dma_start(
                              out=od, out_offset=IndirectOffsetOnAxis(ofs, 0),
                              in_=src, in_offset=None),
                          reads=[b_a, b_val, b_dr], writes=[b_dr],
                          dma=True, no_waw=True)
            kb.barrier(engines=("tensor", "vector", "scalar", "sync"),
                       lane_queues=("sync",))

        if with_attn:
            with contextlib.ExitStack() as p3:
                q_fm, b_qfm = mk(p3, "q_fm", [128, 2 * NH * SEQ], BF16, "right")
                k_fm, b_kfm = mk(p3, "k_fm", [128, 2 * SEQ], BF16, "right")
                v_tm, b_vtm = mk(p3, "v_tm", [128, (SEQ // 128) * HD], BF16, "right")
                sqe, b_sqe = mk(p3, "sqe", [128, 512], BF16, "right")
                sqo, b_sqo = mk(p3, "sqo", [128, 512], BF16, "right")
                rstd_p, b_rstdp = mk(p3, "rstd_p", [1, 512], F32, "right")
                rstd_b, b_rstdb = mk(p3, "rstd_b", [1, 512], BF16, "right")
                sce, b_sce = mk(p3, "sce", [128, 512], F32, "right")
                sco, b_sco = mk(p3, "sco", [128, 512], F32, "right")
                ne_b, b_neb = mk(p3, "ne_b", [128, 512], BF16, "right")
                no_b, b_nob = mk(p3, "no_b", [128, 512], BF16, "right")
                r1, b_r1 = mk(p3, "r1", [128, 512], BF16, "right")
                r2, b_r2 = mk(p3, "r2", [128, 512], BF16, "right")
                s_sb = [mk(p3, f"s_sb{i}", [128, 640], F32, "right") for i in range(2)]
                pn_b = [mk(p3, f"pn_b{i}", [128, 640], BF16, "right") for i in range(2)]
                mx = [mk(p3, f"mx_t{i}", [128, 4], F32, "right") for i in range(2)]
                pT_sb, b_pT = mk(p3, "pT_sb", [128, 640], BF16, "right")
                afm = [mk(p3, f"afm{i}", [128, CCN * 128], BF16, "right")
                       for i in range(2)]
                wo_sb = [mk(p3, f"wo_sb{i}", [128, C], BF16, "right") for i in range(2)]

                for s_i in range(NSEQ):
                    if attn_level < 1:
                        break
                    with contextlib.ExitStack() as pp:
                        ps_a = mkp(pp, "ps_a", [128, 512], F32)
                        b_psa = Buf("ps_a")
                        ps_b = mkp(pp, "ps_b", [128, 512], F32)
                        b_psb = Buf("ps_b")
                        ps_ms = mkp(pp, "ps_ms", [1, 512], F32)
                        b_psms = Buf("ps_ms")
                        ps_sc = mkp(pp, "ps_sc", [128, 512], F32)
                        b_pssc = Buf("ps_sc")

                        def proj_pair(dst_t, b_dst, dst_base, w_t, b_w, wchunk,
                                      wcol0, gam_t, b_gam, tt):
                            t0 = s_i * SEQ + tt * 512
                            mmsE = [lambda e, o=ps_a[:],
                                    l_=w_t[:, cc * wchunk + wcol0:cc * wchunk + wcol0 + 128],
                                    r_=xnfm[:, cc * TOK + t0:cc * TOK + t0 + 512],
                                    f=(cc == 0), la=(cc == CCN - 1):
                                    e.matmul(o, l_, r_, start=f, stop=la)
                                    for cc in range(CCN)]
                            MG(b_psa, mmsE, [b_w, b_xnfm])
                            mmsO = [lambda e, o=ps_b[:],
                                    l_=w_t[:, cc * wchunk + wcol0 + 128:
                                           cc * wchunk + wcol0 + 256],
                                    r_=xnfm[:, cc * TOK + t0:cc * TOK + t0 + 512],
                                    f=(cc == 0), la=(cc == CCN - 1):
                                    e.matmul(o, l_, r_, start=f, stop=la)
                                    for cc in range(CCN)]
                            MG(b_psb, mmsO, [b_w, b_xnfm])
                            O("scalar", lambda e, o=sqe[:], i=ps_a[:]:
                              e.activation(out=o, in_=i, func=AF.Square),
                              reads=[b_psa], writes=[b_sqe])
                            O("scalar", lambda e, o=sqo[:], i=ps_b[:]:
                              e.activation(out=o, in_=i, func=AF.Square),
                              reads=[b_psb], writes=[b_sqo])
                            MG(b_psms, [
                                lambda e, o=ps_ms[:], l_=onesc[:], r_=sqe[:]:
                                e.matmul(o, l_, r_, start=True, stop=False),
                                lambda e, o=ps_ms[:], l_=onesc[:], r_=sqo[:]:
                                e.matmul(o, l_, r_, start=False, stop=True)],
                               [b_onesc, b_sqe, b_sqo])
                            O("scalar", lambda e, o=rstd_p[:], i=ps_ms[:],
                              b_=epsc[0:1, 1:2]:
                              e.activation(out=o, in_=i, func=AF.Ln,
                                           scale=1.0 / HD, bias=b_),
                              reads=[b_psms, b_epsc], writes=[b_rstdp])
                            O("scalar", lambda e, o=rstd_b[:], i=rstd_p[:]:
                              e.activation(out=o, in_=i, func=AF.Exp, scale=-0.5),
                              reads=[b_rstdp], writes=[b_rstdb])
                            MG(b_pssc, [lambda e, o=ps_sc[:], l_=gam_t[:, 0:128],
                                        r_=rstd_b[:]:
                                        e.matmul(o, l_, r_, start=True, stop=True)],
                               [b_gam, b_rstdb])
                            O("vector", lambda e, o=sce[:], i=ps_sc[:]:
                              e.tensor_copy(out=o, in_=i),
                              reads=[b_pssc], writes=[b_sce])
                            MG(b_pssc, [lambda e, o=ps_sc[:], l_=gam_t[:, 128:256],
                                        r_=rstd_b[:]:
                                        e.matmul(o, l_, r_, start=True, stop=True)],
                               [b_gam, b_rstdb])
                            O("vector", lambda e, o=sco[:], i=ps_sc[:]:
                              e.tensor_copy(out=o, in_=i),
                              reads=[b_pssc], writes=[b_sco])
                            O("vector", lambda e, o=ne_b[:], i=ps_a[:], i1=sce[:]:
                              e.tensor_mul(out=o, in0=i, in1=i1),
                              reads=[b_psa, b_sce, b_sqe], writes=[b_neb])
                            O("vector", lambda e, o=no_b[:], i=ps_b[:], i1=sco[:]:
                              e.tensor_mul(out=o, in0=i, in1=i1),
                              reads=[b_psb, b_sco, b_sqo], writes=[b_nob])
                            csl = slice(tt * 512, (tt + 1) * 512)
                            O("vector", lambda e, o=r1[:], i=ne_b[:], i1=cos_t[:, csl]:
                              e.tensor_mul(out=o, in0=i, in1=i1),
                              reads=[b_neb, b_cos], writes=[b_r1])
                            O("vector", lambda e, o=r2[:], i=no_b[:], i1=sin_t[:, csl]:
                              e.tensor_mul(out=o, in0=i, in1=i1),
                              reads=[b_nob, b_sin], writes=[b_r2])
                            O("vector", lambda e,
                              o=dst_t[:, dst_base + tt * 512:dst_base + (tt + 1) * 512],
                              i=r1[:], i1=r2[:]: e.tensor_sub(out=o, in0=i, in1=i1),
                              reads=[b_r1, b_r2], writes=[b_dst], no_waw=True)
                            O("vector", lambda e, o=r1[:], i=ne_b[:], i1=sin_t[:, csl]:
                              e.tensor_mul(out=o, in0=i, in1=i1),
                              reads=[b_neb, b_sin], writes=[b_r1])
                            O("vector", lambda e, o=r2[:], i=no_b[:], i1=cos_t[:, csl]:
                              e.tensor_mul(out=o, in0=i, in1=i1),
                              reads=[b_nob, b_cos], writes=[b_r2])
                            O("vector", lambda e,
                              o=dst_t[:, dst_base + SEQ + tt * 512:
                                      dst_base + SEQ + (tt + 1) * 512],
                              i=r1[:], i1=r2[:]: e.tensor_add(out=o, in0=i, in1=i1),
                              reads=[b_r1, b_r2], writes=[b_dst], no_waw=True)

                        for tt in range(SEQ // 512):
                            proj_pair(k_fm, b_kfm, 0, wk_bf, b_wkbf, HD, 0,
                                      kg_t, b_kg, tt)
                        for h in range(NH):
                            for tt in range(SEQ // 512):
                                proj_pair(q_fm, b_qfm, (2 * h) * SEQ, wq_bf, b_wqbf,
                                          C, h * HD, qg_t, b_qg, tt)
                        for tc in range(SEQ // 128):
                            t0 = s_i * SEQ + tc * 128
                            mms = [lambda e, o=ps_a[:, 0:HD],
                                   l_=xnfm[:, cc * TOK + t0:cc * TOK + t0 + 128],
                                   r_=wv_bf[:, cc * HD:(cc + 1) * HD],
                                   f=(cc == 0), la=(cc == CCN - 1):
                                   e.matmul(o, l_, r_, start=f, stop=la)
                                   for cc in range(CCN)]
                            MG(b_psa, mms, [b_xnfm, b_wvbf])
                            O("scalar", lambda e, o=v_tm[:, tc * HD:(tc + 1) * HD],
                              i=ps_a[:, 0:HD]: e.copy(out=o, in_=i),
                              reads=[b_psa], writes=[b_vtm], no_waw=True)
                        kb.barrier(("tensor", "vector", "scalar"))

                    if attn_level < 2:
                        continue
                    with contextlib.ExitStack() as pa:
                        psA = [(mkp(pa, f"psA{i}", [128, 512], F32),
                                Buf(f"psA{i}")) for i in range(2)]
                        psDp = [(mkp(pa, f"psD{i}", [128, 128], F32),
                                 Buf(f"psD{i}")) for i in range(2)]
                        psav = [(mkp(pa, f"psav{i}", [128, 256], F32),
                                 Buf(f"psav{i}")) for i in range(1)]
                        pspt = [(mkp(pa, f"pspt{i}", [128, 128], BF16),
                                 Buf(f"pspt{i}")) for i in range(2)]
                        pswo = mkp(pa, "pswo", [128, 512], F32)
                        b_pswo = Buf("pswo")

                        iters = [(qb, h) for qb in (qb_list if qb_list is not None
                                                    else range(SEQ // 128))
                                 for h in range(NH)]

                        def emit_qk(i):
                            qb, h = iters[i]
                            hp = i % 2
                            sA, b_sA = psA[hp]
                            nA = min(qb, 4)
                            wA = nA * 128
                            j0 = qb * 128 - wA
                            qsl_e = q_fm[:, 2 * h * SEQ + qb * 128:
                                         2 * h * SEQ + (qb + 1) * 128]
                            qsl_o = q_fm[:, (2 * h + 1) * SEQ + qb * 128:
                                         (2 * h + 1) * SEQ + (qb + 1) * 128]
                            if nA:
                                MG(b_sA, [
                                    lambda e, o=sA[:, 0:wA], l_=qsl_e,
                                    r_=k_fm[:, j0:j0 + wA]:
                                    e.matmul(o, l_, r_, start=True, stop=False),
                                    lambda e, o=sA[:, 0:wA], l_=qsl_o,
                                    r_=k_fm[:, SEQ + j0:SEQ + j0 + wA]:
                                    e.matmul(o, l_, r_, start=False, stop=True)],
                                   [b_qfm, b_kfm])
                            pD, b_pD = psDp[hp]
                            MG(b_pD, [
                                lambda e, o=pD[:], l_=qsl_e,
                                r_=k_fm[:, qb * 128:(qb + 1) * 128]:
                                e.matmul(o, l_, r_, start=True, stop=False),
                                lambda e, o=pD[:], l_=qsl_o,
                                r_=k_fm[:, SEQ + qb * 128:SEQ + (qb + 1) * 128]:
                                e.matmul(o, l_, r_, start=False, stop=True)],
                               [b_qfm, b_kfm])

                        def emit_rest(i):
                            qb, h = iters[i]
                            hp = i % 2
                            par = qb % 2
                            afm_t, b_afm = afm[par]
                            sA, b_sA = psA[hp]
                            pD, b_pD = psDp[hp]
                            ssb, b_ssb = s_sb[hp]
                            pnb, b_pnb = pn_b[hp]
                            mx_t, b_mx = mx[hp]
                            nA = min(qb, 4)
                            wA = nA * 128
                            j0 = qb * 128 - wA
                            if nA:
                                if qb >= 4:
                                    O("vector", lambda e, o=ssb[:, 0:512],
                                      i=sA[:, 0:512], i1=mleft5[:]:
                                      e.tensor_tensor(out=o, in0=i, in1=i1,
                                                      op=ALU.add),
                                      reads=[b_sA, b_mleft5], writes=[b_ssb],
                                      no_waw=True)
                                else:
                                    O("scalar", lambda e, o=ssb[:, 0:wA],
                                      i=sA[:, 0:wA]: e.copy(out=o, in_=i),
                                      reads=[b_sA], writes=[b_ssb], no_waw=True)
                            O("vector", lambda e, o=ssb[:, wA:wA + 128], i=pD[:],
                              i1=mdiag[:]:
                              e.tensor_tensor(out=o, in0=i, in1=i1, op=ALU.add),
                              reads=[b_pD, b_mdiag], writes=[b_ssb], no_waw=True)
                            wtot = wA + 128
                            O("vector", lambda e, o=mx_t[:, 0:1], i=ssb[:, 0:wtot]:
                              e.reduce_max(out=o, in_=i, axis=AX.X, negate=True),
                              reads=[b_ssb], writes=[b_mx])
                            O("scalar", lambda e, o=ssb[:, 0:wtot],
                              i=ssb[:, 0:wtot], b_=mx_t[:, 0:1], a=mx_t[:, 2:3]:
                              e.activation(out=o, in_=i, func=AF.Exp,
                                           bias=b_, accum_out=a),
                              reads=[b_ssb, b_mx], writes=[b_ssb, b_mx],
                              no_waw=True)
                            O("vector", lambda e, o=mx_t[:, 3:4], i=mx_t[:, 2:3]:
                              e.reciprocal(out=o, in_=i),
                              reads=[b_mx], writes=[b_mx])
                            O("vector", lambda e, o=pnb[:, 0:wtot],
                              i=ssb[:, 0:wtot], s1=mx_t[:, 3:4]:
                              e.tensor_scalar(out=o, in0=i, scalar1=s1,
                                              scalar2=None, op0=ALU.mult),
                              reads=[b_ssb, b_mx], writes=[b_pnb])
                            for jc in range(nA + 1):
                                pt, b_pt = pspt[jc % 2]
                                O("tensor", lambda e, o=pt[:],
                                  i=pnb[:, jc * 128:(jc + 1) * 128], ii=idb[:]:
                                  e.transpose(o, i, ii),
                                  reads=[b_pnb, b_idb], writes=[b_pt])
                                dst = pT_sb[:, jc * 128:(jc + 1) * 128]
                                if jc % 2 == 0:
                                    O("scalar", lambda e, o=dst, i=pt[:]:
                                      e.copy(out=o, in_=i),
                                      reads=[b_pt], writes=[b_pT], no_waw=True)
                                else:
                                    O("vector", lambda e, o=dst, i=pt[:]:
                                      e.tensor_copy(out=o, in_=i),
                                      reads=[b_pt], writes=[b_pT], no_waw=True)
                            av, b_av = psav[0]
                            for dc in range(2):
                                mms = []
                                for jc in range(nA + 1):
                                    jcg = j0 // 128 + jc
                                    mms.append(
                                        lambda e, o=av[:, dc * 128:(dc + 1) * 128],
                                        l_=v_tm[:, jcg * HD + dc * 128:
                                                jcg * HD + (dc + 1) * 128],
                                        r_=pT_sb[:, jc * 128:(jc + 1) * 128],
                                        f=(jc == 0), la=(jc == nA):
                                        e.matmul(o, l_, r_, start=f, stop=la))
                                MG(b_av, mms, [b_vtm, b_pT], no_waw=(dc == 1))
                            for dc in range(2):
                                O("scalar", lambda e,
                                  o=afm_t[:, (2 * h + dc) * 128:(2 * h + dc + 1) * 128],
                                  i=av[:, dc * 128:(dc + 1) * 128]:
                                  e.copy(out=o, in_=i),
                                  reads=[b_av], writes=[b_afm], no_waw=True)
                            if h == NH - 1:
                                for ot in range(2):
                                    mms = [lambda e, o=pswo[:],
                                           l_=afm_t[:, dc_ * 128:(dc_ + 1) * 128],
                                           r_=wo_bfw[:, dc_ * C + ot * 512:
                                                     dc_ * C + (ot + 1) * 512],
                                           f=(dc_ == 0), la=(dc_ == CCN - 1):
                                           e.matmul(o, l_, r_, start=f, stop=la)
                                           for dc_ in range(CCN)]
                                    MG(b_pswo, mms, [b_afm, b_wobf])
                                    O("scalar", lambda e,
                                      o=wo_sb[par][0][:, ot * 512:(ot + 1) * 512],
                                      i=pswo[:]: e.copy(out=o, in_=i),
                                      reads=[b_pswo], writes=[wo_sb[par][1]],
                                      no_waw=(ot == 1))
                                O("sync", lambda e,
                                  a=wo_tm_d[s_i * SEQ + qb * 128:
                                            s_i * SEQ + (qb + 1) * 128, :],
                                  s=wo_sb[par][0][:]: e.dma_start(out=a, in_=s),
                                  reads=[wo_sb[par][1]], writes=[b_wotm],
                                  dma=True, no_waw=True)

                        for ii_ in range(len(iters)):
                            emit_qk(ii_)
                            if ii_ > 0:
                                emit_rest(ii_ - 1)
                        if iters:
                            emit_rest(len(iters) - 1)
                        kb.barrier(("tensor", "vector", "scalar"))
        if (not with_attn) or attn_level < 5:
            for ct in range(NCT):
                O("sync", lambda e, a=wo_tm_d[ct * 128:(ct + 1) * 128, :], s=zrow[:]:
                  e.dma_start(out=a, in_=s),
                  reads=[b_zrow], writes=[b_wotm], dma=True, no_waw=True)
        kb.barrier()
        aw_stack.close()

        if with_ffn:
            with contextlib.ExitStack() as p4:
                xg2 = [mk(p4, f"xg_fm{i}", [128, CCN * CCAP], BF16, "right")
                       for i in range(2)]
                h_fm, b_hfm = mk(p4, "h_fm", [128, (FD // 128) * CCAP], BF16, "right")
                wgu_sb = [mk(p4, f"wgu_sb{i}", [128, CCN * 1024], BF16, "right")
                          for i in range(2)]
                wd_sb, b_wdsb = mk(p4, "wd_sb", [128, (FD // 128) * C], BF16, "right")
                idx2 = [mk(p4, f"idx_sb{i}", [128, NSC], I32, "right")
                        for i in range(2)]
                wsl2 = [mk(p4, f"wsl_sb{i}", [128, NSC], F32, "right")
                        for i in range(2)]
                xg_tm = [mk(p4, f"xg_tm{i}", [128, C], BF16, "right") for i in range(2)]
                gg_sb = [mk(p4, f"gg_sb{i}", [128, 512], BF16, "right") for i in range(2)]
                dout_sb = [mk(p4, f"dout_sb{i}", [128, C], BF16, "right")
                           for i in range(2)]
                ptm = [(mkp(p4, f"ptm{i}", [128, 128], BF16),
                        Buf(f"ptm{i}")) for i in range(2)]
                psg = [(mkp(p4, f"psg{i}", [128, 512], F32),
                        Buf(f"psg{i}")) for i in range(2)]
                psu = [(mkp(p4, f"psu{i}", [128, 512], F32),
                        Buf(f"psu{i}")) for i in range(2)]
                psd = [(mkp(p4, f"psd{i}", [128, 512], F32),
                        Buf(f"psd{i}")) for i in range(2)]

                def fetch_expert(e_i):
                    ep = e_i % 2
                    idx_sb, b_idxsb = idx2[ep]
                    wsl_sb, b_wsl = wsl2[ep]
                    xg_fm, b_xgfm = xg2[ep]
                    for sc in range(NSC):
                        r0 = e_i * ES + sc * 128
                        O("sync", lambda e, a=idx_sb[:, sc:sc + 1],
                          s=idx_dram[r0:r0 + 128, :]: e.dma_start(out=a, in_=s),
                          reads=[b_idxdram], writes=[b_idxsb], dma=True,
                          no_waw=(sc > 0))
                        O("sync", lambda e, a=wsl_sb[:, sc:sc + 1],
                          s=w_dram[r0:r0 + 128, :]: e.dma_start(out=a, in_=s),
                          reads=[b_wdram], writes=[b_wsl], dma=True,
                          no_waw=(sc > 0))
                    for sc in range(NSC):
                        gt, b_gt = xg_tm[sc % 2]
                        O("gpsimd", lambda e, a=gt[:], src=xn_tm_d,
                          ofs=idx_sb[:, sc:sc + 1]:
                          e.indirect_dma_start(
                              out=a, out_offset=None, in_=src,
                              in_offset=IndirectOffsetOnAxis(ofs, 0)),
                          reads=[b_idxsb, b_xntm], writes=[b_gt], dma=True)
                        for cc in range(CCN):
                            pt, b_pt = ptm[cc % 2]
                            O("tensor", lambda e, o=pt[:],
                              i=gt[:, cc * 128:(cc + 1) * 128], ii=idb[:]:
                              e.transpose(o, i, ii),
                              reads=[b_gt, b_idb], writes=[b_pt])
                            dst = xg_fm[:, cc * CCAP + sc * 128:cc * CCAP + (sc + 1) * 128]
                            if cc % 2 == 0:
                                O("scalar", lambda e, o=dst, i=pt[:]:
                                  e.copy(out=o, in_=i),
                                  reads=[b_pt], writes=[b_xgfm], no_waw=True)
                            else:
                                O("vector", lambda e, o=dst, i=pt[:]:
                                  e.tensor_copy(out=o, in_=i),
                                  reads=[b_pt], writes=[b_xgfm], no_waw=True)

                fetch_expert(0)
                for e_i in range(NE):
                    idx_sb, b_idxsb = idx2[e_i % 2]
                    wsl_sb, b_wsl = wsl2[e_i % 2]
                    xg_fm, b_xgfm = xg2[e_i % 2]
                    for q4 in range(4):
                        wbuf, b_wbuf = wgu_sb[q4 % 2]
                        for cc in range(CCN):
                            O("gpsimd", lambda e, a=wbuf[:, cc * 1024:cc * 1024 + 512],
                              s=wgu_d[e_i, cc, :, q4 * 512:(q4 + 1) * 512]:
                              e.dma_start(out=a, in_=s),
                              reads=[b_wgu_bf], writes=[b_wbuf], dma=True,
                              no_waw=(cc > 0))
                            O("gpsimd", lambda e, a=wbuf[:, cc * 1024 + 512:(cc + 1) * 1024],
                              s=wgu_d[e_i, cc, :, FD + q4 * 512:FD + (q4 + 1) * 512]:
                              e.dma_start(out=a, in_=s),
                              reads=[b_wgu_bf], writes=[b_wbuf], dma=True, no_waw=True)
                        NTILES = [(i * 512, min(512, CCAP - i * 512))
                                  for i in range((CCAP + 511) // 512)]
                        for fi in range(4):
                            fc = q4 * 4 + fi
                            for nt, (n0, nw) in enumerate(NTILES):
                                pg, b_pg = psg[nt % 2]
                                pu, b_pu = psu[nt % 2]
                                mms = [lambda e, o=pg[:, 0:nw],
                                       l_=wbuf[:, cc * 1024 + fi * 128:
                                               cc * 1024 + (fi + 1) * 128],
                                       r_=xg_fm[:, cc * CCAP + n0:
                                                cc * CCAP + n0 + nw],
                                       f=(cc == 0), la=(cc == CCN - 1):
                                       e.matmul(o, l_, r_, start=f, stop=la)
                                       for cc in range(CCN)]
                                MG(b_pg, mms, [b_wbuf, b_xgfm])
                                mms = [lambda e, o=pu[:, 0:nw],
                                       l_=wbuf[:, cc * 1024 + 512 + fi * 128:
                                               cc * 1024 + 512 + (fi + 1) * 128],
                                       r_=xg_fm[:, cc * CCAP + n0:
                                                cc * CCAP + n0 + nw],
                                       f=(cc == 0), la=(cc == CCN - 1):
                                       e.matmul(o, l_, r_, start=f, stop=la)
                                       for cc in range(CCN)]
                                MG(b_pu, mms, [b_wbuf, b_xgfm])
                                gg, b_gg = gg_sb[nt % 2]
                                O("scalar", lambda e, o=gg[:, 0:nw], i=pg[:, 0:nw]:
                                  e.activation(out=o, in_=i, func=AF.Gelu),
                                  reads=[b_pg], writes=[b_gg])
                                O("vector", lambda e,
                                  o=h_fm[:, fc * CCAP + n0:fc * CCAP + n0 + nw],
                                  i=pu[:, 0:nw], i1=gg[:, 0:nw]:
                                  e.tensor_mul(out=o, in0=i, in1=i1),
                                  reads=[b_pu, b_gg], writes=[b_hfm], no_waw=True)
                    if e_i + 1 < NE:
                        fetch_expert(e_i + 1)
                    for fc in range(FD // 128):
                        O("gpsimd", lambda e, a=wd_sb[:, fc * C:(fc + 1) * C],
                          s=wd_d[e_i, fc]: e.dma_start(out=a, in_=s),
                          reads=[b_wd_bf], writes=[b_wdsb], dma=True,
                          no_waw=(fc > 0))
                    for sc in range(NSC):
                        do_t, b_do = dout_sb[sc % 2]
                        for ctile in range(2):
                            pd, b_pd = psd[ctile % 2]
                            mms = [lambda e, o=pd[:],
                                   l_=h_fm[:, fc_ * CCAP + sc * 128:
                                           fc_ * CCAP + (sc + 1) * 128],
                                   r_=wd_sb[:, fc_ * C + ctile * 512:
                                            fc_ * C + (ctile + 1) * 512],
                                   f=(fc_ == 0), la=(fc_ == FD // 128 - 1):
                                   e.matmul(o, l_, r_, start=f, stop=la)
                                   for fc_ in range(FD // 128)]
                            MG(b_pd, mms, [b_hfm, b_wdsb])
                            O("scalar", lambda e,
                              o=do_t[:, ctile * 512:(ctile + 1) * 512], i=pd[:],
                              s1=wsl_sb[:, sc:sc + 1]:
                              e.activation(out=o, in_=i, func=AF.Copy, scale=s1),
                              reads=[b_pd, b_wsl], writes=[b_do],
                              no_waw=(ctile > 0))
                        O("sync", lambda e,
                          a=dout_d[e_i * ES + sc * 128:e_i * ES + (sc + 1) * 128, :],
                          s=do_t[:]: e.dma_start(out=a, in_=s),
                          reads=[b_do], writes=[b_dout], dma=True, no_waw=True)
            kb.barrier()

        if with_ffn and debug:
            O("sync", lambda e, a=dbg["addrA"], s=addrA[:]: e.dma_start(out=a, in_=s),
              reads=[b_addrA], dma=True)
            O("sync", lambda e, a=dbg["addrB"], s=addrB[:]: e.dma_start(out=a, in_=s),
              reads=[b_addrB], dma=True)
            O("sync", lambda e, a=dbg["w1"], s=w1_t[:]: e.dma_start(out=a, in_=s),
              reads=[b_w1], dma=True)
            O("sync", lambda e, a=dbg["w2"], s=w2_t[:]: e.dma_start(out=a, in_=s),
              reads=[b_w2], dma=True)
            O("gpsimd", lambda e, a=dbg["idx"], s=idx_dram: e.dma_start(out=a, in_=s),
              reads=[b_idxdram], dma=True)
            O("gpsimd", lambda e, a=dbg["warr"], s=w_dram: e.dma_start(out=a, in_=s),
              reads=[b_wdram], dma=True)
            O("gpsimd", lambda e, a=dbg["xntm"], s=xn_tm_d: e.dma_start(out=a, in_=s),
              reads=[b_xntm], dma=True)
            O("gpsimd", lambda e, a=dbg["dout"], s=dout_d: e.dma_start(out=a, in_=s),
              reads=[b_dout, b_doutz], dma=True)

        with contextlib.ExitStack() as p5:
            NPAR5 = 6
            xt5 = [mk(p5, f"xt5{i}", [128, C], F32, "right") for i in range(NPAR5)]
            wt5 = [mk(p5, f"wt5{i}", [128, C], BF16, "right") for i in range(NPAR5)]
            ga5 = [mk(p5, f"ga5{i}", [128, C], BF16, "right") for i in range(NPAR5)]
            gb5 = [mk(p5, f"gb5{i}", [128, C], BF16, "right") for i in range(NPAR5)]
            o5 = [mk(p5, f"o5{i}", [128, C], F32, "right") for i in range(NPAR5)]
            t5 = [mk(p5, f"t5{i}", [128, C], F32, "right") for i in range(NPAR5)]

            for ct in range(NCT):
                par = ct % NPAR5
                xt, b_xt = xt5[par]
                wt, b_wt = wt5[par]
                ga, b_ga = ga5[par]
                gb, b_gb = gb5[par]
                ot, b_ot = o5[par]
                tt_, b_tt = t5[par]
                rows = slice(ct * 128, (ct + 1) * 128)
                O("sync", lambda e, a=xt[:], s=x_d[rows, :]: e.dma_start(out=a, in_=s),
                  writes=[b_xt], dma=True)
                O("sync", lambda e, a=wt[:], s=wo_tm_d[rows, :]:
                  e.dma_start(out=a, in_=s),
                  reads=[b_wotm], writes=[b_wt], dma=True)
                if with_ffn:
                    O("gpsimd", lambda e, a=ga[:], src=dout_d, ofs=addrA[:, ct:ct + 1]:
                      e.indirect_dma_start(out=a, out_offset=None, in_=src,
                                           in_offset=IndirectOffsetOnAxis(ofs, 0)),
                      reads=[b_addrA, b_dout, b_doutz], writes=[b_ga], dma=True)
                    O("gpsimd", lambda e, a=gb[:], src=dout_d, ofs=addrB[:, ct:ct + 1]:
                      e.indirect_dma_start(out=a, out_offset=None, in_=src,
                                           in_offset=IndirectOffsetOnAxis(ofs, 0)),
                      reads=[b_addrB, b_dout, b_doutz], writes=[b_gb], dma=True)
                    eng5 = "vector"
                    O(eng5, lambda e, o=tt_[:], i=ga[:], i1=gb[:]:
                      e.tensor_add(out=o, in0=i, in1=i1),
                      reads=[b_ga, b_gb], writes=[b_tt])
                    O(eng5, lambda e, o=tt_[:], i=wt[:]:
                      e.tensor_add(out=o, in0=o, in1=i),
                      reads=[b_tt, b_wt], writes=[b_tt])
                else:
                    eng5 = "vector"
                    O("vector", lambda e, o=tt_[:], i=wt[:]: e.tensor_copy(out=o, in_=i),
                      reads=[b_wt], writes=[b_tt])
                O(eng5, lambda e, o=ot[:], i=tt_[:], i1=xt[:]:
                  e.tensor_add(out=o, in0=i, in1=i1),
                  reads=[b_tt, b_xt], writes=[b_ot])
                O("sync", lambda e, a=out_d[rows, :], s=ot[:]: e.dma_start(out=a, in_=s),
                  reads=[b_ot], dma=True)

        kb.finalize()
    return nc



def _perm_cols(n_out):
    p = np.empty(n_out, dtype=np.int64)
    for n in range(n_out):
        h, r = divmod(n, 256)
        half, m = divmod(r, 128)
        p[n] = 256 * h + 2 * m + half
    return p


def _host_consts():
    c = {}
    i = np.arange(128)
    c["ident_f"] = np.eye(128, dtype=np.float32)
    c["ident_bf"] = np.eye(128, dtype=bf16)
    c["csum_ut"] = (i[:, None] <= i[None, :]).astype(bf16)
    e_of = i % NE
    ct_of = i // NE
    c["m0"] = ((e_of[:, None] == e_of[None, :]) &
               (ct_of[:, None] < ct_of[None, :])).astype(bf16)
    c["m01"] = (e_of[:, None] == e_of[None, :]).astype(bf16)
    c["mdiag"] = np.where(i[None, :] <= i[:, None], 0.0, -1.0e30).astype(np.float32)
    c["mleft"] = np.where(i[None, :] >= i[:, None], 0.0, -1.0e30).astype(np.float32)
    c["mleft5"] = np.concatenate([c["mleft"], np.zeros((128, 384), np.float32)],
                                 axis=1)
    c["m_idx"] = np.arange(128, dtype=np.float32).reshape(128, 1)
    c["t_row"] = np.arange(SEQ, dtype=np.float32).reshape(1, SEQ)
    c["ones_r"] = np.ones((1, 128), dtype=np.float32)
    c["ones_c"] = np.ones((128, 1), dtype=bf16)
    c["ones_cf"] = np.ones((128, 1), dtype=np.float32)
    c["tokid"] = (np.arange(NCT)[None, :] * 128 +
                  np.arange(128)[:, None]).astype(np.int32)
    c["epsc"] = np.tile(np.array([[1e-5, 1e-6]], dtype=np.float32), (128, 1))
    return c


def prep_inputs(inputs):
    x = np.asarray(inputs["L0_attn_x"])
    wq = np.asarray(inputs["L0_attn_Wq"])
    wk = np.asarray(inputs["L0_attn_Wk"])
    wv = np.asarray(inputs["L0_attn_Wv"])
    wo = np.asarray(inputs["L0_attn_Wo"])
    qg = np.asarray(inputs["L0_attn_q_gamma"])
    kg = np.asarray(inputs["L0_attn_k_gamma"])
    wg = np.asarray(inputs["L0_ffn_W_gate"])
    wgu = np.asarray(inputs["L0_ffn_W_gate_up"])
    wd = np.asarray(inputs["L0_ffn_W_down"])

    pq = _perm_cols(C)
    pk = _perm_cols(HD)
    shared = {
        "wqT": np.ascontiguousarray(wq.T[:, pq], dtype=np.float32),
        "wkT": np.ascontiguousarray(wk.T[:, pk], dtype=np.float32),
        "wvT": np.ascontiguousarray(wv.T, dtype=np.float32),
        "woT": np.ascontiguousarray(wo.T, dtype=np.float32),
        "qg": np.ascontiguousarray((qg[pk] / 16.0).reshape(1, HD)).astype(bf16),
        "kg": np.ascontiguousarray(kg[pk].reshape(1, HD)).astype(bf16),
        "wgT": np.ascontiguousarray(wg.T, dtype=np.float32),
        "wgu": np.ascontiguousarray(
            wgu.transpose(0, 2, 1).reshape(NE, CCN, 128, 2 * FD), dtype=np.float32),
        "wd": np.ascontiguousarray(
            wd.transpose(0, 2, 1).reshape(NE, FD // 128, 128, C), dtype=np.float32),
    }
    shared.update(_host_consts())
    in_maps = []
    for core in range(NCORES):
        m = dict(shared)
        m["x2d"] = np.ascontiguousarray(
            x[core * NSEQ:(core + 1) * NSEQ].reshape(TOK, C), dtype=np.float32)
        in_maps.append(m)
    return in_maps


def kernel(**inputs):
    in_maps = prep_inputs(inputs)
    nc = build_program()
    res = run_bass_kernel_spmd(nc, in_maps, core_ids=list(range(NCORES)))
    outs = [np.asarray(r["out"], dtype=np.float32) for r in res.results]
    return np.concatenate(outs, axis=0).reshape(16, SEQ, C)
